# version 04
# baseline (speedup 1.0000x reference)
import numpy as np
import ml_dtypes
from contextlib import ExitStack

import concourse.bass as bass
import concourse.mybir as mybir
from concourse import bass_utils

F32 = mybir.dt.float32
BF16 = mybir.dt.bfloat16
FP8 = mybir.dt.float8e4
AF = mybir.ActivationFunctionType
OP = mybir.AluOpType
AX = mybir.AxisListType

NCORE = 8
D = 2048
T = 8192
TC = T // NCORE
KT = D // 128
NH, HD, HPC = 16, 128, 2
JH = 5461
JP = 5504
JT = JP // 128
INV_D = 1.0 / D
EPS = 1e-5
ATT_SCALE = float(1.0 / np.sqrt(128.0))
ATT_SHIFT = 8.0

N_STAGES = 4
S4 = 4
S4PARTS = None
S4LN = 4
S4A = 4


def _s4parts():
    if S4PARTS is not None:
        return S4PARTS
    p = ""
    if S4 >= 1: p += "A"
    if S4 >= 2: p += "B"
    if S4 >= 3: p += "C"
    if S4 >= 3.5: p += "c"
    if S4 >= 4: p += "D"
    return p
NPAD = 0
NPAD_ENG = "vector"
DBG = 0
F8MODE = 2

BF = ml_dtypes.bfloat16
F8 = ml_dtypes.float8_e4m3

C_NEG1, C_INVD, C_EPS, C_ATTSC, C_NSHIFT, C_INVCNT2, C_ONE = range(7)
NCONST = 7
PAR = {n: i for i, n in enumerate(
    ["tmk", "tmv", "tmr", "bk", "bv", "br", "bo", "td", "tf", "bo1"])}
NPAR = len(PAR)


class SC:
    def __init__(self, nc, es, name):
        self.sem = es.enter_context(nc.semaphore(name))
        self.val = 0

    def inc(self, inst, n=1):
        inst.then_inc(self.sem, n)
        self.val += n
        return inst


def build_nc(stages):
    nc = bass.Bass("TRN2", target_bir_lowering=False, debug=False, num_devices=NCORE,
                   detect_race_conditions=False)
    es = ExitStack()

    def din(name, shape, dt):
        return nc.dram_tensor(name, shape, dt, kind="ExternalInput").ap()

    def dint(name, shape, dt):
        return nc.dram_tensor(name, shape, dt, kind="Internal").ap()

    xT = din("xT", [D, TC + 1], F32)
    wkT = din("wkT", [D, D], F32)
    wvT = din("wvT", [D, D], F32)
    wrT = din("wrT", [D, D], F32)
    woT = din("woT", [D, D], F32)
    w8T = din("w8T", [D, D], FP8)
    wqkv = din("wqkv", [3 * D, D], F32)
    wo1 = din("wo1", [D, D], F32)
    w1T = din("w1T", [JP, D], F32)
    w3T = din("w3T", [JP, D], F32)
    w2T = din("w2T", [JP, D], F32)
    pcolsI = din("pcols", [128, NPAR * KT], F32)
    bqcolsI = din("bqcols", [128, 48], F32)
    constsI = din("consts", [128, NCONST], F32)
    sclI = din("scl", [128, 2], F32)
    e1024I = din("e1024", [128, 8], F32)
    revrampI = din("revramp", [128, TC], F32)
    masksI = din("masks", [128, 4 * 512], BF16)
    identI = din("ident", [128, 128], BF16)
    idn8I = din("idn8", [8, 8], F32)
    idnfI = din("idnf", [128, 128], F32)
    onescolI = din("onescol", [128, 1], F32)
    onesrowI = din("onesrow", [1, 128], F32)
    onesfI = din("onesf", [128, 128], F32)

    yout = nc.dram_tensor("y", [D, TC], F32, kind="ExternalOutput").ap()

    dkind = "ExternalOutput" if DBG else "Internal"
    k_hbm = nc.dram_tensor("k_hbm", [D, TC], F32, kind=dkind).ap()
    v_hbm = nc.dram_tensor("v_hbm", [D, TC], BF16, kind=dkind).ap()
    sr_hbm = nc.dram_tensor("sr_hbm", [D, TC], BF16, kind=dkind).ap()
    cin_tot = dint("cin_tot", [2 * KT * 128], F32)
    cout_tot = dint("cout_tot", [NCORE, 2 * KT * 128], F32)
    cin_sc = dint("cin_sc", [128 * 512], F32)
    cout_sc = dint("cout_sc", [NCORE, 128 * 512], F32)
    cin_kqv = dint("cin_kqv", [3 * D, TC], BF16)
    cout_kqv = dint("cout_kqv", [NCORE, 3 * D, TC], BF16)
    cin_o = dint("cin_o", [HPC * HD, T], BF16)
    cout_o = dint("cout_o", [NCORE, HPC * HD, T], BF16)
    q_loc = dint("q_loc", [HPC * HD, T], BF16)
    o_loc = dint("o_loc", [D, TC], BF16)
    q1_hbm = dint("q1_hbm", [JP, D], BF16)
    q3_hbm = dint("q3_hbm", [JP, D], BF16)
    q2_hbm = dint("q2_hbm", [JP, D], BF16)

    with es:
        X = es.enter_context(nc.sbuf_tensor("X", [128, KT * (TC + 1)], F32))
        BB = es.enter_context(nc.sbuf_tensor("BB", [128, KT * TC], BF16))
        PC = es.enter_context(nc.sbuf_tensor("PC", [128, NPAR * KT], F32))
        BQ = es.enter_context(nc.sbuf_tensor("BQ", [128, 48], F32))
        CST = es.enter_context(nc.sbuf_tensor("CST", [128, NCONST], F32))
        SCLT = es.enter_context(nc.sbuf_tensor("SCLT", [128, 2], F32))
        E8 = es.enter_context(nc.sbuf_tensor("E8", [128, 8], F32))
        RR = es.enter_context(nc.sbuf_tensor("RR", [128, TC], F32))
        MSK = es.enter_context(nc.sbuf_tensor("MSK", [128, 4 * 512], BF16))
        IDN = es.enter_context(nc.sbuf_tensor("IDN", [128, 128], BF16))
        ID8 = es.enter_context(nc.sbuf_tensor("ID8", [8, 8], F32))
        IDF = es.enter_context(nc.sbuf_tensor("IDF", [128, 128], F32))
        OCol = es.enter_context(nc.sbuf_tensor("OCol", [128, 1], F32))
        ORow = es.enter_context(nc.sbuf_tensor("ORow", [1, 128], F32))
        ONESF = es.enter_context(nc.sbuf_tensor("ONESF", [128, 128], F32))
        DER = es.enter_context(nc.sbuf_tensor("DER", [128, 3 * KT + 8], F32))
        FCT = es.enter_context(nc.sbuf_tensor("FCT", [128, 8 * KT], F32))
        MU = es.enter_context(nc.sbuf_tensor("MU", [128, TC + 1], F32))
        RSTD = es.enter_context(nc.sbuf_tensor("RSTD", [128, TC + 1], F32))
        ST1 = es.enter_context(nc.sbuf_tensor("ST1", [1, TC + 1], F32))
        ST2 = es.enter_context(nc.sbuf_tensor("ST2", [1, TC + 1], F32))
        SROW = es.enter_context(nc.sbuf_tensor("SROW", [1, TC + 1], F32))
        CARR = es.enter_context(nc.sbuf_tensor("CARR", [128, 2 * KT], F32))
        TOTS = es.enter_context(nc.sbuf_tensor("TOTS", [128, 2 * KT], F32))
        SC4 = es.enter_context(nc.sbuf_tensor("SC4", [8, 4], F32))
        SCD = es.enter_context(nc.sbuf_tensor("SCD", [1, 8], F32))
        TOTT = es.enter_context(nc.sbuf_tensor("TOTT", [32, 128], F32))

        def Xt(i, lo=0, hi=TC + 1):
            return X[:, i * (TC + 1) + lo: i * (TC + 1) + hi]

        def Bt(i, lo=0, hi=TC):
            return BB[:, i * TC + lo: i * TC + hi]

        def pcol(name, i):
            j = PAR[name] * KT + i
            return PC[:, j:j + 1]

        def dw(i):  return DER[:, i:i + 1]
        def dew(i): return DER[:, KT + i:KT + i + 1]
        def deb(i): return DER[:, 2 * KT + i:2 * KT + i + 1]
        DSB = 3 * KT
        def dsc(j): return DER[:, DSB + j:DSB + j + 1]
        def cc(j):  return CST[:, j:j + 1]
        def cc1(j): return CST[0:1, j:j + 1]

        s_ld = SC(nc, es, "s_ld")
        s_st = SC(nc, es, "s_st")
        s_gld = SC(nc, es, "s_gld")
        s_ast = SC(nc, es, "s_ast")
        s_act = SC(nc, es, "s_act")
        s_dve = SC(nc, es, "s_dve")
        s_wld = SC(nc, es, "s_wld")
        s_wc = SC(nc, es, "s_wc")
        s_mm = SC(nc, es, "s_mm")
        s_ev = SC(nc, es, "s_ev")
        s_cc = SC(nc, es, "s_cc")

        block = es.enter_context(nc.Block())

        @block.sync
        def _(sync):
            for i in range(KT):
                s_ld.inc(sync.dma_start(out=Xt(i), in_=xT[i * 128:(i + 1) * 128, :]), 16)
            for dst, src in [(PC, pcolsI), (BQ, bqcolsI), (CST, constsI), (SCLT, sclI),
                             (E8, e1024I), (RR, revrampI), (MSK, masksI), (IDN, identI),
                             (ID8, idn8I), (IDF, idnfI), (OCol, onescolI), (ORow, onesrowI),
                             (ONESF, onesfI)]:
                s_ld.inc(sync.dma_start(out=dst[:], in_=src[:]), 16)
        SETUP_LD = s_ld.val

        a0 = s_act.val

        @block.scalar
        def _(s):
            s.wait_ge(s_ld.sem, SETUP_LD)
            s_act.inc(s.activation(DER[:, 0:KT], PC[:, PAR["td"] * KT:(PAR["td"] + 1) * KT], AF.Exp))

        d0 = s_dve.val

        @block.vector
        def _(v):
            v.wait_ge(s_act.sem, a0 + 1)
            s_dve.inc(v.tensor_scalar(DER[:, 0:KT], DER[:, 0:KT], cc(C_NEG1), None, op0=OP.mult))
            v.wait_ge(s_dve.sem, d0 + 1)
            for i in range(KT - 1):
                v.tensor_scalar(FCT[:, 8 * i:8 * i + 8], E8[:], dw(i), None, op0=OP.mult)
            s_dve.inc(v.tensor_scalar(FCT[:, 8 * (KT - 1):8 * KT], E8[:], dw(KT - 1), None, op0=OP.mult))

        @block.scalar
        def _(s):
            s.wait_ge(s_dve.sem, d0 + 2)
            s.activation(DER[:, KT:2 * KT], DER[:, 0:KT], AF.Exp)
            s.activation(FCT[:], FCT[:], AF.Exp)
            s_act.inc(s.activation(DER[:, 2 * KT:3 * KT], PC[:, PAR["tf"] * KT:(PAR["tf"] + 1) * KT], AF.Exp))
        DER_READY = s_act.val

        dve_scb = s_dve.val

        @block.vector
        def _(v):
            v.wait_ge(s_ld.sem, SETUP_LD)
            v.reciprocal(DER[:, DSB:DSB + 1], SCLT[:, 1:2])
            s_dve.inc(v.tensor_tensor(DER[:, DSB + 1:DSB + 2], SCLT[:, 0:1], SCLT[:, 1:2],
                                      op=OP.mult))
        SCALES_DVE = s_dve.val

        def emit_ln(off, ncols, tag, pre_wait=None, upto=4):
            if pre_wait is not None and not isinstance(pre_wait, list):
                pre_wait = [pre_wait]
            ps = ExitStack()
            chunks = []
            c0 = 0
            while c0 < ncols:
                chunks.append((c0, min(512, ncols - c0)))
                c0 += 512
            nch = len(chunks)
            psx = [ps.enter_context(nc.psum_tensor(f"psx{tag}{c}", [1, 512], F32)) for c in range(nch)]
            psq = [ps.enter_context(nc.psum_tensor(f"psq{tag}{c}", [1, 512], F32)) for c in range(nch)]
            sqb = [ps.enter_context(nc.sbuf_tensor(f"sq{tag}{c}", [128, ncols], F32)) for c in range(2)]

            act_b = s_act.val
            mm_b = s_mm.val

            @block.scalar
            def _(s):
                s.wait_ge(s_ld.sem, SETUP_LD)
                if pre_wait is not None:
                    for pw_s, pw_v in pre_wait:
                        s.wait_ge(pw_s.sem, pw_v)
                for i in range(KT):
                    if i >= 2:
                        s.wait_ge(s_mm.sem, mm_b + (i - 1))
                    s_act.inc(s.activation(sqb[i % 2][:], Xt(i, off, off + ncols), AF.Square))

            @block.tensor
            def _(t):
                t.wait_ge(s_ld.sem, SETUP_LD)
                if pre_wait is not None:
                    for pw_s, pw_v in pre_wait:
                        t.wait_ge(pw_s.sem, pw_v)
                for i in range(KT):
                    for ci, (o, w) in enumerate(chunks):
                        t.matmul(psx[ci][:, 0:w], OCol[:], Xt(i, off + o, off + o + w),
                                 start=(i == 0), stop=(i == KT - 1))
                    t.wait_ge(s_act.sem, act_b + i + 1)
                    inst = None
                    for ci, (o, w) in enumerate(chunks):
                        inst = t.matmul(psq[ci][:, 0:w], OCol[:], sqb[i % 2][:, o:o + w],
                                        start=(i == 0), stop=(i == KT - 1))
                    s_mm.inc(inst)
            MM_STATS = s_mm.val

            dve_lnb = s_dve.val

            if upto < 2:
                ps.close()
                return s_ev.val

            @block.vector
            def _(v):
                v.wait_ge(s_mm.sem, MM_STATS)
                for ci, (o, w) in enumerate(chunks):
                    v.tensor_scalar(ST1[0:1, o:o + w], psx[ci][:, 0:w], cc1(C_INVD), None, op0=OP.mult)
                    v.tensor_scalar(ST2[0:1, o:o + w], psq[ci][:, 0:w], cc1(C_INVD), None, op0=OP.mult)
                s_dve.inc(v.engine_nop())
                v.wait_ge(s_dve.sem, dve_lnb + 1)
                s_dve.inc(v.tensor_tensor(SROW[0:1, 0:ncols], ST1[0:1, 0:ncols], ST1[0:1, 0:ncols],
                                          op=OP.mult))
                v.wait_ge(s_dve.sem, dve_lnb + 2)
                s_dve.inc(v.tensor_tensor(ST2[0:1, 0:ncols], ST2[0:1, 0:ncols], SROW[0:1, 0:ncols],
                                          op=OP.subtract))
            DVE_VAR = s_dve.val

            act_b2 = s_act.val

            if upto < 3:
                ps.close()
                return s_ev.val

            @block.scalar
            def _(s):
                s.wait_ge(s_dve.sem, DVE_VAR)
                s_act.inc(s.activation(SROW[0:1, 0:ncols], ST2[0:1, 0:ncols], AF.Sqrt, bias=cc1(C_EPS)))

            @block.vector
            def _(v):
                v.wait_ge(s_act.sem, act_b2 + 1)
                s_dve.inc(v.reciprocal(SROW[0:1, 0:ncols], SROW[0:1, 0:ncols]))
            DVE_RSTD = s_dve.val

            ps.close()
            if upto < 4:
                return s_ev.val
            pb = ExitStack()
            psbc = [pb.enter_context(nc.psum_tensor(f"psb{tag}{c}", [128, 512], F32))
                    for c in range(2 * nch)]
            mm_b2 = s_mm.val

            @block.tensor
            def _(t):
                t.wait_ge(s_dve.sem, DVE_RSTD)
                inst = None
                for ci, (o, w) in enumerate(chunks):
                    t.matmul(psbc[2 * ci][:, 0:w], ORow[:], ST1[0:1, o:o + w], start=True, stop=True)
                    inst = t.matmul(psbc[2 * ci + 1][:, 0:w], ORow[:], SROW[0:1, o:o + w],
                                    start=True, stop=True)
                s_mm.inc(inst)
            MM_B = s_mm.val

            @block.vector
            def _(v):
                v.wait_ge(s_mm.sem, MM_B)
                inst = None
                for ci, (o, w) in enumerate(chunks):
                    v.tensor_copy(MU[:, o:o + w], psbc[2 * ci][:, 0:w])
                    inst = v.tensor_copy(RSTD[:, o:o + w], psbc[2 * ci + 1][:, 0:w])
                s_ev.inc(inst)
            pb.close()
            return s_ev.val

        def emit_proj(tag, wdram, ncols, nk, rhs_fn, evac_fn, evac_eng, rhs_wait,
                      store_fn=None, wdt=F32, pre_waits=()):
            ps = ExitStack()
            nj = ncols // 128
            ntb = 2
            wf = ps.enter_context(nc.sbuf_tensor(f"wf{tag}", [128, nk * 128], wdt))
            wb = [ps.enter_context(nc.sbuf_tensor(f"wb{tag}{p}", [128, nk * 128], BF16))
                  for p in range(2)]
            NB = 4
            pmm = [ps.enter_context(nc.psum_tensor(f"pp{tag}{b}", [128, 512], F32)) for b in range(NB)]

            wld_b = s_wld.val
            wc_b = s_wc.val
            mm_b = s_mm.val
            ev_b = s_ev.val
            st_b = s_st.val

            @block.sync
            def _(sync):
                for sc_, v_ in pre_waits:
                    sync.wait_ge(sc_.sem, v_)
                for j in range(nj):
                    sync.wait_ge(s_wc.sem, wc_b + j)
                    s_wld.inc(sync.dma_start(out=wf[:], in_=wdram[j * 128:(j + 1) * 128, :]), 16)

            def _emit_evac_tile(e, j):
                for tb in range(ntb):
                    g = ntb * j + tb
                    e.wait_ge(s_mm.sem, mm_b + g + 1)
                    if store_fn is not None and tb == 0:
                        e.wait_ge(s_st.sem, st_b + 16 * (j - 1) if j >= 2 else st_b)
                    inst = evac_fn(e, pmm[g % NB][:], j, tb)
                    s_ev.inc(inst)

            if evac_eng == "scalar":
                @block.scalar
                def _(s):
                    for j in range(nj):
                        s.wait_ge(s_wld.sem, wld_b + 16 * (j + 1))
                        s.wait_ge(s_mm.sem, mm_b + ntb * (j - 1) if j >= 2 else mm_b)
                        s_wc.inc(s.activation(wb[j % 2][:], wf[:], AF.Identity))
                        if j >= 2:
                            _emit_evac_tile(s, j - 2)
                    for j in (nj - 2, nj - 1):
                        _emit_evac_tile(s, j)
            else:
                @block.scalar
                def _(s):
                    for j in range(nj):
                        s.wait_ge(s_wld.sem, wld_b + 16 * (j + 1))
                        s.wait_ge(s_mm.sem, mm_b + ntb * (j - 1) if j >= 2 else mm_b)
                        s_wc.inc(s.activation(wb[j % 2][:], wf[:], AF.Identity))

            @block.tensor
            def _(t):
                t.wait_ge(rhs_wait[0].sem, rhs_wait[1])
                for j in range(nj):
                    t.wait_ge(s_wc.sem, wc_b + j + 1)
                    for tb in range(ntb):
                        g = ntb * j + tb
                        t.wait_ge(s_ev.sem, ev_b + g - NB + 1 if g >= NB else ev_b)
                        inst = None
                        for kk in range(nk):
                            inst = t.matmul(pmm[g % NB][:],
                                            wb[j % 2][:, kk * 128:(kk + 1) * 128],
                                            rhs_fn(kk, tb),
                                            start=(kk == 0), stop=(kk == nk - 1))
                        s_mm.inc(inst)

            if evac_eng != "scalar":
                @block.vector
                def _(v):
                    for j in range(nj):
                        _emit_evac_tile(v, j)

            if store_fn is not None:
                @block.gpsimd
                def _(g):
                    for j in range(nj):
                        g.wait_ge(s_ev.sem, ev_b + ntb * (j + 1))
                        s_st.inc(store_fn(g, j), 16)
            ps.close()

        LN1_EV = emit_ln(0, TC + 1, "l1")

        l0 = ExitStack()
        xs1 = l0.enter_context(nc.sbuf_tensor("xs1", [128, TC + 1], F32))
        xs2 = l0.enter_context(nc.sbuf_tensor("xs2", [128, TC], F32))
        kvst = [l0.enter_context(nc.sbuf_tensor(f"kvst{p}", [128, TC], F32)) for p in range(2)]
        bfst = [l0.enter_context(nc.sbuf_tensor(f"bfst{p}", [128, TC], BF16)) for p in range(2)]

        def emit_lerp(parname, waitv):
            @block.vector
            def _(v):
                v.wait_ge(s_ev.sem, LN1_EV)
                if waitv is not None:
                    v.wait_ge(s_mm.sem, waitv)
                for i in range(KT):
                    v.tensor_tensor(xs1[:], Xt(i), MU[:, 0:TC + 1], op=OP.subtract)
                    v.tensor_tensor(xs1[:], xs1[:], RSTD[:, 0:TC + 1], op=OP.mult)
                    v.tensor_tensor(xs2[:], xs1[:, 1:TC + 1], xs1[:, 0:TC], op=OP.subtract)
                    s_dve.inc(v.scalar_tensor_tensor(Bt(i), xs2[:], pcol(parname, i),
                                                     xs1[:, 0:TC], op0=OP.mult, op1=OP.add))
            return s_dve.val

        LERP_K = emit_lerp("tmk", None)

        def evac_k(e, psum, j, tb):
            return e.activation(kvst[j % 2][:, tb * 512:(tb + 1) * 512], psum, AF.Identity,
                                bias=pcol("bk", j))

        emit_proj("k", wkT, D, KT, lambda kk, tb: Bt(kk, tb * 512, (tb + 1) * 512),
                  evac_k, "scalar", (s_dve, LERP_K),
                  store_fn=lambda g, j: g.dma_start(out=k_hbm[j * 128:(j + 1) * 128, :],
                                                    in_=kvst[j % 2][:]))

        LERP_V = emit_lerp("tmv", s_mm.val)

        def evac_v(e, psum, j, tb):
            return e.activation(bfst[j % 2][:, tb * 512:(tb + 1) * 512], psum, AF.Identity,
                                bias=pcol("bv", j))

        emit_proj("v", wvT, D, KT, lambda kk, tb: Bt(kk, tb * 512, (tb + 1) * 512),
                  evac_v, "scalar", (s_dve, LERP_V),
                  store_fn=lambda g, j: g.dma_start(out=v_hbm[j * 128:(j + 1) * 128, :],
                                                    in_=bfst[j % 2][:]))

        LERP_R = emit_lerp("tmr", s_mm.val)

        def evac_r(e, psum, j, tb):
            return e.activation(bfst[j % 2][:, tb * 512:(tb + 1) * 512], psum, AF.Sigmoid,
                                bias=pcol("br", j))

        emit_proj("r", wrT, D, KT, lambda kk, tb: Bt(kk, tb * 512, (tb + 1) * 512),
                  evac_r, "scalar", (s_dve, LERP_R),
                  store_fn=lambda g, j: g.dma_start(out=sr_hbm[j * 128:(j + 1) * 128, :],
                                                    in_=bfst[j % 2][:]))
        KVRS_ST = s_st.val

        l0.close()

        scn = ExitStack()
        kbuf = [scn.enter_context(nc.sbuf_tensor(f"kbf{p}", [128, TC], F32)) for p in range(2)]
        vbuf = [scn.enter_context(nc.sbuf_tensor(f"vbf{p}", [128, TC], BF16)) for p in range(2)]
        srbuf = [scn.enter_context(nc.sbuf_tensor(f"srbf{p}", [128, TC], BF16)) for p in range(2)]
        ekb = [scn.enter_context(nc.sbuf_tensor(f"ekbf{p}", [128, TC], F32)) for p in range(2)]
        argb = scn.enter_context(nc.sbuf_tensor("argb", [128, TC], F32))
        wrev = scn.enter_context(nc.sbuf_tensor("wrev", [128, TC], F32))
        ubuf = scn.enter_context(nc.sbuf_tensor("ubuf", [128, TC], F32))
        tbuf = scn.enter_context(nc.sbuf_tensor("tbuf", [128, TC], F32))
        arun = scn.enter_context(nc.sbuf_tensor("arun", [128, TC], F32))
        brun = scn.enter_context(nc.sbuf_tensor("brun", [128, TC], F32))
        aprev = scn.enter_context(nc.sbuf_tensor("aprev", [128, TC], F32))
        bprev = scn.enter_context(nc.sbuf_tensor("bprev", [128, TC], F32))
        ga8 = scn.enter_context(nc.sbuf_tensor("ga8", [8, 256], F32))
        gat = scn.enter_context(nc.sbuf_tensor("gat", [128, 16], F32))

        gldA = s_gld.val
        actA = s_act.val
        dveA = s_dve.val

        @block.gpsimd
        def _(g):
            g.wait_ge(s_st.sem, KVRS_ST)
            for i in range(KT):
                if i >= 2:
                    g.wait_ge(s_dve.sem, dveA + 2 * (i - 2) + 2)
                s_gld.inc(g.dma_start(out=kbuf[i % 2][:], in_=k_hbm[i * 128:(i + 1) * 128, :]), 16)
                s_gld.inc(g.dma_start(out=vbuf[i % 2][:], in_=v_hbm[i * 128:(i + 1) * 128, :]), 16)

        @block.scalar
        def _(s):
            s.wait_ge(s_act.sem, DER_READY)
            for i in range(KT):
                s.wait_ge(s_gld.sem, gldA + 32 * i + 16)
                if i >= 2:
                    s.wait_ge(s_dve.sem, dveA + 2 * (i - 2) + 2)
                s_act.inc(s.activation(ekb[i % 2][:], kbuf[i % 2][:], AF.Exp))
                s.wait_ge(s_dve.sem, dveA + 2 * i + 1)
                s_act.inc(s.activation(wrev[:], argb[:], AF.Exp))

        @block.vector
        def _(v):
            v.wait_ge(s_act.sem, DER_READY)
            v.wait_ge(s_st.sem, KVRS_ST)
            for i in range(KT):
                if i >= 1:
                    v.wait_ge(s_act.sem, actA + 2 * i)
                s_dve.inc(v.tensor_scalar(argb[:], RR[:], dw(i), None, op0=OP.mult))
                v.wait_ge(s_act.sem, actA + 2 * i + 2)
                v.wait_ge(s_gld.sem, gldA + 32 * (i + 1))
                v.tensor_tensor(ubuf[:], ekb[i % 2][:], vbuf[i % 2][:], op=OP.mult)
                v.tensor_tensor(tbuf[:], ubuf[:], wrev[:], op=OP.mult)
                v.tensor_reduce(TOTS[:, i:i + 1], tbuf[:], axis=AX.X, op=OP.add)
                v.tensor_tensor(tbuf[:], ekb[i % 2][:], wrev[:], op=OP.mult)
                s_dve.inc(v.tensor_reduce(TOTS[:, KT + i:KT + i + 1], tbuf[:], axis=AX.X, op=OP.add))
        PASSA_DVE = s_dve.val

        ccA = s_cc.val
        ptot_es = ExitStack()
        ptot = ptot_es.enter_context(nc.psum_tensor("ptot", [32, 128], F32))
        mmT_b = s_mm.val

        @block.tensor
        def _(t):
            t.wait_ge(s_dve.sem, PASSA_DVE)
            s_mm.inc(t.transpose(ptot[:], TOTS[:], IDF[:]))

        dveT_b = s_dve.val

        @block.vector
        def _(v):
            v.wait_ge(s_mm.sem, mmT_b + 1)
            s_dve.inc(v.tensor_copy(TOTT[:], ptot[:]))
        ptot_es.close()

        @block.gpsimd
        def _(g):
            g.wait_ge(s_dve.sem, dveT_b + 1)
            s_st.inc(g.dma_start(out=cin_tot.rearrange("(r p) -> r p", r=32), in_=TOTT[:]), 16)
            g.wait_ge(s_st.sem, s_st.val)
            s_cc.inc(g.collective_compute(
                "AllGather", OP.bypass, replica_groups=[list(range(NCORE))],
                ins=[cin_tot[:]], outs=[cout_tot[:]]))

        ldC = s_ld.val
        mmC = s_mm.val
        dveC = s_dve.val
        pcar = ExitStack()
        pga = pcar.enter_context(nc.psum_tensor("pga", [128, 8], F32))
        pgb = pcar.enter_context(nc.psum_tensor("pgb", [128, 8], F32))

        @block.sync
        def _(sync):
            sync.wait_ge(s_cc.sem, ccA + 1)
            for i in range(KT):
                if i >= 1:
                    sync.wait_ge(s_mm.sem, mmC + 2 * i)
                cg = cout_tot
                s_ld.inc(sync.dma_start(out=ga8[:, 0:128], in_=cg[:, i * 128:(i + 1) * 128]), 16)
                s_ld.inc(sync.dma_start(out=ga8[:, 128:256],
                                        in_=cg[:, 2048 + i * 128:2048 + (i + 1) * 128]), 16)

        @block.tensor
        def _(t):
            for i in range(KT):
                t.wait_ge(s_ld.sem, ldC + 32 * (i + 1))
                if i >= 1:
                    t.wait_ge(s_dve.sem, dveC + 3 * i)
                t.transpose(pga[:], ga8[:, 0:128], ID8[:])
                s_mm.inc(t.transpose(pgb[:], ga8[:, 128:256], ID8[:]), 2)

        @block.vector
        def _(v):
            for i in range(KT):
                v.wait_ge(s_mm.sem, mmC + 2 * (i + 1))
                v.tensor_tensor(gat[:, 0:8], pga[:], FCT[:, 8 * i:8 * i + 8], op=OP.mult)
                s_dve.inc(v.tensor_tensor(gat[:, 8:16], pgb[:], FCT[:, 8 * i:8 * i + 8], op=OP.mult))
                v.wait_ge(s_dve.sem, dveC + 3 * i + 1)
                v.tensor_reduce(CARR[:, i:i + 1], gat[:, 0:8], axis=AX.X, op=OP.add)
                s_dve.inc(v.tensor_reduce(CARR[:, KT + i:KT + i + 1], gat[:, 8:16],
                                          axis=AX.X, op=OP.add), 2)
        CARR_DVE = s_dve.val
        pcar.close()

        gldB = s_gld.val
        actB = s_act.val
        dveB = s_dve.val

        @block.gpsimd
        def _(g):
            for i in range(KT):
                if i >= 2:
                    g.wait_ge(s_dve.sem, dveB + (i - 1))
                s_gld.inc(g.dma_start(out=kbuf[i % 2][:], in_=k_hbm[i * 128:(i + 1) * 128, :]), 16)
                s_gld.inc(g.dma_start(out=vbuf[i % 2][:], in_=v_hbm[i * 128:(i + 1) * 128, :]), 16)
                s_gld.inc(g.dma_start(out=srbuf[i % 2][:], in_=sr_hbm[i * 128:(i + 1) * 128, :]), 16)

        @block.scalar
        def _(s):
            for i in range(KT):
                s.wait_ge(s_gld.sem, gldB + 48 * i + 16)
                if i >= 2:
                    s.wait_ge(s_dve.sem, dveB + (i - 1))
                s_act.inc(s.activation(ekb[i % 2][:], kbuf[i % 2][:], AF.Exp))

        @block.vector
        def _(v):
            v.wait_ge(s_dve.sem, CARR_DVE)
            for i in range(KT):
                v.wait_ge(s_act.sem, actB + i + 1)
                v.wait_ge(s_gld.sem, gldB + 48 * (i + 1))
                ek = ekb[i % 2]
                v.tensor_tensor(ubuf[:], ek[:], vbuf[i % 2][:], op=OP.mult)
                v.tensor_tensor_scan(arun[:], dew(i).broadcast_to([128, TC]), ubuf[:],
                                     CARR[:, i:i + 1], op0=OP.mult, op1=OP.add)
                v.tensor_tensor_scan(brun[:], dew(i).broadcast_to([128, TC]), ek[:],
                                     CARR[:, KT + i:KT + i + 1], op0=OP.mult, op1=OP.add)
                v.tensor_copy(aprev[:, 1:TC], arun[:, 0:TC - 1])
                v.tensor_copy(aprev[:, 0:1], CARR[:, i:i + 1])
                v.tensor_copy(bprev[:, 1:TC], brun[:, 0:TC - 1])
                v.tensor_copy(bprev[:, 0:1], CARR[:, KT + i:KT + i + 1])
                v.scalar_tensor_tensor(ubuf[:], ubuf[:], deb(i), aprev[:], op0=OP.mult, op1=OP.add)
                v.scalar_tensor_tensor(tbuf[:], ek[:], deb(i), bprev[:], op0=OP.mult, op1=OP.add)
                v.reciprocal(tbuf[:], tbuf[:])
                v.tensor_tensor(ubuf[:], ubuf[:], tbuf[:], op=OP.mult)
                s_dve.inc(v.tensor_tensor(Bt(i), ubuf[:], srbuf[i % 2][:], op=OP.mult))
        SCAN_DVE = s_dve.val
        scn.close()

        wo_ps = ExitStack()

        def evac_wo(e, psum, j, tb):
            xsl = Xt(j, 1 + tb * 512, 1 + (tb + 1) * 512)
            return e.scalar_tensor_tensor(xsl, psum, pcol("bo", j), xsl, op0=OP.add, op1=OP.add)

        emit_proj("wo", woT, D, KT, lambda kk, tb: Bt(kk, tb * 512, (tb + 1) * 512),
                  evac_wo, "vector", (s_dve, SCAN_DVE), pre_waits=((s_dve, SCAN_DVE),))
        wo_ps.close()
        STAGE1_EV = s_ev.val

        if stages >= 2:
            LN2_MM0 = s_mm.val
            LN2_EV = emit_ln(1, TC, "l2", pre_wait=(s_ev, s_ev.val))
            LN2_MM = s_mm.val

            f0 = ExitStack()
            xq1 = f0.enter_context(nc.sbuf_tensor("xq1", [128, TC], F32))
            fq8 = f0.enter_context(nc.sbuf_tensor("fq8", [128, TC], FP8))
            dve_q = s_dve.val

            if F8MODE >= 1:
                @block.vector
                def _(v):
                    v.wait_ge(s_ev.sem, LN2_EV)
                    v.wait_ge(s_dve.sem, SCALES_DVE)
                    for i in range(KT):
                        v.tensor_tensor(xq1[:], Xt(i, 1, TC + 1), MU[:, 0:TC], op=OP.subtract)
                        v.tensor_tensor(xq1[:], xq1[:], RSTD[:, 0:TC], op=OP.mult)
                        v.tensor_scalar(xq1[:], xq1[:], dsc(0), None, op0=OP.mult)
                        v.tensor_copy(fq8[:], xq1[:])
                        s_dve.inc(v.tensor_copy(Bt(i), fq8[:]))
            QDQ_DVE = s_dve.val

            def evac_f8(e, psum, j, tb):
                xsl = Xt(j, 1 + tb * 512, 1 + (tb + 1) * 512)
                return e.scalar_tensor_tensor(xsl, psum, dsc(1), xsl, op0=OP.mult, op1=OP.add)

            if F8MODE >= 2:
                emit_proj("f8", w8T, D, KT, lambda kk, tb: Bt(kk, tb * 512, (tb + 1) * 512),
                          evac_f8, "vector", (s_dve, QDQ_DVE), wdt=FP8,
                          pre_waits=((s_mm, LN2_MM), (s_dve, QDQ_DVE)))
            f0.close()


        if stages >= 3:
            LN3_EV = emit_ln(1, TC, "l3", pre_wait=(s_ev, s_ev.val))
            LN3_MM = s_mm.val

            a3 = ExitStack()
            xs3 = a3.enter_context(nc.sbuf_tensor("xs3", [128, TC], F32))
            qst = [a3.enter_context(nc.sbuf_tensor(f"qst{p}", [128, TC], BF16)) for p in range(2)]
            dve_x3 = s_dve.val

            @block.vector
            def _(v):
                v.wait_ge(s_ev.sem, LN3_EV)
                for i in range(KT):
                    v.tensor_tensor(xs3[:], Xt(i, 1, TC + 1), MU[:, 0:TC], op=OP.subtract)
                    s_dve.inc(v.tensor_tensor(Bt(i), xs3[:], RSTD[:, 0:TC], op=OP.mult))
            XLN3_DVE = s_dve.val

            def evac_qkv(e, psum, j, tb):
                return e.activation(qst[j % 2][:, tb * 512:(tb + 1) * 512], psum, AF.Identity,
                                    bias=BQ[:, j:j + 1])

            emit_proj("qkv", wqkv, 3 * D, KT, lambda kk, tb: Bt(kk, tb * 512, (tb + 1) * 512),
                      evac_qkv, "scalar", (s_dve, XLN3_DVE),
                      store_fn=lambda g, j: g.dma_start(out=cin_kqv[j * 128:(j + 1) * 128, :],
                                                        in_=qst[j % 2][:]),
                      pre_waits=((s_mm, LN3_MM),))
            QKV_ST = s_st.val
            a3.close()

            cc_kb = s_cc.val

            @block.gpsimd
            def _(g):
                g.wait_ge(s_st.sem, QKV_ST)
                s_cc.inc(g.collective_compute(
                    "AllGather", OP.bypass, replica_groups=[list(range(NCORE))],
                    ins=[cin_kqv[:]], outs=[cout_kqv[:]]))
            CC_KQV = s_cc.val

            att = ExitStack()
            Ksb = att.enter_context(nc.sbuf_tensor("Ksb", [128, T], BF16))
            Vsb = att.enter_context(nc.sbuf_tensor("Vsb", [128, T], BF16))
            Qsb = att.enter_context(nc.sbuf_tensor("Qsb", [128, T], BF16))
            Psb = [att.enter_context(nc.sbuf_tensor(f"Psb{p}", [128, 512], BF16)) for p in range(2)]
            lacc = att.enter_context(nc.sbuf_tensor("lacc", [128, 512], F32))
            lrow = att.enter_context(nc.sbuf_tensor("lrow", [1, 512], F32))
            rbsb = att.enter_context(nc.sbuf_tensor("rbsb", [128, 512], F32))
            Ostg = [att.enter_context(nc.sbuf_tensor(f"Ostg{p}", [128, 512], BF16)) for p in range(2)]
            pvt = [att.enter_context(nc.psum_tensor(f"pvt{p}", [128, 128], BF16)) for p in range(2)]
            pS = [att.enter_context(nc.psum_tensor(f"pS{p}", [128, 512], F32)) for p in range(2)]
            pO = att.enter_context(nc.psum_tensor("pO", [128, 512], F32))
            plr = att.enter_context(nc.psum_tensor("plr", [1, 512], F32))
            pbc = att.enter_context(nc.psum_tensor("pbc", [128, 512], F32))

            NQB = T // 512
            NST = T // 128

            for hh in range(HPC):
                ld_h = s_ld.val
                mm_h = s_mm.val
                act_h = s_act.val
                dve_h = s_dve.val
                st_h = s_st.val
                prev_mm = mm_h
                prev_dve = dve_h

                @block.sync
                def _(sync, hh=hh, prev_mm=prev_mm, prev_dve=prev_dve):
                    sync.wait_ge(s_cc.sem, CC_KQV)
                    if hh > 0:
                        sync.wait_ge(s_mm.sem, prev_mm)
                        sync.wait_ge(s_dve.sem, prev_dve)
                    pid = sync.partition_id()
                    preg = sync.to_reg(pid)
                    for r in range(NCORE):
                        hg = HPC * r + hh
                        with sync.If_eq(preg, r):
                            for dst, base in ((Qsb, 0), (Ksb, D), (Vsb, 2 * D)):
                                srcv = cout_kqv[:, base + 128 * hg: base + 128 * (hg + 1), :]
                                srcv = srcv.rearrange("r d t -> d r t")
                                dstv = dst.ap().rearrange("p (r t) -> p r t", t=TC)
                                s_ld.inc(sync.dma_start(out=dstv, in_=srcv), 16)
                        if r != NCORE - 1:
                            s_ld.val -= 48
                LD_H = s_ld.val

                @block.tensor
                def _(t):
                    t.wait_ge(s_ld.sem, LD_H)
                    for st in range(NST):
                        if st >= 2:
                            t.wait_ge(s_dve.sem, dve_h + st - 1)
                        s_mm.inc(t.transpose(pvt[st % 2][:], Vsb[:, st * 128:(st + 1) * 128], IDN[:]))

                @block.vector
                def _(v):
                    for st in range(NST):
                        v.wait_ge(s_mm.sem, mm_h + st + 1)
                        s_dve.inc(v.tensor_copy(Vsb[:, st * 128:(st + 1) * 128], pvt[st % 2][:]))

                mmA = s_mm.val
                actA2 = s_act.val
                dveA2 = s_dve.val
                stA = s_st.val

                mm_S = {}; mm_PV = {}; act_exp = {}; dve_l = {}
                mm_lr = {}; mm_bc = {}; act_rb = {}; dve_rc = {}; dve_om = {}; st_o = {}
                cmm = 0; cact = 0; cdve = 0; cst = 0
                flatS = []
                for qb in range(NQB):
                    ns = 4 * (qb + 1)
                    for st in range(ns):
                        cmm += 1; mm_S[(qb, st)] = cmm
                        cmm += 1; mm_PV[(qb, st)] = cmm
                        cact += 1; act_exp[(qb, st)] = cact
                        cdve += 1; dve_l[(qb, st)] = cdve
                        flatS.append((qb, st))
                    cmm += 1; mm_lr[qb] = cmm
                    cdve += 1; dve_rc[qb] = cdve
                    cmm += 1; mm_bc[qb] = cmm
                    cact += 1; act_rb[qb] = cact
                    cdve += 1; dve_om[qb] = cdve
                    cst += 16; st_o[qb] = cst
                idxS = {k: i for i, k in enumerate(flatS)}

                @block.tensor
                def _(t):
                    t.wait_ge(s_dve.sem, dveA2)
                    for qb in range(NQB):
                        ns = 4 * (qb + 1)
                        for st in range(ns):
                            gi = idxS[(qb, st)]
                            if gi >= 2:
                                t.wait_ge(s_act.sem, actA2 + act_exp[flatS[gi - 2]])
                            s_mm.inc(t.matmul(pS[gi % 2][:], Ksb[:, st * 128:(st + 1) * 128],
                                              Qsb[:, qb * 512:(qb + 1) * 512],
                                              start=True, stop=True))
                            t.wait_ge(s_dve.sem, dveA2 + dve_l[(qb, st)])
                            s_mm.inc(t.matmul(pO[:], Vsb[:, st * 128:(st + 1) * 128],
                                              Psb[gi % 2][:],
                                              start=(st == 0), stop=(st == ns - 1)))
                        t.wait_ge(s_dve.sem, dveA2 + dve_l[(qb, ns - 1)])
                        s_mm.inc(t.matmul(plr[:], OCol[:], lacc[:], start=True, stop=True))
                        t.wait_ge(s_dve.sem, dveA2 + dve_rc[qb])
                        if qb >= 1:
                            t.wait_ge(s_act.sem, actA2 + act_rb[qb - 1])
                        s_mm.inc(t.matmul(pbc[:], ORow[:], lrow[0:1, :], start=True, stop=True))

                @block.scalar
                def _(s):
                    for qb in range(NQB):
                        ns = 4 * (qb + 1)
                        for st in range(ns):
                            gi = idxS[(qb, st)]
                            s.wait_ge(s_mm.sem, mmA + mm_S[(qb, st)])
                            if gi >= 2:
                                s.wait_ge(s_mm.sem, mmA + mm_PV[flatS[gi - 2]])
                            s_act.inc(s.activation(Psb[gi % 2][:], pS[gi % 2][:], AF.Exp,
                                                   scale=cc(C_ATTSC), bias=cc(C_NSHIFT)))
                        s.wait_ge(s_mm.sem, mmA + mm_bc[qb])
                        if qb >= 1:
                            s.wait_ge(s_dve.sem, dveA2 + dve_om[qb - 1])
                        s_act.inc(s.activation(rbsb[:], pbc[:], AF.Identity))

                @block.vector
                def _(v):
                    for qb in range(NQB):
                        ns = 4 * (qb + 1)
                        for st in range(ns):
                            gi = idxS[(qb, st)]
                            v.wait_ge(s_act.sem, actA2 + act_exp[(qb, st)])
                            rel = st - 4 * qb
                            if rel >= 0:
                                v.tensor_tensor(Psb[gi % 2][:], Psb[gi % 2][:],
                                                MSK[:, rel * 512:(rel + 1) * 512], op=OP.mult)
                            if st == 0:
                                if qb >= 1:
                                    v.wait_ge(s_mm.sem, mmA + mm_lr[qb - 1])
                                s_dve.inc(v.tensor_copy(lacc[:], Psb[gi % 2][:]))
                            else:
                                s_dve.inc(v.tensor_tensor(lacc[:], lacc[:], Psb[gi % 2][:],
                                                          op=OP.add))
                        v.wait_ge(s_mm.sem, mmA + mm_lr[qb])
                        s_dve.inc(v.reciprocal(lrow[0:1, :], plr[:]))
                        v.wait_ge(s_act.sem, actA2 + act_rb[qb])
                        if qb >= 2:
                            v.wait_ge(s_st.sem, stA + st_o[qb - 2])
                        s_dve.inc(v.tensor_tensor(Ostg[qb % 2][:], pO[:], rbsb[:], op=OP.mult))

                @block.gpsimd
                def _(g, hh=hh):
                    for qb in range(NQB):
                        g.wait_ge(s_dve.sem, dveA2 + dve_om[qb])
                        s_st.inc(g.dma_start(
                            out=cin_o[hh * 128:(hh + 1) * 128, qb * 512:(qb + 1) * 512],
                            in_=Ostg[qb % 2][:]), 16)

            ATT_ST = s_st.val
            att.close()

            @block.gpsimd
            def _(g):
                g.wait_ge(s_st.sem, ATT_ST)
                s_cc.inc(g.collective_compute(
                    "AllGather", OP.bypass, replica_groups=[list(range(NCORE))],
                    ins=[cin_o[:]], outs=[cout_o[:]]))
            CC_O = s_cc.val

            @block.sync
            def _(sync):
                sync.wait_ge(s_cc.sem, CC_O)
                pid = sync.partition_id()
                preg = sync.to_reg(pid)
                for r in range(NCORE):
                    with sync.If_eq(preg, r):
                        srcv = cout_o[:, :, r * TC:(r + 1) * TC]
                        s_ld.inc(sync.dma_start(
                            out=o_loc.rearrange("(r p) t -> r p t", p=HPC * HD), in_=srcv), 16)
                    if r != NCORE - 1:
                        s_ld.val -= 16
                sync.wait_ge(s_ld.sem, s_ld.val)
                for i in range(KT):
                    s_ld.inc(sync.dma_start(out=Bt(i), in_=o_loc[i * 128:(i + 1) * 128, :]), 16)
            OB_LD = s_ld.val

            def evac_wo1(e, psum, j, tb):
                xsl = Xt(j, 1 + tb * 512, 1 + (tb + 1) * 512)
                return e.scalar_tensor_tensor(xsl, psum, pcol("bo1", j), xsl,
                                              op0=OP.add, op1=OP.add)

            emit_proj("wo1", wo1, D, KT, lambda kk, tb: Bt(kk, tb * 512, (tb + 1) * 512),
                      evac_wo1, "vector", (s_ld, OB_LD), pre_waits=((s_ld, OB_LD),))


        if stages >= 4:
            P4 = _s4parts()
            if "A" in P4:
                t4 = ExitStack()
                wsc = [t4.enter_context(nc.sbuf_tensor(f"wsc{p}", [128, 2048], F32)) for p in range(2)]
                pacc = t4.enter_context(nc.sbuf_tensor("pacc", [128, 512], F32))
                prsc = t4.enter_context(nc.sbuf_tensor("prsc", [128, 1], F32))
                pbk4 = t4.enter_context(nc.psum_tensor("pbk4", [128, 512], F32))

                chunks4 = []
                for r_ in range(NCORE):
                    cl = []
                    for cw, wd in ((0, w1T), (1, w3T), (2, w2T)):
                        r0 = 688 * r_
                        left = 688
                        rr = r0
                        while left > 0:
                            nr = min(128, left)
                            cl.append((wd, rr, nr, 0, 2048, cw))
                            rr += nr
                            left -= nr
                    chunks4.append(cl)
                NCH4 = len(chunks4[0])
                assert all(len(c) == NCH4 for c in chunks4)

                ld4 = s_ld.val
                dve4 = s_dve.val

                @block.vector
                def _(v):
                    v.wait_ge(s_ld.sem, SETUP_LD)
                    s_dve.inc(v.memset(pacc[:], 0.0))

                @block.sync
                def _(sync):
                    sync.wait_ge(s_dve.sem, dve4 + 1)
                    pid = sync.partition_id()
                    preg = sync.to_reg(pid)
                    for c in range(NCH4):
                        if c >= 2:
                            sync.wait_ge(s_dve.sem, dve4 + 1 + 2 * (c - 1))
                        for r_ in range(NCORE):
                            wd, r0, nr, c0, ncl, cw = chunks4[r_][c]
                            with sync.If_eq(preg, r_):
                                s_ld.inc(sync.dma_start(out=wsc[c % 2][0:nr, 0:ncl],
                                                        in_=wd[r0:r0 + nr, c0:c0 + ncl]), 16)
                            if r_ != NCORE - 1:
                                s_ld.val -= 16

                @block.vector
                def _(v):
                    for c in range(NCH4):
                        wd, r0, nr, c0, ncl, cw = chunks4[0][c]
                        v.wait_ge(s_ld.sem, ld4 + 16 * (c + 1))
                        s_dve.inc(v.tensor_reduce(prsc[0:nr, :], wsc[c % 2][0:nr, 0:ncl],
                                                  axis=AX.X, op=OP.add, apply_absolute_value=True))
                        v.wait_ge(s_dve.sem, s_dve.val)
                        s_dve.inc(v.tensor_tensor(pacc[0:nr, cw:cw + 1], pacc[0:nr, cw:cw + 1],
                                                  prsc[0:nr, :], op=OP.add))
                PACC_DVE = s_dve.val

                if S4A >= 2:
                    st4 = s_st.val
                    cc4 = s_cc.val

                    @block.gpsimd
                    def _(g):
                        g.wait_ge(s_dve.sem, PACC_DVE)
                        s_st.inc(g.dma_start(out=cin_sc.rearrange("(p c) -> p c", p=128), in_=pacc[:]), 16)
                        g.wait_ge(s_st.sem, st4 + 16)
                        s_cc.inc(g.collective_compute(
                            "AllGather", OP.bypass, replica_groups=[list(range(NCORE))],
                            ins=[cin_sc[:]], outs=[cout_sc[:]]))
                    CC_SC = s_cc.val

                if S4A >= 3:
                    ld4b = s_ld.val
                    dve4b = s_dve.val

                    @block.sync
                    def _(sync):
                        sync.wait_ge(s_cc.sem, CC_SC)
                        for r_ in range(NCORE):
                            if r_ >= 2:
                                sync.wait_ge(s_dve.sem, dve4b + (r_ - 1))
                            s_ld.inc(sync.dma_start(out=wsc[r_ % 2][:, 0:512],
                                                    in_=cout_sc[r_, :].rearrange("(p c) -> p c", p=128)), 16)

                    @block.vector
                    def _(v):
                        for r_ in range(NCORE):
                            v.wait_ge(s_ld.sem, ld4b + 16 * (r_ + 1))
                            if r_ == 0:
                                s_dve.inc(v.tensor_copy(pacc[:], wsc[0][:, 0:512]))
                            else:
                                s_dve.inc(v.tensor_tensor(pacc[:], pacc[:], wsc[r_ % 2][:, 0:512], op=OP.add))
                    SUM8_DVE = s_dve.val

                if S4A >= 4:
                    mm4 = s_mm.val

                    @block.tensor
                    def _(t):
                        t.wait_ge(s_dve.sem, SUM8_DVE)
                        s_mm.inc(t.matmul(pbk4[:], ONESF[:], pacc[:], start=True, stop=True))

                    dve4c = s_dve.val

                    @block.vector
                    def _(v):
                        v.wait_ge(s_mm.sem, mm4 + 1)
                        s_dve.inc(v.tensor_scalar(DER[:, DSB + 2:DSB + 5], pbk4[:, 0:3], cc(C_INVCNT2),
                                                  None, op0=OP.mult))
                        v.wait_ge(s_dve.sem, dve4c + 1)
                        s_dve.inc(v.tensor_scalar(DER[:, DSB + 5:DSB + 8], DER[:, DSB + 2:DSB + 5],
                                                  cc(C_NEG1), None, op0=OP.mult))
                SCALE4_DVE = s_dve.val
                t4.close()

            if "A" not in P4:
                SCALE4_DVE = s_dve.val
            if "B" in P4:
                tq = ExitStack()
                wq = [tq.enter_context(nc.sbuf_tensor(f"wq{p}", [128, 2048], F32)) for p in range(2)]
                sgA = tq.enter_context(nc.sbuf_tensor("sgA", [128, 2048], F32))
                qbb = [tq.enter_context(nc.sbuf_tensor(f"qbb{p}", [128, 2048], BF16)) for p in range(2)]

                qchunks = []
                for wd, qd, which, nrow, ncol in ((w1T, q1_hbm, 0, JP, D), (w3T, q3_hbm, 1, JP, D),
                                                  (w2T, q2_hbm, 2, JP, D)):
                    for r0 in range(0, nrow, 128):
                        qchunks.append((wd, qd, which, r0, 0, ncol))
                NQC = len(qchunks)

                ldq = s_ld.val
                actq = s_act.val
                dveq = s_dve.val
                astq = s_ast.val

                @block.sync
                def _(sync):
                    sync.wait_ge(s_dve.sem, SCALE4_DVE)
                    for c in range(NQC):
                        if c >= 2:
                            sync.wait_ge(s_dve.sem, dveq + (c - 1))
                        wd, qd, cw, r0, c0, ncl = qchunks[c]
                        s_ld.inc(sync.dma_start(out=wq[c % 2][:, 0:ncl],
                                                in_=wd[r0:r0 + 128, c0:c0 + ncl]), 16)

                @block.scalar
                def _(s):
                    s.wait_ge(s_dve.sem, SCALE4_DVE)
                    for c in range(NQC):
                        wd, qd, cw, r0, c0, ncl = qchunks[c]
                        s.wait_ge(s_ld.sem, ldq + 16 * (c + 1))
                        if c >= 1:
                            _, pqd, _, pr0, pc0, pncl = qchunks[c - 1]
                            s.wait_ge(s_dve.sem, dveq + c)
                            s_ast.inc(s.dma_start(out=pqd[pr0:pr0 + 128, pc0:pc0 + pncl],
                                                  in_=qbb[(c - 1) % 2][:, 0:pncl]), 16)
                        if c >= 2:
                            s.wait_ge(s_ast.sem, astq + 16 * (c - 1))
                        s.activation(sgA[:, 0:ncl], wq[c % 2][:, 0:ncl], AF.Sign, bias=dsc(5 + cw))
                        s_act.inc(s.activation(qbb[c % 2][:, 0:ncl], wq[c % 2][:, 0:ncl], AF.Sign,
                                               bias=dsc(2 + cw)))
                    _, pqd, _, pr0, pc0, pncl = qchunks[NQC - 1]
                    s.wait_ge(s_dve.sem, dveq + NQC)
                    s_ast.inc(s.dma_start(out=pqd[pr0:pr0 + 128, pc0:pc0 + pncl],
                                          in_=qbb[(NQC - 1) % 2][:, 0:pncl]), 16)

                @block.vector
                def _(v):
                    for c in range(NQC):
                        wd, qd, cw, r0, c0, ncl = qchunks[c]
                        v.wait_ge(s_act.sem, actq + c + 1)
                        s_dve.inc(v.tensor_tensor(qbb[c % 2][:, 0:ncl], sgA[:, 0:ncl],
                                                  qbb[c % 2][:, 0:ncl], op=OP.add))
                QNT_AST = s_ast.val
                tq.close()

            if "C" in P4:
                LN4_EV = emit_ln(1, TC, "l4", upto=S4LN,
                                 pre_wait=[(s_ev, s_ev.val), (s_dve, SCALE4_DVE)])

                f4 = ExitStack()
                xs4 = f4.enter_context(nc.sbuf_tensor("xs4", [128, TC], F32))
                dve_x4 = s_dve.val

                if "c" in P4:
                    @block.vector
                    def _(v):
                        v.wait_ge(s_ev.sem, LN4_EV)
                        v.wait_ge(s_mm.sem, s_mm.val)
                        for i in range(KT):
                            v.tensor_tensor(xs4[:], Xt(i, 1, TC + 1), MU[:, 0:TC], op=OP.subtract)
                            s_dve.inc(v.tensor_tensor(Bt(i), xs4[:], RSTD[:, 0:TC], op=OP.mult))
                XLN4_DVE = s_dve.val

            if "D" in P4:
                gsb = f4.enter_context(nc.sbuf_tensor("gsb", [128, JT * 512], BF16))
                q1s = [f4.enter_context(nc.sbuf_tensor(f"q1s{p}", [128, KT * 128], BF16)) for p in range(2)]
                q3s = [f4.enter_context(nc.sbuf_tensor(f"q3s{p}", [128, KT * 128], BF16)) for p in range(2)]
                silb = [f4.enter_context(nc.sbuf_tensor(f"silb{p}", [128, 512], F32)) for p in range(2)]
                q2s = [f4.enter_context(nc.sbuf_tensor(f"q2s{p}", [128, 1024], BF16)) for p in range(2)]

                W2MM = {"v": s_mm.val}
                W2DVE = {"v": s_dve.val}
                for half in range(2):
                    hp = ExitStack()
                    ph1 = [hp.enter_context(nc.psum_tensor(f"ph1{half}{p}", [128, 512], F32)) for p in range(2)]
                    ph3 = [hp.enter_context(nc.psum_tensor(f"ph3{half}{p}", [128, 512], F32)) for p in range(2)]

                    ldh = s_ld.val
                    mmh = s_mm.val
                    acth = s_act.val
                    dveh = s_dve.val
                    w2prev = W2MM["v"]
                    w2dveprev = W2DVE["v"]

                    @block.sync
                    def _(sync, half=half, w2prev=w2prev):
                        sync.wait_ge(s_ast.sem, QNT_AST)
                        sync.wait_ge(s_dve.sem, XLN4_DVE)
                        if half > 0:
                            sync.wait_ge(s_mm.sem, w2prev)
                        for j in range(JT):
                            if j >= 1:
                                sync.wait_ge(s_mm.sem, mmh + 2 * j - 2 if j >= 2 else mmh)
                            s_ld.inc(sync.dma_start(out=q1s[j % 2][:],
                                                    in_=q1_hbm[j * 128:(j + 1) * 128, :]), 16)
                            s_ld.inc(sync.dma_start(out=q3s[j % 2][:],
                                                    in_=q3_hbm[j * 128:(j + 1) * 128, :]), 16)

                    @block.tensor
                    def _(t, half=half, w2prev=w2prev, w2dveprev=w2dveprev):
                        t.wait_ge(s_dve.sem, XLN4_DVE)
                        if half > 0:
                            t.wait_ge(s_mm.sem, w2prev)
                            t.wait_ge(s_dve.sem, w2dveprev)
                        for j in range(JT):
                            t.wait_ge(s_ld.sem, ldh + 32 * (j + 1))
                            if j >= 2:
                                t.wait_ge(s_dve.sem, dveh + (j - 1))
                            inst = None
                            for kk in range(KT):
                                inst = t.matmul(ph1[j % 2][:], q1s[j % 2][:, kk * 128:(kk + 1) * 128],
                                                Bt(kk, half * 512, (half + 1) * 512),
                                                start=(kk == 0), stop=(kk == KT - 1))
                            s_mm.inc(inst)
                            for kk in range(KT):
                                inst = t.matmul(ph3[j % 2][:], q3s[j % 2][:, kk * 128:(kk + 1) * 128],
                                                Bt(kk, half * 512, (half + 1) * 512),
                                                start=(kk == 0), stop=(kk == KT - 1))
                            s_mm.inc(inst)

                    @block.scalar
                    def _(s):
                        for j in range(JT):
                            s.wait_ge(s_mm.sem, mmh + 2 * j + 1)
                            if j >= 2:
                                s.wait_ge(s_dve.sem, dveh + (j - 1))
                            s_act.inc(s.activation(silb[j % 2][:], ph1[j % 2][:], AF.Silu,
                                                   scale=dsc(2)))

                    @block.vector
                    def _(v, half=half):
                        for j in range(JT):
                            v.wait_ge(s_mm.sem, mmh + 2 * j + 2)
                            v.wait_ge(s_act.sem, acth + j + 1)
                            s_dve.inc(v.scalar_tensor_tensor(gsb[:, j * 512:(j + 1) * 512],
                                                             ph3[j % 2][:], dsc(3), silb[j % 2][:],
                                                             op0=OP.mult, op1=OP.mult))
                    G_DVE = s_dve.val
                    hp.close()

                    w2p = ExitStack()
                    pw2 = [w2p.enter_context(nc.psum_tensor(f"pw2{half}{b}", [128, 512], F32))
                           for b in range(8)]
                    for grp in range(2):
                        ldw2 = s_ld.val
                        mmw2 = s_mm.val
                        dvew2 = s_dve.val

                        @block.sync
                        def _(sync, grp=grp, half=half):
                            sync.wait_ge(s_dve.sem, G_DVE)
                            for j in range(JT):
                                if j >= 2:
                                    sync.wait_ge(s_mm.sem, mmw2 + (j - 1))
                                src = q2_hbm[j * 128:(j + 1) * 128, grp * 1024:(grp + 1) * 1024]
                                s_ld.inc(sync.dma_start(out=q2s[j % 2][:], in_=src), 16)

                        @block.tensor
                        def _(t, grp=grp, half=half):
                            t.wait_ge(s_dve.sem, G_DVE)
                            if grp > 0:
                                t.wait_ge(s_dve.sem, dvew2)
                            for j in range(JT):
                                t.wait_ge(s_ld.sem, ldw2 + 16 * (j + 1))
                                inst = None
                                for dt_ in range(8):
                                    inst = t.matmul(pw2[dt_][:],
                                                    q2s[j % 2][:, dt_ * 128:(dt_ + 1) * 128],
                                                    gsb[:, j * 512:(j + 1) * 512],
                                                    start=(j == 0), stop=(j == JT - 1))
                                s_mm.inc(inst)
                        MM_W2 = s_mm.val

                        @block.vector
                        def _(v, grp=grp, half=half):
                            v.wait_ge(s_mm.sem, MM_W2)
                            for dt_ in range(8):
                                i_gl = grp * 8 + dt_
                                xsl = Xt(i_gl, 1 + half * 512, 1 + (half + 1) * 512)
                                s_dve.inc(v.scalar_tensor_tensor(xsl, pw2[dt_][:], dsc(4), xsl,
                                                                 op0=OP.mult, op1=OP.add))
                    W2MM["v"] = s_mm.val
                    W2DVE["v"] = s_dve.val
                    w2p.close()
                f4.close()
            if "C" in P4 and "D" not in P4:
                f4.close()

        if NPAD:
            eng = {"vector": block.vector, "tensor": block.tensor,
                   "scalar": block.scalar, "sync": block.sync}[NPAD_ENG]

            @eng
            def _(e):
                for _ in range(NPAD):
                    e.wait_ge(s_ld.sem, 0)

        if DBG:
            dmu = nc.dram_tensor("dmu", [128, TC + 1], F32, kind="ExternalOutput").ap()
            drstd = nc.dram_tensor("drstd", [128, TC + 1], F32, kind="ExternalOutput").ap()
            dcarr = nc.dram_tensor("dcarr", [128, 2 * KT], F32, kind="ExternalOutput").ap()
            dtots = nc.dram_tensor("dtots", [128, 2 * KT], F32, kind="ExternalOutput").ap()
            dbb = nc.dram_tensor("dbb", [128, KT * TC], BF16, kind="ExternalOutput").ap()
            dder = nc.dram_tensor("dder", [128, 3 * KT + 8], F32, kind="ExternalOutput").ap()
            dfct = nc.dram_tensor("dfct", [128, 8 * KT], F32, kind="ExternalOutput").ap()
            dcout = nc.dram_tensor("dcout", [NCORE, 2 * KT * 128], F32, kind="ExternalOutput").ap()

        @block.gpsimd
        def _(g):
            g.wait_ge(s_ev.sem, s_ev.val)
            g.wait_ge(s_dve.sem, s_dve.val)
            for i in range(KT):
                s_st.inc(g.dma_start(out=yout[i * 128:(i + 1) * 128, :], in_=Xt(i, 1, TC + 1)), 16)
            if DBG:
                s_st.inc(g.dma_start(out=dmu[:], in_=MU[:]), 16)
                s_st.inc(g.dma_start(out=drstd[:], in_=RSTD[:]), 16)
                s_st.inc(g.dma_start(out=dcarr[:], in_=CARR[:]), 16)
                s_st.inc(g.dma_start(out=dtots[:], in_=TOTS[:]), 16)
                s_st.inc(g.dma_start(out=dbb[:], in_=BB[:]), 16)
                s_st.inc(g.dma_start(out=dder[:], in_=DER[:]), 16)
                s_st.inc(g.dma_start(out=dfct[:], in_=FCT[:]), 16)
                s_st.inc(g.dma_start(out=dcout[:], in_=cout_tot[:]), 16)
            g.wait_ge(s_st.sem, s_st.val)

    return nc


def _tp(a):
    return np.ascontiguousarray(np.asarray(a).T)


def _wtile(w):
    w = np.asarray(w)
    K, M = w.shape
    nk, nj = K // 128, M // 128
    return np.ascontiguousarray(
        w.reshape(nk, 128, nj, 128).transpose(2, 1, 0, 3).reshape(nj * 128, nk * 128))


def _colpack(vec):
    return np.ascontiguousarray(np.asarray(vec).reshape(KT, 128).T)


def kernel(**inputs):
    x = np.asarray(inputs["L0_attn_x"])[0]
    stages = N_STAGES

    pc = np.zeros((128, NPAR * KT), np.float32)
    pars = {
        "tmk": np.asarray(inputs["L0_attn_tmk"]).reshape(D),
        "tmv": np.asarray(inputs["L0_attn_tmv"]).reshape(D),
        "tmr": np.asarray(inputs["L0_attn_tmr"]).reshape(D),
        "bk": np.asarray(inputs["L0_attn_bk"]),
        "bv": np.asarray(inputs["L0_attn_bv"]),
        "br": np.asarray(inputs["L0_attn_br"]),
        "bo": np.asarray(inputs["L0_attn_bo"]),
        "td": np.asarray(inputs["L0_attn_time_decay"]),
        "tf": np.asarray(inputs["L0_attn_time_first"]),
        "bo1": np.asarray(inputs["L1_attn_bo"]),
    }
    for n, v in pars.items():
        pc[:, PAR[n] * KT:(PAR[n] + 1) * KT] = _colpack(v.astype(np.float32))

    bq = np.ascontiguousarray(np.asarray(inputs["L1_attn_bqkv"]).astype(np.float32)
                              .reshape(48, 128).T)

    consts = np.zeros((128, NCONST), np.float32)
    consts[:, C_NEG1] = -1.0
    consts[:, C_INVD] = INV_D
    consts[:, C_EPS] = EPS
    consts[:, C_ATTSC] = ATT_SCALE
    consts[:, C_NSHIFT] = -ATT_SHIFT
    consts[:, C_INVCNT2] = 0.5 / (JH * D)
    consts[:, C_ONE] = 1.0

    ws = np.float32(np.asarray(inputs["L0_ffn_weight_scale"]))
    sin = np.float32(np.asarray(inputs["L0_ffn_input_scale"]).astype(np.float32))
    scl = np.tile(np.array([[ws, sin]], np.float32), (128, 1))

    revramp = np.tile((np.float32(TC - 1) - np.arange(TC, dtype=np.float32))[None, :], (128, 1))

    m = np.zeros((128, 4 * 512), BF)
    s_i = np.arange(128)[:, None]
    q_i = np.arange(512)[None, :]
    for r in range(4):
        m[:, r * 512:(r + 1) * 512] = (s_i <= q_i - 128 * r).astype(BF)

    ident = np.eye(128, dtype=BF)
    idnf = np.eye(128, dtype=np.float32)
    idn8 = np.eye(8, dtype=np.float32)
    onescol = np.ones((128, 1), np.float32)
    onesrow = np.ones((1, 128), np.float32)
    onesf = np.ones((128, 128), np.float32)

    wkT = _wtile(_tp(inputs["L0_attn_Wk"]))
    wvT = _wtile(_tp(inputs["L0_attn_Wv"]))
    wrT = _wtile(_tp(inputs["L0_attn_Wr"]))
    woT = _wtile(_tp(inputs["L0_attn_Wo"]))
    w8T = _wtile(np.ascontiguousarray(np.asarray(inputs["L0_ffn_weight"]).view(F8).T))
    wqkv = _wtile(np.asarray(inputs["L1_attn_Wqkv"]).astype(np.float32))
    wo1 = _wtile(np.asarray(inputs["L1_attn_Wo"]).astype(np.float32))
    w1 = np.asarray(inputs["L1_ffn_w1"])
    w3 = np.asarray(inputs["L1_ffn_w3"])
    w2 = np.asarray(inputs["L1_ffn_w2"])
    w1T = np.zeros((D, JP), np.float32); w1T[:, 0:JH] = w1.T
    w3T = np.zeros((D, JP), np.float32); w3T[:, 0:JH] = w3.T
    w1T = _wtile(w1T)
    w3T = _wtile(w3T)
    w2T = np.zeros((JP, D), np.float32); w2T[0:JH, :] = w2.T

    shared = dict(wkT=wkT, wvT=wvT, wrT=wrT, woT=woT, w8T=w8T, wqkv=wqkv, wo1=wo1,
                  w1T=w1T, w3T=w3T, w2T=w2T, pcols=pc, bqcols=bq, consts=consts,
                  scl=scl, revramp=revramp, masks=m, ident=ident, idn8=idn8, idnf=idnf, onesf=onesf,
                  onescol=onescol, onesrow=onesrow)

    in_maps = []
    for r in range(NCORE):
        xs = np.zeros((TC + 1, D), np.float32)
        if r > 0:
            xs[0] = x[r * TC - 1]
        xs[1:] = x[r * TC:(r + 1) * TC]
        e8 = np.zeros((128, 8), np.float32)
        for rp in range(8):
            e8[:, rp] = 1024.0 * (r - 1 - rp) if rp < r else 1e9
        in_maps.append(dict(shared, xT=_tp(xs), e1024=e8))

    nc = build_nc(stages)
    res = bass_utils.run_bass_kernel_spmd(nc, in_maps, core_ids=list(range(NCORE)))
    if DBG:
        import pickle
        with open("/tmp/dbg_results.pkl", "wb") as f:
            pickle.dump([{k: np.asarray(v) for k, v in r.items()} for r in res.results], f)
    out = np.concatenate([np.asarray(r["y"]).T for r in res.results], axis=0)
    return out.reshape(1, T, D).astype(np.float32)


# version 06 (final)
# speedup vs baseline: 1.4290x; 1.4290x over previous
import numpy as np
import ml_dtypes
from contextlib import ExitStack

import concourse.bass as bass
import concourse.mybir as mybir
from concourse import bass_utils

F32 = mybir.dt.float32
BF16 = mybir.dt.bfloat16
FP8 = mybir.dt.float8e4
AF = mybir.ActivationFunctionType
OP = mybir.AluOpType
AX = mybir.AxisListType

NCORE = 8
D = 2048
T = 8192
TC = T // NCORE
KT = D // 128
NH, HD, HPC = 16, 128, 2
JH = 5461
JP = 5504
JT = JP // 128
NQS = (JT + NCORE - 1) // NCORE
INV_D = 1.0 / D
EPS = 1e-5
ATT_SCALE = float(1.0 / np.sqrt(128.0))
ATT_SHIFT = 8.0

N_STAGES = 4
S4 = 4
S4PARTS = None
S4LN = 4
S4A = 4


def _s4parts():
    if S4PARTS is not None:
        return S4PARTS
    p = ""
    if S4 >= 1: p += "A"
    if S4 >= 2: p += "B"
    if S4 >= 3: p += "C"
    if S4 >= 3.5: p += "c"
    if S4 >= 4: p += "D"
    return p
NPAD = 0
QHOIST = "AB"
NPAD_ENG = "vector"
DBG = 0
F8MODE = 2

BF = ml_dtypes.bfloat16
F8 = ml_dtypes.float8_e4m3

C_NEG1, C_INVD, C_EPS, C_ATTSC, C_NSHIFT, C_INVCNT2, C_ONE = range(7)
NCONST = 7
PAR = {n: i for i, n in enumerate(
    ["tmk", "tmv", "tmr", "bk", "bv", "br", "bo", "td", "tf", "bo1"])}
NPAR = len(PAR)


class SC:
    def __init__(self, nc, es, name):
        self.sem = es.enter_context(nc.semaphore(name))
        self.val = 0

    def inc(self, inst, n=1):
        inst.then_inc(self.sem, n)
        self.val += n
        return inst


def build_nc(stages):
    nc = bass.Bass("TRN2", target_bir_lowering=False, debug=False, num_devices=NCORE,
                   detect_race_conditions=False)
    es = ExitStack()

    def din(name, shape, dt):
        return nc.dram_tensor(name, shape, dt, kind="ExternalInput").ap()

    def dint(name, shape, dt):
        return nc.dram_tensor(name, shape, dt, kind="Internal").ap()

    xT = din("xT", [D, TC + 1], F32)
    wkT = din("wkT", [D, D], F32)
    wvT = din("wvT", [D, D], F32)
    wrT = din("wrT", [D, D], F32)
    woT = din("woT", [D, D], F32)
    w8T = din("w8T", [D, D], FP8)
    wqkv = din("wqkv", [3 * D, D], F32)
    wo1 = din("wo1", [D, D], F32)
    wqmy = din("wqmy", [3 * NQS * 128, D], F32)
    pcolsI = din("pcols", [128, NPAR * KT], F32)
    bqcolsI = din("bqcols", [128, 48], F32)
    constsI = din("consts", [128, NCONST], F32)
    sclI = din("scl", [128, 2], F32)
    e1024I = din("e1024", [128, 8], F32)
    revrampI = din("revramp", [128, TC], F32)
    masksI = din("masks", [128, 4 * 512], BF16)
    identI = din("ident", [128, 128], BF16)
    idn8I = din("idn8", [8, 8], F32)
    idnfI = din("idnf", [128, 128], F32)
    onescolI = din("onescol", [128, 1], F32)
    onesrowI = din("onesrow", [1, 128], F32)
    onesfI = din("onesf", [128, 128], F32)

    yout = nc.dram_tensor("y", [D, TC], F32, kind="ExternalOutput").ap()

    dkind = "ExternalOutput" if DBG else "Internal"
    k_hbm = nc.dram_tensor("k_hbm", [D, TC], F32, kind=dkind).ap()
    v_hbm = nc.dram_tensor("v_hbm", [D, TC], BF16, kind=dkind).ap()
    sr_hbm = nc.dram_tensor("sr_hbm", [D, TC], BF16, kind=dkind).ap()
    cin_tot = dint("cin_tot", [2 * KT * 128], F32)
    cout_tot = dint("cout_tot", [NCORE, 2 * KT * 128], F32)
    cin_sc = dint("cin_sc", [128 * 512], F32)
    cout_sc = dint("cout_sc", [NCORE, 128 * 512], F32)
    cin_kqv = dint("cin_kqv", [3 * D, TC], BF16)
    cout_kqv = dint("cout_kqv", [NCORE, 3 * D, TC], BF16)
    cin_o = dint("cin_o", [HPC * HD, T], BF16)
    cout_o = dint("cout_o", [NCORE, HPC * HD, T], BF16)
    q_loc = dint("q_loc", [HPC * HD, T], BF16)
    o_loc = dint("o_loc", [D, TC], BF16)
    cin_q = dint("cin_q", [3 * NQS * 128, D], BF16)
    cout_q = dint("cout_q", [NCORE, 3 * NQS * 128, D], BF16)

    with es:
        X = es.enter_context(nc.sbuf_tensor("X", [128, KT * (TC + 1)], F32))
        BB = es.enter_context(nc.sbuf_tensor("BB", [128, KT * TC], BF16))
        PC = es.enter_context(nc.sbuf_tensor("PC", [128, NPAR * KT], F32))
        BQ = es.enter_context(nc.sbuf_tensor("BQ", [128, 48], F32))
        CST = es.enter_context(nc.sbuf_tensor("CST", [128, NCONST], F32))
        SCLT = es.enter_context(nc.sbuf_tensor("SCLT", [128, 2], F32))
        E8 = es.enter_context(nc.sbuf_tensor("E8", [128, 8], F32))
        RR = es.enter_context(nc.sbuf_tensor("RR", [128, TC], F32))
        MSK = es.enter_context(nc.sbuf_tensor("MSK", [128, 4 * 512], BF16))
        IDN = es.enter_context(nc.sbuf_tensor("IDN", [128, 128], BF16))
        ID8 = es.enter_context(nc.sbuf_tensor("ID8", [8, 8], F32))
        IDF = es.enter_context(nc.sbuf_tensor("IDF", [128, 128], F32))
        OCol = es.enter_context(nc.sbuf_tensor("OCol", [128, 1], F32))
        ORow = es.enter_context(nc.sbuf_tensor("ORow", [1, 128], F32))
        ONESF = es.enter_context(nc.sbuf_tensor("ONESF", [128, 128], F32))
        DER = es.enter_context(nc.sbuf_tensor("DER", [128, 3 * KT + 8], F32))
        FCT = es.enter_context(nc.sbuf_tensor("FCT", [128, 8 * KT], F32))
        MU = es.enter_context(nc.sbuf_tensor("MU", [128, TC + 1], F32))
        RSTD = es.enter_context(nc.sbuf_tensor("RSTD", [128, TC + 1], F32))
        ST1 = es.enter_context(nc.sbuf_tensor("ST1", [1, TC + 1], F32))
        ST2 = es.enter_context(nc.sbuf_tensor("ST2", [1, TC + 1], F32))
        SROW = es.enter_context(nc.sbuf_tensor("SROW", [1, TC + 1], F32))
        CARR = es.enter_context(nc.sbuf_tensor("CARR", [128, 2 * KT], F32))
        TOTS = es.enter_context(nc.sbuf_tensor("TOTS", [128, 2 * KT], F32))
        SC4 = es.enter_context(nc.sbuf_tensor("SC4", [8, 4], F32))
        SCD = es.enter_context(nc.sbuf_tensor("SCD", [1, 8], F32))
        TOTT = es.enter_context(nc.sbuf_tensor("TOTT", [32, 128], F32))

        def Xt(i, lo=0, hi=TC + 1):
            return X[:, i * (TC + 1) + lo: i * (TC + 1) + hi]

        def Bt(i, lo=0, hi=TC):
            return BB[:, i * TC + lo: i * TC + hi]

        def pcol(name, i):
            j = PAR[name] * KT + i
            return PC[:, j:j + 1]

        def dw(i):  return DER[:, i:i + 1]
        def dew(i): return DER[:, KT + i:KT + i + 1]
        def deb(i): return DER[:, 2 * KT + i:2 * KT + i + 1]
        DSB = 3 * KT
        def dsc(j): return DER[:, DSB + j:DSB + j + 1]
        def cc(j):  return CST[:, j:j + 1]
        def cc1(j): return CST[0:1, j:j + 1]

        s_ld = SC(nc, es, "s_ld")
        s_st = SC(nc, es, "s_st")
        s_gld = SC(nc, es, "s_gld")
        s_ast = SC(nc, es, "s_ast")
        s_act = SC(nc, es, "s_act")
        s_dve = SC(nc, es, "s_dve")
        s_wld = SC(nc, es, "s_wld")
        s_wc = SC(nc, es, "s_wc")
        s_mm = SC(nc, es, "s_mm")
        s_ev = SC(nc, es, "s_ev")
        s_cc = SC(nc, es, "s_cc")

        block = es.enter_context(nc.Block())

        @block.sync
        def _(sync):
            for i in range(KT):
                s_ld.inc(sync.dma_start(out=Xt(i), in_=xT[i * 128:(i + 1) * 128, :]), 16)
            for dst, src in [(PC, pcolsI), (BQ, bqcolsI), (CST, constsI), (SCLT, sclI),
                             (E8, e1024I), (RR, revrampI), (MSK, masksI), (IDN, identI),
                             (ID8, idn8I), (IDF, idnfI), (OCol, onescolI), (ORow, onesrowI),
                             (ONESF, onesfI)]:
                s_ld.inc(sync.dma_start(out=dst[:], in_=src[:]), 16)
        SETUP_LD = s_ld.val

        a0 = s_act.val

        @block.scalar
        def _(s):
            s.wait_ge(s_ld.sem, SETUP_LD)
            s_act.inc(s.activation(DER[:, 0:KT], PC[:, PAR["td"] * KT:(PAR["td"] + 1) * KT], AF.Exp))

        d0 = s_dve.val

        @block.vector
        def _(v):
            v.wait_ge(s_act.sem, a0 + 1)
            s_dve.inc(v.tensor_scalar(DER[:, 0:KT], DER[:, 0:KT], cc(C_NEG1), None, op0=OP.mult))
            v.wait_ge(s_dve.sem, d0 + 1)
            for i in range(KT - 1):
                v.tensor_scalar(FCT[:, 8 * i:8 * i + 8], E8[:], dw(i), None, op0=OP.mult)
            s_dve.inc(v.tensor_scalar(FCT[:, 8 * (KT - 1):8 * KT], E8[:], dw(KT - 1), None, op0=OP.mult))

        @block.scalar
        def _(s):
            s.wait_ge(s_dve.sem, d0 + 2)
            s.activation(DER[:, KT:2 * KT], DER[:, 0:KT], AF.Exp)
            s.activation(FCT[:], FCT[:], AF.Exp)
            s_act.inc(s.activation(DER[:, 2 * KT:3 * KT], PC[:, PAR["tf"] * KT:(PAR["tf"] + 1) * KT], AF.Exp))
        DER_READY = s_act.val

        dve_scb = s_dve.val

        @block.vector
        def _(v):
            v.wait_ge(s_ld.sem, SETUP_LD)
            v.reciprocal(DER[:, DSB:DSB + 1], SCLT[:, 1:2])
            s_dve.inc(v.tensor_tensor(DER[:, DSB + 1:DSB + 2], SCLT[:, 0:1], SCLT[:, 1:2],
                                      op=OP.mult))
        SCALES_DVE = s_dve.val

        def emit_ln(off, ncols, tag, pre_wait=None, upto=4):
            if pre_wait is not None and not isinstance(pre_wait, list):
                pre_wait = [pre_wait]
            ps = ExitStack()
            chunks = []
            c0 = 0
            while c0 < ncols:
                chunks.append((c0, min(512, ncols - c0)))
                c0 += 512
            nch = len(chunks)
            psx = [ps.enter_context(nc.psum_tensor(f"psx{tag}{c}", [1, 512], F32)) for c in range(nch)]
            psq = [ps.enter_context(nc.psum_tensor(f"psq{tag}{c}", [1, 512], F32)) for c in range(nch)]
            sqb = [ps.enter_context(nc.sbuf_tensor(f"sq{tag}{c}", [128, ncols], F32)) for c in range(2)]

            act_b = s_act.val
            mm_b = s_mm.val

            @block.scalar
            def _(s):
                s.wait_ge(s_ld.sem, SETUP_LD)
                if pre_wait is not None:
                    for pw_s, pw_v in pre_wait:
                        s.wait_ge(pw_s.sem, pw_v)
                for i in range(KT):
                    if i >= 2:
                        s.wait_ge(s_mm.sem, mm_b + (i - 1))
                    s_act.inc(s.activation(sqb[i % 2][:], Xt(i, off, off + ncols), AF.Square))

            @block.tensor
            def _(t):
                t.wait_ge(s_ld.sem, SETUP_LD)
                if pre_wait is not None:
                    for pw_s, pw_v in pre_wait:
                        t.wait_ge(pw_s.sem, pw_v)
                for i in range(KT):
                    for ci, (o, w) in enumerate(chunks):
                        t.matmul(psx[ci][:, 0:w], OCol[:], Xt(i, off + o, off + o + w),
                                 start=(i == 0), stop=(i == KT - 1))
                    t.wait_ge(s_act.sem, act_b + i + 1)
                    inst = None
                    for ci, (o, w) in enumerate(chunks):
                        inst = t.matmul(psq[ci][:, 0:w], OCol[:], sqb[i % 2][:, o:o + w],
                                        start=(i == 0), stop=(i == KT - 1))
                    s_mm.inc(inst)
            MM_STATS = s_mm.val

            dve_lnb = s_dve.val

            if upto < 2:
                ps.close()
                return s_ev.val

            @block.vector
            def _(v):
                v.wait_ge(s_mm.sem, MM_STATS)
                for ci, (o, w) in enumerate(chunks):
                    v.tensor_scalar(ST1[0:1, o:o + w], psx[ci][:, 0:w], cc1(C_INVD), None, op0=OP.mult)
                    v.tensor_scalar(ST2[0:1, o:o + w], psq[ci][:, 0:w], cc1(C_INVD), None, op0=OP.mult)
                s_dve.inc(v.engine_nop())
                v.wait_ge(s_dve.sem, dve_lnb + 1)
                s_dve.inc(v.tensor_tensor(SROW[0:1, 0:ncols], ST1[0:1, 0:ncols], ST1[0:1, 0:ncols],
                                          op=OP.mult))
                v.wait_ge(s_dve.sem, dve_lnb + 2)
                s_dve.inc(v.tensor_tensor(ST2[0:1, 0:ncols], ST2[0:1, 0:ncols], SROW[0:1, 0:ncols],
                                          op=OP.subtract))
            DVE_VAR = s_dve.val

            act_b2 = s_act.val

            if upto < 3:
                ps.close()
                return s_ev.val

            @block.scalar
            def _(s):
                s.wait_ge(s_dve.sem, DVE_VAR)
                s_act.inc(s.activation(SROW[0:1, 0:ncols], ST2[0:1, 0:ncols], AF.Sqrt, bias=cc1(C_EPS)))

            @block.vector
            def _(v):
                v.wait_ge(s_act.sem, act_b2 + 1)
                s_dve.inc(v.reciprocal(SROW[0:1, 0:ncols], SROW[0:1, 0:ncols]))
            DVE_RSTD = s_dve.val

            ps.close()
            if upto < 4:
                return s_ev.val
            pb = ExitStack()
            psbc = [pb.enter_context(nc.psum_tensor(f"psb{tag}{c}", [128, 512], F32))
                    for c in range(2 * nch)]
            mm_b2 = s_mm.val

            @block.tensor
            def _(t):
                t.wait_ge(s_dve.sem, DVE_RSTD)
                inst = None
                for ci, (o, w) in enumerate(chunks):
                    t.matmul(psbc[2 * ci][:, 0:w], ORow[:], ST1[0:1, o:o + w], start=True, stop=True)
                    inst = t.matmul(psbc[2 * ci + 1][:, 0:w], ORow[:], SROW[0:1, o:o + w],
                                    start=True, stop=True)
                s_mm.inc(inst)
            MM_B = s_mm.val

            @block.vector
            def _(v):
                v.wait_ge(s_mm.sem, MM_B)
                inst = None
                for ci, (o, w) in enumerate(chunks):
                    v.tensor_copy(MU[:, o:o + w], psbc[2 * ci][:, 0:w])
                    inst = v.tensor_copy(RSTD[:, o:o + w], psbc[2 * ci + 1][:, 0:w])
                s_ev.inc(inst)
            pb.close()
            return s_ev.val

        def emit_proj(tag, wdram, ncols, nk, rhs_fn, evac_fn, evac_eng, rhs_wait,
                      store_fn=None, wdt=F32, pre_waits=()):
            ps = ExitStack()
            nj = ncols // 128
            ntb = 2
            wf = ps.enter_context(nc.sbuf_tensor(f"wf{tag}", [128, nk * 128], wdt))
            wb = [ps.enter_context(nc.sbuf_tensor(f"wb{tag}{p}", [128, nk * 128], BF16))
                  for p in range(2)]
            NB = 4
            pmm = [ps.enter_context(nc.psum_tensor(f"pp{tag}{b}", [128, 512], F32)) for b in range(NB)]

            wld_b = s_wld.val
            wc_b = s_wc.val
            mm_b = s_mm.val
            ev_b = s_ev.val
            st_b = s_st.val

            @block.sync
            def _(sync):
                for sc_, v_ in pre_waits:
                    sync.wait_ge(sc_.sem, v_)
                for j in range(nj):
                    sync.wait_ge(s_wc.sem, wc_b + j)
                    s_wld.inc(sync.dma_start(out=wf[:], in_=wdram[j * 128:(j + 1) * 128, :]), 16)

            def _emit_evac_tile(e, j):
                for tb in range(ntb):
                    g = ntb * j + tb
                    e.wait_ge(s_mm.sem, mm_b + g + 1)
                    if store_fn is not None and tb == 0:
                        e.wait_ge(s_st.sem, st_b + 16 * (j - 1) if j >= 2 else st_b)
                    inst = evac_fn(e, pmm[g % NB][:], j, tb)
                    s_ev.inc(inst)

            if evac_eng == "scalar":
                @block.scalar
                def _(s):
                    for j in range(nj):
                        s.wait_ge(s_wld.sem, wld_b + 16 * (j + 1))
                        s.wait_ge(s_mm.sem, mm_b + ntb * (j - 1) if j >= 2 else mm_b)
                        s_wc.inc(s.activation(wb[j % 2][:], wf[:], AF.Identity))
                        if j >= 2:
                            _emit_evac_tile(s, j - 2)
                    for j in (nj - 2, nj - 1):
                        _emit_evac_tile(s, j)
            else:
                @block.scalar
                def _(s):
                    for j in range(nj):
                        s.wait_ge(s_wld.sem, wld_b + 16 * (j + 1))
                        s.wait_ge(s_mm.sem, mm_b + ntb * (j - 1) if j >= 2 else mm_b)
                        s_wc.inc(s.activation(wb[j % 2][:], wf[:], AF.Identity))

            @block.tensor
            def _(t):
                t.wait_ge(rhs_wait[0].sem, rhs_wait[1])
                for j in range(nj):
                    t.wait_ge(s_wc.sem, wc_b + j + 1)
                    for tb in range(ntb):
                        g = ntb * j + tb
                        t.wait_ge(s_ev.sem, ev_b + g - NB + 1 if g >= NB else ev_b)
                        inst = None
                        for kk in range(nk):
                            inst = t.matmul(pmm[g % NB][:],
                                            wb[j % 2][:, kk * 128:(kk + 1) * 128],
                                            rhs_fn(kk, tb),
                                            start=(kk == 0), stop=(kk == nk - 1))
                        s_mm.inc(inst)

            if evac_eng != "scalar":
                @block.vector
                def _(v):
                    for j in range(nj):
                        _emit_evac_tile(v, j)

            if store_fn is not None:
                @block.gpsimd
                def _(g):
                    for j in range(nj):
                        g.wait_ge(s_ev.sem, ev_b + ntb * (j + 1))
                        s_st.inc(store_fn(g, j), 16)
            ps.close()

        LN1_EV = emit_ln(0, TC + 1, "l1")

        l0 = ExitStack()
        xs1 = l0.enter_context(nc.sbuf_tensor("xs1", [128, TC + 1], F32))
        xs2 = l0.enter_context(nc.sbuf_tensor("xs2", [128, TC], F32))
        kvst = [l0.enter_context(nc.sbuf_tensor(f"kvst{p}", [128, TC], F32)) for p in range(2)]
        bfst = [l0.enter_context(nc.sbuf_tensor(f"bfst{p}", [128, TC], BF16)) for p in range(2)]

        def emit_lerp(parname, waitv):
            @block.vector
            def _(v):
                v.wait_ge(s_ev.sem, LN1_EV)
                if waitv is not None:
                    v.wait_ge(s_mm.sem, waitv)
                for i in range(KT):
                    v.tensor_tensor(xs1[:], Xt(i), MU[:, 0:TC + 1], op=OP.subtract)
                    v.tensor_tensor(xs1[:], xs1[:], RSTD[:, 0:TC + 1], op=OP.mult)
                    v.tensor_tensor(xs2[:], xs1[:, 1:TC + 1], xs1[:, 0:TC], op=OP.subtract)
                    s_dve.inc(v.scalar_tensor_tensor(Bt(i), xs2[:], pcol(parname, i),
                                                     xs1[:, 0:TC], op0=OP.mult, op1=OP.add))
            return s_dve.val

        LERP_K = emit_lerp("tmk", None)

        def evac_k(e, psum, j, tb):
            return e.activation(kvst[j % 2][:, tb * 512:(tb + 1) * 512], psum, AF.Identity,
                                bias=pcol("bk", j))

        emit_proj("k", wkT, D, KT, lambda kk, tb: Bt(kk, tb * 512, (tb + 1) * 512),
                  evac_k, "scalar", (s_dve, LERP_K),
                  store_fn=lambda g, j: g.dma_start(out=k_hbm[j * 128:(j + 1) * 128, :],
                                                    in_=kvst[j % 2][:]))

        LERP_V = emit_lerp("tmv", s_mm.val)

        def evac_v(e, psum, j, tb):
            return e.activation(bfst[j % 2][:, tb * 512:(tb + 1) * 512], psum, AF.Identity,
                                bias=pcol("bv", j))

        emit_proj("v", wvT, D, KT, lambda kk, tb: Bt(kk, tb * 512, (tb + 1) * 512),
                  evac_v, "scalar", (s_dve, LERP_V),
                  store_fn=lambda g, j: g.dma_start(out=v_hbm[j * 128:(j + 1) * 128, :],
                                                    in_=bfst[j % 2][:]))

        LERP_R = emit_lerp("tmr", s_mm.val)

        def evac_r(e, psum, j, tb):
            return e.activation(bfst[j % 2][:, tb * 512:(tb + 1) * 512], psum, AF.Sigmoid,
                                bias=pcol("br", j))

        emit_proj("r", wrT, D, KT, lambda kk, tb: Bt(kk, tb * 512, (tb + 1) * 512),
                  evac_r, "scalar", (s_dve, LERP_R),
                  store_fn=lambda g, j: g.dma_start(out=sr_hbm[j * 128:(j + 1) * 128, :],
                                                    in_=bfst[j % 2][:]))
        KVRS_ST = s_st.val

        l0.close()

        scn = ExitStack()
        kbuf = [scn.enter_context(nc.sbuf_tensor(f"kbf{p}", [128, TC], F32)) for p in range(2)]
        vbuf = [scn.enter_context(nc.sbuf_tensor(f"vbf{p}", [128, TC], BF16)) for p in range(2)]
        srbuf = [scn.enter_context(nc.sbuf_tensor(f"srbf{p}", [128, TC], BF16)) for p in range(2)]
        ekb = [scn.enter_context(nc.sbuf_tensor(f"ekbf{p}", [128, TC], F32)) for p in range(2)]
        argb = scn.enter_context(nc.sbuf_tensor("argb", [128, TC], F32))
        wrev = scn.enter_context(nc.sbuf_tensor("wrev", [128, TC], F32))
        ubuf = scn.enter_context(nc.sbuf_tensor("ubuf", [128, TC], F32))
        tbuf = scn.enter_context(nc.sbuf_tensor("tbuf", [128, TC], F32))
        arun = scn.enter_context(nc.sbuf_tensor("arun", [128, TC], F32))
        brun = scn.enter_context(nc.sbuf_tensor("brun", [128, TC], F32))
        aprev = scn.enter_context(nc.sbuf_tensor("aprev", [128, TC], F32))
        bprev = scn.enter_context(nc.sbuf_tensor("bprev", [128, TC], F32))
        ga8 = scn.enter_context(nc.sbuf_tensor("ga8", [8, 256], F32))
        gat = scn.enter_context(nc.sbuf_tensor("gat", [128, 16], F32))

        gldA = s_gld.val
        actA = s_act.val
        dveA = s_dve.val

        @block.gpsimd
        def _(g):
            g.wait_ge(s_st.sem, KVRS_ST)
            for i in range(KT):
                if i >= 2:
                    g.wait_ge(s_dve.sem, dveA + 2 * (i - 2) + 2)
                s_gld.inc(g.dma_start(out=kbuf[i % 2][:], in_=k_hbm[i * 128:(i + 1) * 128, :]), 16)
                s_gld.inc(g.dma_start(out=vbuf[i % 2][:], in_=v_hbm[i * 128:(i + 1) * 128, :]), 16)

        @block.scalar
        def _(s):
            s.wait_ge(s_act.sem, DER_READY)
            for i in range(KT):
                s.wait_ge(s_gld.sem, gldA + 32 * i + 16)
                if i >= 2:
                    s.wait_ge(s_dve.sem, dveA + 2 * (i - 2) + 2)
                s_act.inc(s.activation(ekb[i % 2][:], kbuf[i % 2][:], AF.Exp))
                s.wait_ge(s_dve.sem, dveA + 2 * i + 1)
                s_act.inc(s.activation(wrev[:], argb[:], AF.Exp))

        @block.vector
        def _(v):
            v.wait_ge(s_act.sem, DER_READY)
            v.wait_ge(s_st.sem, KVRS_ST)
            for i in range(KT):
                if i >= 1:
                    v.wait_ge(s_act.sem, actA + 2 * i)
                s_dve.inc(v.tensor_scalar(argb[:], RR[:], dw(i), None, op0=OP.mult))
                v.wait_ge(s_act.sem, actA + 2 * i + 2)
                v.wait_ge(s_gld.sem, gldA + 32 * (i + 1))
                v.tensor_tensor(ubuf[:], ekb[i % 2][:], vbuf[i % 2][:], op=OP.mult)
                v.tensor_tensor(tbuf[:], ubuf[:], wrev[:], op=OP.mult)
                v.tensor_reduce(TOTS[:, i:i + 1], tbuf[:], axis=AX.X, op=OP.add)
                v.tensor_tensor(tbuf[:], ekb[i % 2][:], wrev[:], op=OP.mult)
                s_dve.inc(v.tensor_reduce(TOTS[:, KT + i:KT + i + 1], tbuf[:], axis=AX.X, op=OP.add))
        PASSA_DVE = s_dve.val

        ccA = s_cc.val
        ptot_es = ExitStack()
        ptot = ptot_es.enter_context(nc.psum_tensor("ptot", [32, 128], F32))
        mmT_b = s_mm.val

        @block.tensor
        def _(t):
            t.wait_ge(s_dve.sem, PASSA_DVE)
            s_mm.inc(t.transpose(ptot[:], TOTS[:], IDF[:]))

        dveT_b = s_dve.val

        @block.vector
        def _(v):
            v.wait_ge(s_mm.sem, mmT_b + 1)
            s_dve.inc(v.tensor_copy(TOTT[:], ptot[:]))
        ptot_es.close()

        @block.gpsimd
        def _(g):
            g.wait_ge(s_dve.sem, dveT_b + 1)
            s_st.inc(g.dma_start(out=cin_tot.rearrange("(r p) -> r p", r=32), in_=TOTT[:]), 16)
            g.wait_ge(s_st.sem, s_st.val)
            s_cc.inc(g.collective_compute(
                "AllGather", OP.bypass, replica_groups=[list(range(NCORE))],
                ins=[cin_tot[:]], outs=[cout_tot[:]]))

        ldC = s_ld.val
        mmC = s_mm.val
        dveC = s_dve.val
        pcar = ExitStack()
        pga = pcar.enter_context(nc.psum_tensor("pga", [128, 8], F32))
        pgb = pcar.enter_context(nc.psum_tensor("pgb", [128, 8], F32))

        @block.sync
        def _(sync):
            sync.wait_ge(s_cc.sem, ccA + 1)
            for i in range(KT):
                if i >= 1:
                    sync.wait_ge(s_mm.sem, mmC + 2 * i)
                cg = cout_tot
                s_ld.inc(sync.dma_start(out=ga8[:, 0:128], in_=cg[:, i * 128:(i + 1) * 128]), 16)
                s_ld.inc(sync.dma_start(out=ga8[:, 128:256],
                                        in_=cg[:, 2048 + i * 128:2048 + (i + 1) * 128]), 16)

        @block.tensor
        def _(t):
            for i in range(KT):
                t.wait_ge(s_ld.sem, ldC + 32 * (i + 1))
                if i >= 1:
                    t.wait_ge(s_dve.sem, dveC + 3 * i)
                t.transpose(pga[:], ga8[:, 0:128], ID8[:])
                s_mm.inc(t.transpose(pgb[:], ga8[:, 128:256], ID8[:]), 2)

        @block.vector
        def _(v):
            for i in range(KT):
                v.wait_ge(s_mm.sem, mmC + 2 * (i + 1))
                v.tensor_tensor(gat[:, 0:8], pga[:], FCT[:, 8 * i:8 * i + 8], op=OP.mult)
                s_dve.inc(v.tensor_tensor(gat[:, 8:16], pgb[:], FCT[:, 8 * i:8 * i + 8], op=OP.mult))
                v.wait_ge(s_dve.sem, dveC + 3 * i + 1)
                v.tensor_reduce(CARR[:, i:i + 1], gat[:, 0:8], axis=AX.X, op=OP.add)
                s_dve.inc(v.tensor_reduce(CARR[:, KT + i:KT + i + 1], gat[:, 8:16],
                                          axis=AX.X, op=OP.add), 2)
        CARR_DVE = s_dve.val
        pcar.close()

        gldB = s_gld.val
        actB = s_act.val
        dveB = s_dve.val

        @block.gpsimd
        def _(g):
            for i in range(KT):
                if i >= 2:
                    g.wait_ge(s_dve.sem, dveB + (i - 1))
                s_gld.inc(g.dma_start(out=kbuf[i % 2][:], in_=k_hbm[i * 128:(i + 1) * 128, :]), 16)
                s_gld.inc(g.dma_start(out=vbuf[i % 2][:], in_=v_hbm[i * 128:(i + 1) * 128, :]), 16)
                s_gld.inc(g.dma_start(out=srbuf[i % 2][:], in_=sr_hbm[i * 128:(i + 1) * 128, :]), 16)

        @block.scalar
        def _(s):
            for i in range(KT):
                s.wait_ge(s_gld.sem, gldB + 48 * i + 16)
                if i >= 2:
                    s.wait_ge(s_dve.sem, dveB + (i - 1))
                s_act.inc(s.activation(ekb[i % 2][:], kbuf[i % 2][:], AF.Exp))

        @block.vector
        def _(v):
            v.wait_ge(s_dve.sem, CARR_DVE)
            for i in range(KT):
                v.wait_ge(s_act.sem, actB + i + 1)
                v.wait_ge(s_gld.sem, gldB + 48 * (i + 1))
                ek = ekb[i % 2]
                v.tensor_tensor(ubuf[:], ek[:], vbuf[i % 2][:], op=OP.mult)
                v.tensor_tensor_scan(arun[:], dew(i).broadcast_to([128, TC]), ubuf[:],
                                     CARR[:, i:i + 1], op0=OP.mult, op1=OP.add)
                v.tensor_tensor_scan(brun[:], dew(i).broadcast_to([128, TC]), ek[:],
                                     CARR[:, KT + i:KT + i + 1], op0=OP.mult, op1=OP.add)
                v.tensor_copy(aprev[:, 1:TC], arun[:, 0:TC - 1])
                v.tensor_copy(aprev[:, 0:1], CARR[:, i:i + 1])
                v.tensor_copy(bprev[:, 1:TC], brun[:, 0:TC - 1])
                v.tensor_copy(bprev[:, 0:1], CARR[:, KT + i:KT + i + 1])
                v.scalar_tensor_tensor(ubuf[:], ubuf[:], deb(i), aprev[:], op0=OP.mult, op1=OP.add)
                v.scalar_tensor_tensor(tbuf[:], ek[:], deb(i), bprev[:], op0=OP.mult, op1=OP.add)
                v.reciprocal(tbuf[:], tbuf[:])
                v.tensor_tensor(ubuf[:], ubuf[:], tbuf[:], op=OP.mult)
                s_dve.inc(v.tensor_tensor(Bt(i), ubuf[:], srbuf[i % 2][:], op=OP.mult))
        SCAN_DVE = s_dve.val
        scn.close()

        wo_ps = ExitStack()

        def evac_wo(e, psum, j, tb):
            xsl = Xt(j, 1 + tb * 512, 1 + (tb + 1) * 512)
            return e.scalar_tensor_tensor(xsl, psum, pcol("bo", j), xsl, op0=OP.add, op1=OP.add)

        emit_proj("wo", woT, D, KT, lambda kk, tb: Bt(kk, tb * 512, (tb + 1) * 512),
                  evac_wo, "vector", (s_dve, SCAN_DVE), pre_waits=((s_dve, SCAN_DVE),))
        wo_ps.close()
        STAGE1_EV = s_ev.val

        if stages >= 2:
            LN2_MM0 = s_mm.val
            LN2_EV = emit_ln(1, TC, "l2", pre_wait=(s_ev, s_ev.val))
            LN2_MM = s_mm.val

            f0 = ExitStack()
            xq1 = f0.enter_context(nc.sbuf_tensor("xq1", [128, TC], F32))
            fq8 = f0.enter_context(nc.sbuf_tensor("fq8", [128, TC], FP8))
            dve_q = s_dve.val

            if F8MODE >= 1:
                @block.vector
                def _(v):
                    v.wait_ge(s_ev.sem, LN2_EV)
                    v.wait_ge(s_dve.sem, SCALES_DVE)
                    for i in range(KT):
                        v.tensor_tensor(xq1[:], Xt(i, 1, TC + 1), MU[:, 0:TC], op=OP.subtract)
                        v.tensor_tensor(xq1[:], xq1[:], RSTD[:, 0:TC], op=OP.mult)
                        v.tensor_scalar(xq1[:], xq1[:], dsc(0), None, op0=OP.mult)
                        v.tensor_copy(fq8[:], xq1[:])
                        s_dve.inc(v.tensor_copy(Bt(i), fq8[:]))
            QDQ_DVE = s_dve.val

            def evac_f8(e, psum, j, tb):
                xsl = Xt(j, 1 + tb * 512, 1 + (tb + 1) * 512)
                return e.scalar_tensor_tensor(xsl, psum, dsc(1), xsl, op0=OP.mult, op1=OP.add)

            if F8MODE >= 2:
                emit_proj("f8", w8T, D, KT, lambda kk, tb: Bt(kk, tb * 512, (tb + 1) * 512),
                          evac_f8, "vector", (s_dve, QDQ_DVE), wdt=FP8,
                          pre_waits=((s_mm, LN2_MM), (s_dve, QDQ_DVE)))
            f0.close()


        Q4 = {}

        def emit_quant_A():
            t4 = ExitStack()
            wsc = [t4.enter_context(nc.sbuf_tensor(f"wsc{p}", [128, 2048], F32)) for p in range(4)]
            pacc = t4.enter_context(nc.sbuf_tensor("pacc", [128, 512], F32))
            prsc = t4.enter_context(nc.sbuf_tensor("prsc", [128, 1], F32))
            pbk4 = t4.enter_context(nc.psum_tensor("pbk4", [128, 512], F32))
            NQC = 3 * NQS

            astA = s_ast.val
            dveA = s_dve.val
            st_gate = s_st.val

            @block.vector
            def _(v):
                v.wait_ge(s_ld.sem, SETUP_LD)
                v.wait_ge(s_st.sem, st_gate)
                s_dve.inc(v.memset(pacc[:], 0.0))

            @block.scalar
            def _(s):
                s.wait_ge(s_st.sem, st_gate)
                for c in range(NQC):
                    if c >= 4:
                        s.wait_ge(s_dve.sem, dveA + 1 + 2 * (c - 3))
                    s_ast.inc(s.dma_start(out=wsc[c % 4][:],
                                          in_=wqmy[c * 128:(c + 1) * 128, :]), 16)

            @block.vector
            def _(v):
                for c in range(NQC):
                    cw = c // NQS
                    v.wait_ge(s_ast.sem, astA + 16 * (c + 1))
                    s_dve.inc(v.tensor_reduce(prsc[:], wsc[c % 4][:], axis=AX.X,
                                              op=OP.add, apply_absolute_value=True))
                    s_dve.inc(v.tensor_tensor(pacc[:, cw:cw + 1], pacc[:, cw:cw + 1],
                                              prsc[:], op=OP.add))
            PACC_DVE = s_dve.val

            st4 = s_st.val

            @block.gpsimd
            def _(g):
                g.wait_ge(s_dve.sem, PACC_DVE)
                s_st.inc(g.dma_start(out=cin_sc.rearrange("(p c) -> p c", p=128), in_=pacc[:]), 16)
                g.wait_ge(s_st.sem, st4 + 16)
                s_cc.inc(g.collective_compute(
                    "AllGather", OP.bypass, replica_groups=[list(range(NCORE))],
                    ins=[cin_sc[:]], outs=[cout_sc[:]]))
            CC_SC = s_cc.val

            ld4b = s_ld.val
            dve4b = s_dve.val

            @block.sync
            def _(sync):
                sync.wait_ge(s_cc.sem, CC_SC)
                for r_ in range(NCORE):
                    if r_ >= 2:
                        sync.wait_ge(s_dve.sem, dve4b + (r_ - 1))
                    s_ld.inc(sync.dma_start(out=wsc[r_ % 2][:, 0:512],
                                            in_=cout_sc[r_, :].rearrange("(p c) -> p c", p=128)), 16)

            @block.vector
            def _(v):
                for r_ in range(NCORE):
                    v.wait_ge(s_ld.sem, ld4b + 16 * (r_ + 1))
                    if r_ == 0:
                        s_dve.inc(v.tensor_copy(pacc[:], wsc[0][:, 0:512]))
                    else:
                        s_dve.inc(v.tensor_tensor(pacc[:], pacc[:], wsc[r_ % 2][:, 0:512],
                                                  op=OP.add))
            SUM8_DVE = s_dve.val

            mm4 = s_mm.val
            act_gate = s_act.val

            @block.tensor
            def _(t):
                t.wait_ge(s_dve.sem, SUM8_DVE)
                t.wait_ge(s_act.sem, act_gate)
                s_mm.inc(t.matmul(pbk4[:], ONESF[:], pacc[:], start=True, stop=True))

            dve4c = s_dve.val

            @block.vector
            def _(v):
                v.wait_ge(s_mm.sem, mm4 + 1)
                s_dve.inc(v.tensor_scalar(DER[:, DSB + 2:DSB + 5], pbk4[:, 0:3], cc(C_INVCNT2),
                                          None, op0=OP.mult))
                v.wait_ge(s_dve.sem, dve4c + 1)
                s_dve.inc(v.tensor_scalar(DER[:, DSB + 5:DSB + 8], DER[:, DSB + 2:DSB + 5],
                                          cc(C_NEG1), None, op0=OP.mult))
            Q4["SCALE4_DVE"] = s_dve.val
            t4.close()

        def emit_quant_B():
            tq = ExitStack()
            wq = [tq.enter_context(nc.sbuf_tensor(f"wq{p}", [128, 2048], F32)) for p in range(2)]
            sgA = tq.enter_context(nc.sbuf_tensor("sgA", [128, 2048], F32))
            qbb = [tq.enter_context(nc.sbuf_tensor(f"qbb{p}", [128, 2048], BF16)) for p in range(2)]
            NQC = 3 * NQS
            astq = s_ast.val
            actq = s_act.val
            dveq = s_dve.val
            SC4D = Q4["SCALE4_DVE"]
            st_gate = s_st.val

            @block.scalar
            def _(s):
                s.wait_ge(s_dve.sem, SC4D)
                s.wait_ge(s_st.sem, st_gate)
                for c in range(NQC):
                    cw = c // NQS
                    s_ast.inc(s.dma_start(out=wq[c % 2][:],
                                          in_=wqmy[c * 128:(c + 1) * 128, :]), 16)
                    if c >= 1:
                        s.wait_ge(s_dve.sem, dveq + c)
                        s_ast.inc(s.dma_start(out=cin_q[(c - 1) * 128:c * 128, :],
                                              in_=qbb[(c - 1) % 2][:]), 16)
                    s.wait_ge(s_ast.sem, astq + 16 * (2 * c + 1))
                    s.activation(sgA[:], wq[c % 2][:], AF.Sign, bias=dsc(5 + cw))
                    s_act.inc(s.activation(qbb[c % 2][:], wq[c % 2][:], AF.Sign,
                                           bias=dsc(2 + cw)))
                s.wait_ge(s_dve.sem, dveq + NQC)
                s_ast.inc(s.dma_start(out=cin_q[(NQC - 1) * 128:NQC * 128, :],
                                      in_=qbb[(NQC - 1) % 2][:]), 16)

            @block.vector
            def _(v):
                for c in range(NQC):
                    v.wait_ge(s_act.sem, actq + c + 1)
                    s_dve.inc(v.tensor_tensor(qbb[c % 2][:], sgA[:], qbb[c % 2][:], op=OP.add))
            Q4["QNT_AST"] = s_ast.val
            tq.close()

        def emit_quant_B_ag():
            @block.gpsimd
            def _(g):
                g.wait_ge(s_ast.sem, Q4["QNT_AST"])
                s_cc.inc(g.collective_compute(
                    "AllGather", OP.bypass, replica_groups=[list(range(NCORE))],
                    ins=[cin_q[:]], outs=[cout_q[:]]))
            Q4["CC_Q"] = s_cc.val

        if stages >= 3:
            LN3_EV = emit_ln(1, TC, "l3", pre_wait=(s_ev, s_ev.val))
            LN3_MM = s_mm.val

            a3 = ExitStack()
            xs3 = a3.enter_context(nc.sbuf_tensor("xs3", [128, TC], F32))
            qst = [a3.enter_context(nc.sbuf_tensor(f"qst{p}", [128, TC], BF16)) for p in range(2)]
            dve_x3 = s_dve.val

            @block.vector
            def _(v):
                v.wait_ge(s_ev.sem, LN3_EV)
                for i in range(KT):
                    v.tensor_tensor(xs3[:], Xt(i, 1, TC + 1), MU[:, 0:TC], op=OP.subtract)
                    s_dve.inc(v.tensor_tensor(Bt(i), xs3[:], RSTD[:, 0:TC], op=OP.mult))
            XLN3_DVE = s_dve.val

            def evac_qkv(e, psum, j, tb):
                return e.activation(qst[j % 2][:, tb * 512:(tb + 1) * 512], psum, AF.Identity,
                                    bias=BQ[:, j:j + 1])

            emit_proj("qkv", wqkv, 3 * D, KT, lambda kk, tb: Bt(kk, tb * 512, (tb + 1) * 512),
                      evac_qkv, "scalar", (s_dve, XLN3_DVE),
                      store_fn=lambda g, j: g.dma_start(out=cin_kqv[j * 128:(j + 1) * 128, :],
                                                        in_=qst[j % 2][:]),
                      pre_waits=((s_mm, LN3_MM),))
            QKV_ST = s_st.val
            a3.close()

            if stages >= 4:
                if "A" in QHOIST:
                    emit_quant_A()
                if "B" in QHOIST:
                    emit_quant_B()

            cc_kb = s_cc.val

            @block.gpsimd
            def _(g):
                g.wait_ge(s_st.sem, QKV_ST)
                s_cc.inc(g.collective_compute(
                    "AllGather", OP.bypass, replica_groups=[list(range(NCORE))],
                    ins=[cin_kqv[:]], outs=[cout_kqv[:]]))
            CC_KQV = s_cc.val

            att = ExitStack()
            Ksb = att.enter_context(nc.sbuf_tensor("Ksb", [128, T], BF16))
            Vsb = att.enter_context(nc.sbuf_tensor("Vsb", [128, T], BF16))
            Qsb = att.enter_context(nc.sbuf_tensor("Qsb", [128, T], BF16))
            Psb = [att.enter_context(nc.sbuf_tensor(f"Psb{p}", [128, 512], BF16)) for p in range(4)]
            lacc = att.enter_context(nc.sbuf_tensor("lacc", [128, 512], F32))
            lrow = att.enter_context(nc.sbuf_tensor("lrow", [1, 512], F32))
            rbsb = att.enter_context(nc.sbuf_tensor("rbsb", [128, 512], F32))
            Ostg = [att.enter_context(nc.sbuf_tensor(f"Ostg{p}", [128, 512], BF16)) for p in range(2)]
            pS = [att.enter_context(nc.psum_tensor(f"pS{p}", [128, 512], F32)) for p in range(3)]
            pO = [att.enter_context(nc.psum_tensor(f"pO{p}", [128, 512], F32)) for p in range(2)]
            plr = att.enter_context(nc.psum_tensor("plr", [1, 512], F32))
            pbc = att.enter_context(nc.psum_tensor("pbc", [128, 512], F32))

            NQB = T // 512
            NST = T // 128
            DEPTH = 3

            for hh in range(HPC):
                ld_h = s_ld.val
                mm_h = s_mm.val
                act_h = s_act.val
                dve_h = s_dve.val
                st_h = s_st.val
                prev_mm = mm_h
                prev_dve = dve_h

                @block.sync
                def _(sync, hh=hh, prev_mm=prev_mm, prev_dve=prev_dve):
                    sync.wait_ge(s_cc.sem, CC_KQV)
                    if hh == 0 and stages >= 4 and "B" in QHOIST:
                        sync.wait_ge(s_ast.sem, Q4["QNT_AST"])
                    if hh > 0:
                        sync.wait_ge(s_mm.sem, prev_mm)
                        sync.wait_ge(s_dve.sem, prev_dve)
                    pid = sync.partition_id()
                    preg = sync.to_reg(pid)
                    for r in range(NCORE):
                        hg = HPC * r + hh
                        with sync.If_eq(preg, r):
                            for dst, base in ((Qsb, 0), (Ksb, D), (Vsb, 2 * D)):
                                srcv = cout_kqv[:, base + 128 * hg: base + 128 * (hg + 1), :]
                                srcv = srcv.rearrange("r d t -> d r t")
                                dstv = dst.ap().rearrange("p (r t) -> p r t", t=TC)
                                s_ld.inc(sync.dma_start(out=dstv, in_=srcv), 16)
                        if r != NCORE - 1:
                            s_ld.val -= 48
                LD_H = s_ld.val

                @block.tensor
                def _(t):
                    t.wait_ge(s_ld.sem, LD_H)
                    for st in range(NST):
                        if st >= 3:
                            t.wait_ge(s_dve.sem, dve_h + st - 2)
                        s_mm.inc(t.matmul(pS[st % 3][:, 0:128],
                                          Vsb[:, st * 128:(st + 1) * 128], IDN[:],
                                          start=True, stop=True))

                @block.vector
                def _(v):
                    for st in range(NST):
                        v.wait_ge(s_mm.sem, mm_h + st + 1)
                        s_dve.inc(v.tensor_copy(Vsb[:, st * 128:(st + 1) * 128],
                                                pS[st % 3][:, 0:128]))

                mmA = s_mm.val
                actA2 = s_act.val
                dveA2 = s_dve.val
                stA = s_st.val

                flatS = []
                for qb in range(NQB):
                    for st in range(4 * (qb + 1)):
                        flatS.append((qb, st))
                NG = len(flatS)

                ops = []
                tails = []
                for gi in range(NG + DEPTH):
                    if gi < NG:
                        ops.append(("S",) + flatS[gi])
                    j = gi - DEPTH
                    if j >= 0:
                        qb, st = flatS[j]
                        ops.append(("PV", qb, st))
                        if st == 4 * (qb + 1) - 1:
                            tails.append([gi + 1, ("LR", qb)])
                            tails.append([gi + 2, ("BC", qb)])
                    while tails and tails[0][0] <= gi:
                        ops.append(tails.pop(0)[1])
                for _, op in tails:
                    ops.append(op)

                mm_S = {}; mm_PV = {}; mm_lr = {}; mm_bc = {}
                act_exp = {}; act_rb = {}
                dve_msk = {}; dve_l = {}; dve_rc = {}; dve_om = {}; st_o = {}
                cmm = cact = cdve = cst = 0
                for op in ops:
                    if op[0] == "S":
                        cmm += 1; mm_S[op[1:]] = cmm
                        cact += 1; act_exp[op[1:]] = cact
                        qb, st = op[1:]
                        if st >= 4 * qb:
                            cdve += 1; dve_msk[op[1:]] = cdve
                        cdve += 1; dve_l[op[1:]] = cdve
                    elif op[0] == "PV":
                        cmm += 1; mm_PV[op[1:]] = cmm
                    elif op[0] == "LR":
                        cmm += 1; mm_lr[op[1]] = cmm
                        cdve += 1; dve_rc[op[1]] = cdve
                    else:
                        cmm += 1; mm_bc[op[1]] = cmm
                        cact += 1; act_rb[op[1]] = cact
                        cdve += 1; dve_om[op[1]] = cdve
                        cst += 16; st_o[op[1]] = cst
                gidx = {k: i for i, k in enumerate(flatS)}

                @block.tensor
                def _(t):
                    t.wait_ge(s_dve.sem, dveA2)
                    for op in ops:
                        if op[0] == "S":
                            qb, st = op[1:]
                            gi = gidx[(qb, st)]
                            if gi >= 3:
                                t.wait_ge(s_act.sem, actA2 + act_exp[flatS[gi - 3]])
                            s_mm.inc(t.matmul(pS[gi % 3][:], Ksb[:, st * 128:(st + 1) * 128],
                                              Qsb[:, qb * 512:(qb + 1) * 512],
                                              start=True, stop=True))
                        elif op[0] == "PV":
                            qb, st = op[1:]
                            gi = gidx[(qb, st)]
                            ns = 4 * (qb + 1)
                            if st == 0 and qb >= 2:
                                t.wait_ge(s_dve.sem, dveA2 + dve_om[qb - 2])
                            if st >= 4 * qb:
                                t.wait_ge(s_dve.sem, dveA2 + dve_msk[(qb, st)])
                            else:
                                t.wait_ge(s_act.sem, actA2 + act_exp[(qb, st)])
                            s_mm.inc(t.matmul(pO[qb % 2][:], Vsb[:, st * 128:(st + 1) * 128],
                                              Psb[gi % 4][:],
                                              start=(st == 0), stop=(st == ns - 1)))
                        elif op[0] == "LR":
                            qb = op[1]
                            t.wait_ge(s_dve.sem, dveA2 + dve_l[(qb, 4 * (qb + 1) - 1)])
                            if qb >= 1:
                                t.wait_ge(s_dve.sem, dveA2 + dve_rc[qb - 1])
                            s_mm.inc(t.matmul(plr[:], OCol[:], lacc[:], start=True, stop=True))
                        else:
                            qb = op[1]
                            t.wait_ge(s_dve.sem, dveA2 + dve_rc[qb])
                            if qb >= 1:
                                t.wait_ge(s_act.sem, actA2 + act_rb[qb - 1])
                            s_mm.inc(t.matmul(pbc[:], ORow[:], lrow[0:1, :], start=True, stop=True))

                @block.scalar
                def _(s):
                    for op in ops:
                        if op[0] == "S":
                            qb, st = op[1:]
                            gi = gidx[(qb, st)]
                            s.wait_ge(s_mm.sem, mmA + mm_S[(qb, st)])
                            if gi >= 4:
                                s.wait_ge(s_mm.sem, mmA + mm_PV[flatS[gi - 4]])
                                s.wait_ge(s_dve.sem, dveA2 + dve_l[flatS[gi - 4]])
                            s_act.inc(s.activation(Psb[gi % 4][:], pS[gi % 3][:], AF.Exp,
                                                   scale=cc(C_ATTSC), bias=cc(C_NSHIFT)))
                        elif op[0] == "BC":
                            qb = op[1]
                            s.wait_ge(s_mm.sem, mmA + mm_bc[qb])
                            if qb >= 1:
                                s.wait_ge(s_dve.sem, dveA2 + dve_om[qb - 1])
                            s_act.inc(s.activation(rbsb[:], pbc[:], AF.Identity))

                @block.vector
                def _(v):
                    for op in ops:
                        if op[0] == "S":
                            qb, st = op[1:]
                            gi = gidx[(qb, st)]
                            v.wait_ge(s_act.sem, actA2 + act_exp[(qb, st)])
                            rel = st - 4 * qb
                            if rel >= 0:
                                s_dve.inc(v.tensor_tensor(Psb[gi % 4][:], Psb[gi % 4][:],
                                                          MSK[:, rel * 512:(rel + 1) * 512],
                                                          op=OP.mult))
                            if st == 0:
                                if qb >= 1:
                                    v.wait_ge(s_mm.sem, mmA + mm_lr[qb - 1])
                                s_dve.inc(v.tensor_copy(lacc[:], Psb[gi % 4][:]))
                            else:
                                s_dve.inc(v.tensor_tensor(lacc[:], lacc[:], Psb[gi % 4][:],
                                                          op=OP.add))
                        elif op[0] == "LR":
                            qb = op[1]
                            v.wait_ge(s_mm.sem, mmA + mm_lr[qb])
                            s_dve.inc(v.reciprocal(lrow[0:1, :], plr[:]))
                        elif op[0] == "BC":
                            qb = op[1]
                            v.wait_ge(s_act.sem, actA2 + act_rb[qb])
                            if qb >= 2:
                                v.wait_ge(s_st.sem, stA + st_o[qb - 2])
                            s_dve.inc(v.tensor_tensor(Ostg[qb % 2][:], pO[qb % 2][:], rbsb[:],
                                                      op=OP.mult))

                @block.gpsimd
                def _(g, hh=hh):
                    for qb in range(NQB):
                        g.wait_ge(s_dve.sem, dveA2 + dve_om[qb])
                        s_st.inc(g.dma_start(
                            out=cin_o[hh * 128:(hh + 1) * 128, qb * 512:(qb + 1) * 512],
                            in_=Ostg[qb % 2][:]), 16)

            ATT_ST = s_st.val
            att.close()

            @block.gpsimd
            def _(g):
                g.wait_ge(s_st.sem, ATT_ST)
                s_cc.inc(g.collective_compute(
                    "AllGather", OP.bypass, replica_groups=[list(range(NCORE))],
                    ins=[cin_o[:]], outs=[cout_o[:]]))
            CC_O = s_cc.val

            if stages >= 4 and "B" in QHOIST:
                emit_quant_B_ag()

            @block.sync
            def _(sync):
                sync.wait_ge(s_cc.sem, CC_O)
                pid = sync.partition_id()
                preg = sync.to_reg(pid)
                for r in range(NCORE):
                    with sync.If_eq(preg, r):
                        srcv = cout_o[:, :, r * TC:(r + 1) * TC]
                        s_ld.inc(sync.dma_start(
                            out=o_loc.rearrange("(r p) t -> r p t", p=HPC * HD), in_=srcv), 16)
                    if r != NCORE - 1:
                        s_ld.val -= 16
                sync.wait_ge(s_ld.sem, s_ld.val)
                for i in range(KT):
                    s_ld.inc(sync.dma_start(out=Bt(i), in_=o_loc[i * 128:(i + 1) * 128, :]), 16)
            OB_LD = s_ld.val

            def evac_wo1(e, psum, j, tb):
                xsl = Xt(j, 1 + tb * 512, 1 + (tb + 1) * 512)
                return e.scalar_tensor_tensor(xsl, psum, pcol("bo1", j), xsl,
                                              op0=OP.add, op1=OP.add)

            emit_proj("wo1", wo1, D, KT, lambda kk, tb: Bt(kk, tb * 512, (tb + 1) * 512),
                      evac_wo1, "vector", (s_ld, OB_LD), pre_waits=((s_ld, OB_LD),))


        if stages >= 4:
            P4 = _s4parts()
            if "C" in P4:
                LN4_EV = emit_ln(1, TC, "l4", upto=S4LN,
                                 pre_wait=[(s_ev, s_ev.val), (s_dve, Q4["SCALE4_DVE"])])

                f4 = ExitStack()
                xs4 = f4.enter_context(nc.sbuf_tensor("xs4", [128, TC], F32))
                dve_x4 = s_dve.val

                if "c" in P4:
                    @block.vector
                    def _(v):
                        v.wait_ge(s_ev.sem, LN4_EV)
                        v.wait_ge(s_mm.sem, s_mm.val)
                        for i in range(KT):
                            v.tensor_tensor(xs4[:], Xt(i, 1, TC + 1), MU[:, 0:TC], op=OP.subtract)
                            s_dve.inc(v.tensor_tensor(Bt(i), xs4[:], RSTD[:, 0:TC], op=OP.mult))
                XLN4_DVE = s_dve.val

            if "D" in P4:
                gsb = f4.enter_context(nc.sbuf_tensor("gsb", [128, JT * 512], BF16))
                q1s = [f4.enter_context(nc.sbuf_tensor(f"q1s{p}", [128, KT * 128], BF16)) for p in range(2)]
                q3s = [f4.enter_context(nc.sbuf_tensor(f"q3s{p}", [128, KT * 128], BF16)) for p in range(2)]
                silb = [f4.enter_context(nc.sbuf_tensor(f"silb{p}", [128, 512], F32)) for p in range(2)]
                q2s = [f4.enter_context(nc.sbuf_tensor(f"q2s{p}", [128, 1024], BF16)) for p in range(2)]

                W2MM = {"v": s_mm.val}
                W2DVE = {"v": s_dve.val}
                for half in range(2):
                    hp = ExitStack()
                    ph1 = [hp.enter_context(nc.psum_tensor(f"ph1{half}{p}", [128, 512], F32)) for p in range(2)]
                    ph3 = [hp.enter_context(nc.psum_tensor(f"ph3{half}{p}", [128, 512], F32)) for p in range(2)]

                    ldh = s_ld.val
                    mmh = s_mm.val
                    acth = s_act.val
                    dveh = s_dve.val
                    w2prev = W2MM["v"]
                    w2dveprev = W2DVE["v"]

                    @block.sync
                    def _(sync, half=half, w2prev=w2prev):
                        sync.wait_ge(s_cc.sem, Q4["CC_Q"])
                        sync.wait_ge(s_dve.sem, XLN4_DVE)
                        if half > 0:
                            sync.wait_ge(s_mm.sem, w2prev)
                        for j in range(JT):
                            if j >= 1:
                                sync.wait_ge(s_mm.sem, mmh + 2 * j - 2 if j >= 2 else mmh)
                            s_ld.inc(sync.dma_start(
                                out=q1s[j % 2][:],
                                in_=cout_q[j % NCORE,
                                           (0 * NQS + j // NCORE) * 128:
                                           (0 * NQS + j // NCORE + 1) * 128, :]), 16)
                            s_ld.inc(sync.dma_start(
                                out=q3s[j % 2][:],
                                in_=cout_q[j % NCORE,
                                           (1 * NQS + j // NCORE) * 128:
                                           (1 * NQS + j // NCORE + 1) * 128, :]), 16)

                    @block.tensor
                    def _(t, half=half, w2prev=w2prev, w2dveprev=w2dveprev):
                        t.wait_ge(s_dve.sem, XLN4_DVE)
                        if half > 0:
                            t.wait_ge(s_mm.sem, w2prev)
                            t.wait_ge(s_dve.sem, w2dveprev)
                        for j in range(JT):
                            t.wait_ge(s_ld.sem, ldh + 32 * (j + 1))
                            if j >= 2:
                                t.wait_ge(s_dve.sem, dveh + (j - 1))
                            inst = None
                            for kk in range(KT):
                                inst = t.matmul(ph1[j % 2][:], q1s[j % 2][:, kk * 128:(kk + 1) * 128],
                                                Bt(kk, half * 512, (half + 1) * 512),
                                                start=(kk == 0), stop=(kk == KT - 1))
                            s_mm.inc(inst)
                            for kk in range(KT):
                                inst = t.matmul(ph3[j % 2][:], q3s[j % 2][:, kk * 128:(kk + 1) * 128],
                                                Bt(kk, half * 512, (half + 1) * 512),
                                                start=(kk == 0), stop=(kk == KT - 1))
                            s_mm.inc(inst)

                    @block.scalar
                    def _(s):
                        for j in range(JT):
                            s.wait_ge(s_mm.sem, mmh + 2 * j + 1)
                            if j >= 2:
                                s.wait_ge(s_dve.sem, dveh + (j - 1))
                            s_act.inc(s.activation(silb[j % 2][:], ph1[j % 2][:], AF.Silu,
                                                   scale=dsc(2)))

                    @block.vector
                    def _(v, half=half):
                        for j in range(JT):
                            v.wait_ge(s_mm.sem, mmh + 2 * j + 2)
                            v.wait_ge(s_act.sem, acth + j + 1)
                            s_dve.inc(v.scalar_tensor_tensor(gsb[:, j * 512:(j + 1) * 512],
                                                             ph3[j % 2][:], dsc(3), silb[j % 2][:],
                                                             op0=OP.mult, op1=OP.mult))
                    G_DVE = s_dve.val
                    hp.close()

                    w2p = ExitStack()
                    pw2 = [w2p.enter_context(nc.psum_tensor(f"pw2{half}{b}", [128, 512], F32))
                           for b in range(8)]
                    for grp in range(2):
                        ldw2 = s_ld.val
                        mmw2 = s_mm.val
                        dvew2 = s_dve.val

                        @block.sync
                        def _(sync, grp=grp, half=half):
                            sync.wait_ge(s_dve.sem, G_DVE)
                            for j in range(JT):
                                if j >= 2:
                                    sync.wait_ge(s_mm.sem, mmw2 + (j - 1))
                                src = cout_q[j % NCORE,
                                             (2 * NQS + j // NCORE) * 128:
                                             (2 * NQS + j // NCORE + 1) * 128,
                                             grp * 1024:(grp + 1) * 1024]
                                s_ld.inc(sync.dma_start(out=q2s[j % 2][:], in_=src), 16)

                        @block.tensor
                        def _(t, grp=grp, half=half):
                            t.wait_ge(s_dve.sem, G_DVE)
                            if grp > 0:
                                t.wait_ge(s_dve.sem, dvew2)
                            for j in range(JT):
                                t.wait_ge(s_ld.sem, ldw2 + 16 * (j + 1))
                                inst = None
                                for dt_ in range(8):
                                    inst = t.matmul(pw2[dt_][:],
                                                    q2s[j % 2][:, dt_ * 128:(dt_ + 1) * 128],
                                                    gsb[:, j * 512:(j + 1) * 512],
                                                    start=(j == 0), stop=(j == JT - 1))
                                s_mm.inc(inst)
                        MM_W2 = s_mm.val

                        @block.vector
                        def _(v, grp=grp, half=half):
                            v.wait_ge(s_mm.sem, MM_W2)
                            for dt_ in range(8):
                                i_gl = grp * 8 + dt_
                                xsl = Xt(i_gl, 1 + half * 512, 1 + (half + 1) * 512)
                                s_dve.inc(v.scalar_tensor_tensor(xsl, pw2[dt_][:], dsc(4), xsl,
                                                                 op0=OP.mult, op1=OP.add))
                    W2MM["v"] = s_mm.val
                    W2DVE["v"] = s_dve.val
                    w2p.close()
                f4.close()
            if "C" in P4 and "D" not in P4:
                f4.close()

        if NPAD:
            eng = {"vector": block.vector, "tensor": block.tensor,
                   "scalar": block.scalar, "sync": block.sync}[NPAD_ENG]

            @eng
            def _(e):
                for _ in range(NPAD):
                    e.wait_ge(s_ld.sem, 0)

        if DBG:
            dmu = nc.dram_tensor("dmu", [128, TC + 1], F32, kind="ExternalOutput").ap()
            drstd = nc.dram_tensor("drstd", [128, TC + 1], F32, kind="ExternalOutput").ap()
            dcarr = nc.dram_tensor("dcarr", [128, 2 * KT], F32, kind="ExternalOutput").ap()
            dtots = nc.dram_tensor("dtots", [128, 2 * KT], F32, kind="ExternalOutput").ap()
            dbb = nc.dram_tensor("dbb", [128, KT * TC], BF16, kind="ExternalOutput").ap()
            dder = nc.dram_tensor("dder", [128, 3 * KT + 8], F32, kind="ExternalOutput").ap()
            dfct = nc.dram_tensor("dfct", [128, 8 * KT], F32, kind="ExternalOutput").ap()
            dcout = nc.dram_tensor("dcout", [NCORE, 2 * KT * 128], F32, kind="ExternalOutput").ap()

        @block.gpsimd
        def _(g):
            g.wait_ge(s_ev.sem, s_ev.val)
            g.wait_ge(s_dve.sem, s_dve.val)
            for i in range(KT):
                s_st.inc(g.dma_start(out=yout[i * 128:(i + 1) * 128, :], in_=Xt(i, 1, TC + 1)), 16)
            if DBG:
                s_st.inc(g.dma_start(out=dmu[:], in_=MU[:]), 16)
                s_st.inc(g.dma_start(out=drstd[:], in_=RSTD[:]), 16)
                s_st.inc(g.dma_start(out=dcarr[:], in_=CARR[:]), 16)
                s_st.inc(g.dma_start(out=dtots[:], in_=TOTS[:]), 16)
                s_st.inc(g.dma_start(out=dbb[:], in_=BB[:]), 16)
                s_st.inc(g.dma_start(out=dder[:], in_=DER[:]), 16)
                s_st.inc(g.dma_start(out=dfct[:], in_=FCT[:]), 16)
                s_st.inc(g.dma_start(out=dcout[:], in_=cout_tot[:]), 16)
            g.wait_ge(s_st.sem, s_st.val)

    return nc


def _tp(a):
    return np.ascontiguousarray(np.asarray(a).T)


def _wtile(w):
    w = np.asarray(w)
    K, M = w.shape
    nk, nj = K // 128, M // 128
    return np.ascontiguousarray(
        w.reshape(nk, 128, nj, 128).transpose(2, 1, 0, 3).reshape(nj * 128, nk * 128))


def _colpack(vec):
    return np.ascontiguousarray(np.asarray(vec).reshape(KT, 128).T)


def kernel(**inputs):
    x = np.asarray(inputs["L0_attn_x"])[0]
    stages = N_STAGES

    pc = np.zeros((128, NPAR * KT), np.float32)
    pars = {
        "tmk": np.asarray(inputs["L0_attn_tmk"]).reshape(D),
        "tmv": np.asarray(inputs["L0_attn_tmv"]).reshape(D),
        "tmr": np.asarray(inputs["L0_attn_tmr"]).reshape(D),
        "bk": np.asarray(inputs["L0_attn_bk"]),
        "bv": np.asarray(inputs["L0_attn_bv"]),
        "br": np.asarray(inputs["L0_attn_br"]),
        "bo": np.asarray(inputs["L0_attn_bo"]),
        "td": np.asarray(inputs["L0_attn_time_decay"]),
        "tf": np.asarray(inputs["L0_attn_time_first"]),
        "bo1": np.asarray(inputs["L1_attn_bo"]),
    }
    for n, v in pars.items():
        pc[:, PAR[n] * KT:(PAR[n] + 1) * KT] = _colpack(v.astype(np.float32))

    bq = np.ascontiguousarray(np.asarray(inputs["L1_attn_bqkv"]).astype(np.float32)
                              .reshape(48, 128).T)

    consts = np.zeros((128, NCONST), np.float32)
    consts[:, C_NEG1] = -1.0
    consts[:, C_INVD] = INV_D
    consts[:, C_EPS] = EPS
    consts[:, C_ATTSC] = ATT_SCALE
    consts[:, C_NSHIFT] = -ATT_SHIFT
    consts[:, C_INVCNT2] = 0.5 / (JH * D)
    consts[:, C_ONE] = 1.0

    ws = np.float32(np.asarray(inputs["L0_ffn_weight_scale"]))
    sin = np.float32(np.asarray(inputs["L0_ffn_input_scale"]).astype(np.float32))
    scl = np.tile(np.array([[ws, sin]], np.float32), (128, 1))

    revramp = np.tile((np.float32(TC - 1) - np.arange(TC, dtype=np.float32))[None, :], (128, 1))

    m = np.zeros((128, 4 * 512), BF)
    s_i = np.arange(128)[:, None]
    q_i = np.arange(512)[None, :]
    for r in range(4):
        m[:, r * 512:(r + 1) * 512] = (s_i <= q_i - 128 * r).astype(BF)

    ident = np.eye(128, dtype=BF)
    idnf = np.eye(128, dtype=np.float32)
    idn8 = np.eye(8, dtype=np.float32)
    onescol = np.ones((128, 1), np.float32)
    onesrow = np.ones((1, 128), np.float32)
    onesf = np.ones((128, 128), np.float32)

    wkT = _wtile(_tp(inputs["L0_attn_Wk"]))
    wvT = _wtile(_tp(inputs["L0_attn_Wv"]))
    wrT = _wtile(_tp(inputs["L0_attn_Wr"]))
    woT = _wtile(_tp(inputs["L0_attn_Wo"]))
    w8T = _wtile(np.ascontiguousarray(np.asarray(inputs["L0_ffn_weight"]).view(F8).T))
    wqkv = _wtile(np.asarray(inputs["L1_attn_Wqkv"]).astype(np.float32))
    wo1 = _wtile(np.asarray(inputs["L1_attn_Wo"]).astype(np.float32))
    w1 = np.asarray(inputs["L1_ffn_w1"])
    w3 = np.asarray(inputs["L1_ffn_w3"])
    w2 = np.asarray(inputs["L1_ffn_w2"])
    w1T = np.zeros((D, JP), np.float32); w1T[:, 0:JH] = w1.T
    w3T = np.zeros((D, JP), np.float32); w3T[:, 0:JH] = w3.T
    w1T = _wtile(w1T)
    w3T = _wtile(w3T)
    w2T = np.zeros((JP, D), np.float32); w2T[0:JH, :] = w2.T
    wqmy_r = []
    for r in range(NCORE):
        qm = np.zeros((3 * NQS * 128, D), np.float32)
        for cw, wt in enumerate((w1T, w3T, w2T)):
            for s_i, j in enumerate(range(r, JT, NCORE)):
                qm[(cw * NQS + s_i) * 128:(cw * NQS + s_i + 1) * 128] = \
                    wt[j * 128:(j + 1) * 128]
        wqmy_r.append(qm)

    shared = dict(wkT=wkT, wvT=wvT, wrT=wrT, woT=woT, w8T=w8T, wqkv=wqkv, wo1=wo1,
                  pcols=pc, bqcols=bq, consts=consts,
                  scl=scl, revramp=revramp, masks=m, ident=ident, idn8=idn8, idnf=idnf, onesf=onesf,
                  onescol=onescol, onesrow=onesrow)

    in_maps = []
    for r in range(NCORE):
        xs = np.zeros((TC + 1, D), np.float32)
        if r > 0:
            xs[0] = x[r * TC - 1]
        xs[1:] = x[r * TC:(r + 1) * TC]
        e8 = np.zeros((128, 8), np.float32)
        for rp in range(8):
            e8[:, rp] = 1024.0 * (r - 1 - rp) if rp < r else 1e9
        in_maps.append(dict(shared, xT=_tp(xs), e1024=e8, wqmy=wqmy_r[r]))

    nc = build_nc(stages)
    res = bass_utils.run_bass_kernel_spmd(nc, in_maps, core_ids=list(range(NCORE)))
    if DBG:
        import pickle
        with open("/tmp/dbg_results.pkl", "wb") as f:
            pickle.dump([{k: np.asarray(v) for k, v in r.items()} for r in res.results], f)
    out = np.concatenate([np.asarray(r["y"]).T for r in res.results], axis=0)
    return out.reshape(1, T, D).astype(np.float32)


# version 07
# speedup vs baseline: 1.5249x; 1.0671x over previous
import numpy as np
import ml_dtypes
from contextlib import ExitStack

import concourse.bass as bass
import concourse.mybir as mybir
from concourse import bass_utils

F32 = mybir.dt.float32
BF16 = mybir.dt.bfloat16
FP8 = mybir.dt.float8e4
AF = mybir.ActivationFunctionType
OP = mybir.AluOpType
AX = mybir.AxisListType

NCORE = 8
D = 2048
T = 8192
TC = T // NCORE
KT = D // 128
NH, HD, HPC = 16, 128, 2
JH = 5461
JP = 5504
JT = JP // 128
NQS = (JT + NCORE - 1) // NCORE
INV_D = 1.0 / D
EPS = 1e-5
ATT_SCALE = float(1.0 / np.sqrt(128.0))
ATT_SHIFT = 8.0

N_STAGES = 4
S4 = 4
S4PARTS = None
S4LN = 4
S4A = 4


def _s4parts():
    if S4PARTS is not None:
        return S4PARTS
    p = ""
    if S4 >= 1: p += "A"
    if S4 >= 2: p += "B"
    if S4 >= 3: p += "C"
    if S4 >= 3.5: p += "c"
    if S4 >= 4: p += "D"
    return p
NPAD = 0
QHOIST = "AB"
NPAD_ENG = "vector"
DBG = 0
F8MODE = 2

BF = ml_dtypes.bfloat16
F8 = ml_dtypes.float8_e4m3

C_NEG1, C_INVD, C_EPS, C_ATTSC, C_NSHIFT, C_INVCNT2, C_ONE = range(7)
NCONST = 7
PAR = {n: i for i, n in enumerate(
    ["tmk", "tmv", "tmr", "bk", "bv", "br", "bo", "td", "tf", "bo1"])}
NPAR = len(PAR)


class SC:
    def __init__(self, nc, es, name):
        self.sem = es.enter_context(nc.semaphore(name))
        self.val = 0

    def inc(self, inst, n=1):
        inst.then_inc(self.sem, n)
        self.val += n
        return inst


def build_nc(stages):
    nc = bass.Bass("TRN2", target_bir_lowering=False, debug=False, num_devices=NCORE,
                   detect_race_conditions=False)
    es = ExitStack()

    def din(name, shape, dt):
        return nc.dram_tensor(name, shape, dt, kind="ExternalInput").ap()

    def dint(name, shape, dt):
        return nc.dram_tensor(name, shape, dt, kind="Internal").ap()

    xT = din("xT", [D, TC + 1], F32)
    wkT = din("wkT", [D, D], F32)
    wvT = din("wvT", [D, D], F32)
    wrT = din("wrT", [D, D], F32)
    woT = din("woT", [D, D], F32)
    w8T = din("w8T", [D, D], FP8)
    wqkv = din("wqkv", [3 * D, D], F32)
    wo1 = din("wo1", [D, D], F32)
    wqmy = din("wqmy", [3 * NQS * 128, D], F32)
    pcolsI = din("pcols", [128, NPAR * KT], F32)
    bqcolsI = din("bqcols", [128, 48], F32)
    constsI = din("consts", [128, NCONST], F32)
    sclI = din("scl", [128, 2], F32)
    e1024I = din("e1024", [128, 8], F32)
    revrampI = din("revramp", [128, TC], F32)
    masksI = din("masks", [128, 4 * 512], BF16)
    identI = din("ident", [128, 128], BF16)
    idn8I = din("idn8", [8, 8], F32)
    idnfI = din("idnf", [128, 128], F32)
    onescolI = din("onescol", [128, 1], F32)
    onesrowI = din("onesrow", [1, 128], F32)
    onesfI = din("onesf", [128, 128], F32)

    yout = nc.dram_tensor("y", [D, TC], F32, kind="ExternalOutput").ap()

    dkind = "ExternalOutput" if DBG else "Internal"
    k_hbm = nc.dram_tensor("k_hbm", [D, TC], F32, kind=dkind).ap()
    v_hbm = nc.dram_tensor("v_hbm", [D, TC], BF16, kind=dkind).ap()
    sr_hbm = nc.dram_tensor("sr_hbm", [D, TC], BF16, kind=dkind).ap()
    cin_tot = dint("cin_tot", [2 * KT * 128], F32)
    cout_tot = dint("cout_tot", [NCORE, 2 * KT * 128], F32)
    cin_sc = dint("cin_sc", [128 * 512], F32)
    cout_sc = dint("cout_sc", [NCORE, 128 * 512], F32)
    cin_kqv = dint("cin_kqv", [3 * D, TC], BF16)
    cout_kqv = dint("cout_kqv", [NCORE, 3 * D, TC], BF16)
    cin_o = dint("cin_o", [HPC * HD, T], BF16)
    cout_o = dint("cout_o", [NCORE, HPC * HD, T], BF16)
    q_loc = dint("q_loc", [HPC * HD, T], BF16)
    o_loc = dint("o_loc", [D, TC], BF16)
    cin_q = dint("cin_q", [3 * NQS * 128, D], BF16)
    cout_q = dint("cout_q", [NCORE, 3 * NQS * 128, D], BF16)

    with es:
        X = es.enter_context(nc.sbuf_tensor("X", [128, KT * (TC + 1)], F32))
        BB = es.enter_context(nc.sbuf_tensor("BB", [128, KT * TC], BF16))
        PC = es.enter_context(nc.sbuf_tensor("PC", [128, NPAR * KT], F32))
        BQ = es.enter_context(nc.sbuf_tensor("BQ", [128, 48], F32))
        CST = es.enter_context(nc.sbuf_tensor("CST", [128, NCONST], F32))
        SCLT = es.enter_context(nc.sbuf_tensor("SCLT", [128, 2], F32))
        E8 = es.enter_context(nc.sbuf_tensor("E8", [128, 8], F32))
        RR = es.enter_context(nc.sbuf_tensor("RR", [128, TC], F32))
        MSK = es.enter_context(nc.sbuf_tensor("MSK", [128, 4 * 512], BF16))
        IDN = es.enter_context(nc.sbuf_tensor("IDN", [128, 128], BF16))
        ID8 = es.enter_context(nc.sbuf_tensor("ID8", [8, 8], F32))
        IDF = es.enter_context(nc.sbuf_tensor("IDF", [128, 128], F32))
        OCol = es.enter_context(nc.sbuf_tensor("OCol", [128, 1], F32))
        ORow = es.enter_context(nc.sbuf_tensor("ORow", [1, 128], F32))
        ONESF = es.enter_context(nc.sbuf_tensor("ONESF", [128, 128], F32))
        DER = es.enter_context(nc.sbuf_tensor("DER", [128, 3 * KT + 8], F32))
        FCT = es.enter_context(nc.sbuf_tensor("FCT", [128, 8 * KT], F32))
        MU = es.enter_context(nc.sbuf_tensor("MU", [128, TC + 1], F32))
        RSTD = es.enter_context(nc.sbuf_tensor("RSTD", [128, TC + 1], F32))
        ST1 = es.enter_context(nc.sbuf_tensor("ST1", [1, TC + 1], F32))
        ST2 = es.enter_context(nc.sbuf_tensor("ST2", [1, TC + 1], F32))
        SROW = es.enter_context(nc.sbuf_tensor("SROW", [1, TC + 1], F32))
        CARR = es.enter_context(nc.sbuf_tensor("CARR", [128, 2 * KT], F32))
        TOTS = es.enter_context(nc.sbuf_tensor("TOTS", [128, 2 * KT], F32))
        SC4 = es.enter_context(nc.sbuf_tensor("SC4", [8, 4], F32))
        SCD = es.enter_context(nc.sbuf_tensor("SCD", [1, 8], F32))
        TOTT = es.enter_context(nc.sbuf_tensor("TOTT", [32, 128], F32))

        def Xt(i, lo=0, hi=TC + 1):
            return X[:, i * (TC + 1) + lo: i * (TC + 1) + hi]

        def Bt(i, lo=0, hi=TC):
            return BB[:, i * TC + lo: i * TC + hi]

        def pcol(name, i):
            j = PAR[name] * KT + i
            return PC[:, j:j + 1]

        def dw(i):  return DER[:, i:i + 1]
        def dew(i): return DER[:, KT + i:KT + i + 1]
        def deb(i): return DER[:, 2 * KT + i:2 * KT + i + 1]
        DSB = 3 * KT
        def dsc(j): return DER[:, DSB + j:DSB + j + 1]
        def cc(j):  return CST[:, j:j + 1]
        def cc1(j): return CST[0:1, j:j + 1]

        s_ld = SC(nc, es, "s_ld")
        s_st = SC(nc, es, "s_st")
        s_gld = SC(nc, es, "s_gld")
        s_ast = SC(nc, es, "s_ast")
        s_act = SC(nc, es, "s_act")
        s_dve = SC(nc, es, "s_dve")
        s_wld = SC(nc, es, "s_wld")
        s_wc = SC(nc, es, "s_wc")
        s_mm = SC(nc, es, "s_mm")
        s_ev = SC(nc, es, "s_ev")
        s_cc = SC(nc, es, "s_cc")

        block = es.enter_context(nc.Block())

        @block.sync
        def _(sync):
            for i in range(KT):
                s_ld.inc(sync.dma_start(out=Xt(i), in_=xT[i * 128:(i + 1) * 128, :]), 16)
            for dst, src in [(PC, pcolsI), (BQ, bqcolsI), (CST, constsI), (SCLT, sclI),
                             (E8, e1024I), (RR, revrampI), (MSK, masksI), (IDN, identI),
                             (ID8, idn8I), (IDF, idnfI), (OCol, onescolI), (ORow, onesrowI),
                             (ONESF, onesfI)]:
                s_ld.inc(sync.dma_start(out=dst[:], in_=src[:]), 16)
        SETUP_LD = s_ld.val

        a0 = s_act.val

        @block.scalar
        def _(s):
            s.wait_ge(s_ld.sem, SETUP_LD)
            s_act.inc(s.activation(DER[:, 0:KT], PC[:, PAR["td"] * KT:(PAR["td"] + 1) * KT], AF.Exp))

        d0 = s_dve.val

        @block.vector
        def _(v):
            v.wait_ge(s_act.sem, a0 + 1)
            s_dve.inc(v.tensor_scalar(DER[:, 0:KT], DER[:, 0:KT], cc(C_NEG1), None, op0=OP.mult))
            v.wait_ge(s_dve.sem, d0 + 1)
            for i in range(KT - 1):
                v.tensor_scalar(FCT[:, 8 * i:8 * i + 8], E8[:], dw(i), None, op0=OP.mult)
            s_dve.inc(v.tensor_scalar(FCT[:, 8 * (KT - 1):8 * KT], E8[:], dw(KT - 1), None, op0=OP.mult))

        @block.scalar
        def _(s):
            s.wait_ge(s_dve.sem, d0 + 2)
            s.activation(DER[:, KT:2 * KT], DER[:, 0:KT], AF.Exp)
            s.activation(FCT[:], FCT[:], AF.Exp)
            s_act.inc(s.activation(DER[:, 2 * KT:3 * KT], PC[:, PAR["tf"] * KT:(PAR["tf"] + 1) * KT], AF.Exp))
        DER_READY = s_act.val

        dve_scb = s_dve.val

        @block.vector
        def _(v):
            v.wait_ge(s_ld.sem, SETUP_LD)
            v.reciprocal(DER[:, DSB:DSB + 1], SCLT[:, 1:2])
            s_dve.inc(v.tensor_tensor(DER[:, DSB + 1:DSB + 2], SCLT[:, 0:1], SCLT[:, 1:2],
                                      op=OP.mult))
        SCALES_DVE = s_dve.val

        def emit_ln(off, ncols, tag, pre_wait=None, upto=4):
            if pre_wait is not None and not isinstance(pre_wait, list):
                pre_wait = [pre_wait]
            ps = ExitStack()
            chunks = []
            c0 = 0
            while c0 < ncols:
                chunks.append((c0, min(512, ncols - c0)))
                c0 += 512
            nch = len(chunks)
            psx = [ps.enter_context(nc.psum_tensor(f"psx{tag}{c}", [1, 512], F32)) for c in range(nch)]
            psq = [ps.enter_context(nc.psum_tensor(f"psq{tag}{c}", [1, 512], F32)) for c in range(nch)]
            sqb = [ps.enter_context(nc.sbuf_tensor(f"sq{tag}{c}", [128, ncols], F32)) for c in range(2)]

            act_b = s_act.val
            mm_b = s_mm.val

            @block.scalar
            def _(s):
                s.wait_ge(s_ld.sem, SETUP_LD)
                if pre_wait is not None:
                    for pw_s, pw_v in pre_wait:
                        s.wait_ge(pw_s.sem, pw_v)
                for i in range(KT):
                    if i >= 2:
                        s.wait_ge(s_mm.sem, mm_b + (i - 1))
                    s_act.inc(s.activation(sqb[i % 2][:], Xt(i, off, off + ncols), AF.Square))

            @block.tensor
            def _(t):
                t.wait_ge(s_ld.sem, SETUP_LD)
                if pre_wait is not None:
                    for pw_s, pw_v in pre_wait:
                        t.wait_ge(pw_s.sem, pw_v)
                for i in range(KT):
                    for ci, (o, w) in enumerate(chunks):
                        t.matmul(psx[ci][:, 0:w], OCol[:], Xt(i, off + o, off + o + w),
                                 start=(i == 0), stop=(i == KT - 1))
                    t.wait_ge(s_act.sem, act_b + i + 1)
                    inst = None
                    for ci, (o, w) in enumerate(chunks):
                        inst = t.matmul(psq[ci][:, 0:w], OCol[:], sqb[i % 2][:, o:o + w],
                                        start=(i == 0), stop=(i == KT - 1))
                    s_mm.inc(inst)
            MM_STATS = s_mm.val

            dve_lnb = s_dve.val

            if upto < 2:
                ps.close()
                return s_ev.val

            @block.vector
            def _(v):
                v.wait_ge(s_mm.sem, MM_STATS)
                for ci, (o, w) in enumerate(chunks):
                    v.tensor_scalar(ST1[0:1, o:o + w], psx[ci][:, 0:w], cc1(C_INVD), None, op0=OP.mult)
                    v.tensor_scalar(ST2[0:1, o:o + w], psq[ci][:, 0:w], cc1(C_INVD), None, op0=OP.mult)
                s_dve.inc(v.engine_nop())
                v.wait_ge(s_dve.sem, dve_lnb + 1)
                s_dve.inc(v.tensor_tensor(SROW[0:1, 0:ncols], ST1[0:1, 0:ncols], ST1[0:1, 0:ncols],
                                          op=OP.mult))
                v.wait_ge(s_dve.sem, dve_lnb + 2)
                s_dve.inc(v.tensor_tensor(ST2[0:1, 0:ncols], ST2[0:1, 0:ncols], SROW[0:1, 0:ncols],
                                          op=OP.subtract))
            DVE_VAR = s_dve.val

            act_b2 = s_act.val

            if upto < 3:
                ps.close()
                return s_ev.val

            @block.scalar
            def _(s):
                s.wait_ge(s_dve.sem, DVE_VAR)
                s_act.inc(s.activation(SROW[0:1, 0:ncols], ST2[0:1, 0:ncols], AF.Sqrt, bias=cc1(C_EPS)))

            @block.vector
            def _(v):
                v.wait_ge(s_act.sem, act_b2 + 1)
                s_dve.inc(v.reciprocal(SROW[0:1, 0:ncols], SROW[0:1, 0:ncols]))
            DVE_RSTD = s_dve.val

            ps.close()
            if upto < 4:
                return s_ev.val
            pb = ExitStack()
            psbc = [pb.enter_context(nc.psum_tensor(f"psb{tag}{c}", [128, 512], F32))
                    for c in range(2 * nch)]
            mm_b2 = s_mm.val

            @block.tensor
            def _(t):
                t.wait_ge(s_dve.sem, DVE_RSTD)
                inst = None
                for ci, (o, w) in enumerate(chunks):
                    t.matmul(psbc[2 * ci][:, 0:w], ORow[:], ST1[0:1, o:o + w], start=True, stop=True)
                    inst = t.matmul(psbc[2 * ci + 1][:, 0:w], ORow[:], SROW[0:1, o:o + w],
                                    start=True, stop=True)
                s_mm.inc(inst)
            MM_B = s_mm.val

            @block.vector
            def _(v):
                v.wait_ge(s_mm.sem, MM_B)
                inst = None
                for ci, (o, w) in enumerate(chunks):
                    v.tensor_copy(MU[:, o:o + w], psbc[2 * ci][:, 0:w])
                    inst = v.tensor_copy(RSTD[:, o:o + w], psbc[2 * ci + 1][:, 0:w])
                s_ev.inc(inst)
            pb.close()
            return s_ev.val

        def emit_proj(tag, wdram, ncols, nk, rhs_fn, evac_fn, evac_eng, rhs_wait,
                      store_fn=None, wdt=F32, pre_waits=()):
            ps = ExitStack()
            nj = ncols // 128
            ntb = 2
            wf = ps.enter_context(nc.sbuf_tensor(f"wf{tag}", [128, nk * 128], wdt))
            wb = [ps.enter_context(nc.sbuf_tensor(f"wb{tag}{p}", [128, nk * 128], BF16))
                  for p in range(2)]
            NB = 4
            pmm = [ps.enter_context(nc.psum_tensor(f"pp{tag}{b}", [128, 512], F32)) for b in range(NB)]

            wld_b = s_wld.val
            wc_b = s_wc.val
            mm_b = s_mm.val
            ev_b = s_ev.val
            st_b = s_st.val

            @block.sync
            def _(sync):
                for sc_, v_ in pre_waits:
                    sync.wait_ge(sc_.sem, v_)
                for j in range(nj):
                    sync.wait_ge(s_wc.sem, wc_b + j)
                    s_wld.inc(sync.dma_start(out=wf[:], in_=wdram[j * 128:(j + 1) * 128, :]), 16)

            def _emit_evac_tile(e, j):
                for tb in range(ntb):
                    g = ntb * j + tb
                    e.wait_ge(s_mm.sem, mm_b + g + 1)
                    if store_fn is not None and tb == 0:
                        e.wait_ge(s_st.sem, st_b + 16 * (j - 1) if j >= 2 else st_b)
                    inst = evac_fn(e, pmm[g % NB][:], j, tb)
                    s_ev.inc(inst)

            if evac_eng == "scalar":
                @block.scalar
                def _(s):
                    for j in range(nj):
                        s.wait_ge(s_wld.sem, wld_b + 16 * (j + 1))
                        s.wait_ge(s_mm.sem, mm_b + ntb * (j - 1) if j >= 2 else mm_b)
                        s_wc.inc(s.activation(wb[j % 2][:], wf[:], AF.Identity))
                        if j >= 2:
                            _emit_evac_tile(s, j - 2)
                    for j in (nj - 2, nj - 1):
                        _emit_evac_tile(s, j)
            else:
                @block.scalar
                def _(s):
                    for j in range(nj):
                        s.wait_ge(s_wld.sem, wld_b + 16 * (j + 1))
                        s.wait_ge(s_mm.sem, mm_b + ntb * (j - 1) if j >= 2 else mm_b)
                        s_wc.inc(s.activation(wb[j % 2][:], wf[:], AF.Identity))

            @block.tensor
            def _(t):
                t.wait_ge(rhs_wait[0].sem, rhs_wait[1])
                for j in range(nj):
                    t.wait_ge(s_wc.sem, wc_b + j + 1)
                    for tb in range(ntb):
                        g = ntb * j + tb
                        t.wait_ge(s_ev.sem, ev_b + g - NB + 1 if g >= NB else ev_b)
                        inst = None
                        for kk in range(nk):
                            inst = t.matmul(pmm[g % NB][:],
                                            wb[j % 2][:, kk * 128:(kk + 1) * 128],
                                            rhs_fn(kk, tb),
                                            start=(kk == 0), stop=(kk == nk - 1))
                        s_mm.inc(inst)

            if evac_eng != "scalar":
                @block.vector
                def _(v):
                    for j in range(nj):
                        _emit_evac_tile(v, j)

            if store_fn is not None:
                @block.gpsimd
                def _(g):
                    for j in range(nj):
                        g.wait_ge(s_ev.sem, ev_b + ntb * (j + 1))
                        s_st.inc(store_fn(g, j), 16)
            ps.close()

        LN1_EV = emit_ln(0, TC + 1, "l1")

        l0 = ExitStack()
        xs1 = l0.enter_context(nc.sbuf_tensor("xs1", [128, TC + 1], F32))
        xs2 = l0.enter_context(nc.sbuf_tensor("xs2", [128, TC], F32))
        kvst = [l0.enter_context(nc.sbuf_tensor(f"kvst{p}", [128, TC], F32)) for p in range(2)]
        bfst = [l0.enter_context(nc.sbuf_tensor(f"bfst{p}", [128, TC], BF16)) for p in range(2)]

        def emit_lerp(parname, waitv):
            @block.vector
            def _(v):
                v.wait_ge(s_ev.sem, LN1_EV)
                if waitv is not None:
                    v.wait_ge(s_mm.sem, waitv)
                for i in range(KT):
                    v.tensor_tensor(xs1[:], Xt(i), MU[:, 0:TC + 1], op=OP.subtract)
                    v.tensor_tensor(xs1[:], xs1[:], RSTD[:, 0:TC + 1], op=OP.mult)
                    v.tensor_tensor(xs2[:], xs1[:, 1:TC + 1], xs1[:, 0:TC], op=OP.subtract)
                    s_dve.inc(v.scalar_tensor_tensor(Bt(i), xs2[:], pcol(parname, i),
                                                     xs1[:, 0:TC], op0=OP.mult, op1=OP.add))
            return s_dve.val

        LERP_K = emit_lerp("tmk", None)

        def evac_k(e, psum, j, tb):
            return e.activation(kvst[j % 2][:, tb * 512:(tb + 1) * 512], psum, AF.Identity,
                                bias=pcol("bk", j))

        emit_proj("k", wkT, D, KT, lambda kk, tb: Bt(kk, tb * 512, (tb + 1) * 512),
                  evac_k, "scalar", (s_dve, LERP_K),
                  store_fn=lambda g, j: g.dma_start(out=k_hbm[j * 128:(j + 1) * 128, :],
                                                    in_=kvst[j % 2][:]))

        LERP_V = emit_lerp("tmv", s_mm.val)

        def evac_v(e, psum, j, tb):
            return e.activation(bfst[j % 2][:, tb * 512:(tb + 1) * 512], psum, AF.Identity,
                                bias=pcol("bv", j))

        emit_proj("v", wvT, D, KT, lambda kk, tb: Bt(kk, tb * 512, (tb + 1) * 512),
                  evac_v, "scalar", (s_dve, LERP_V),
                  store_fn=lambda g, j: g.dma_start(out=v_hbm[j * 128:(j + 1) * 128, :],
                                                    in_=bfst[j % 2][:]))

        LERP_R = emit_lerp("tmr", s_mm.val)

        def evac_r(e, psum, j, tb):
            return e.activation(bfst[j % 2][:, tb * 512:(tb + 1) * 512], psum, AF.Sigmoid,
                                bias=pcol("br", j))

        emit_proj("r", wrT, D, KT, lambda kk, tb: Bt(kk, tb * 512, (tb + 1) * 512),
                  evac_r, "scalar", (s_dve, LERP_R),
                  store_fn=lambda g, j: g.dma_start(out=sr_hbm[j * 128:(j + 1) * 128, :],
                                                    in_=bfst[j % 2][:]))
        KVRS_ST = s_st.val

        l0.close()

        scn = ExitStack()
        kbuf = [scn.enter_context(nc.sbuf_tensor(f"kbf{p}", [128, TC], F32)) for p in range(2)]
        vbuf = [scn.enter_context(nc.sbuf_tensor(f"vbf{p}", [128, TC], BF16)) for p in range(2)]
        srbuf = [scn.enter_context(nc.sbuf_tensor(f"srbf{p}", [128, TC], BF16)) for p in range(2)]
        ekb = [scn.enter_context(nc.sbuf_tensor(f"ekbf{p}", [128, TC], F32)) for p in range(2)]
        argb = scn.enter_context(nc.sbuf_tensor("argb", [128, TC], F32))
        wrev = scn.enter_context(nc.sbuf_tensor("wrev", [128, TC], F32))
        ubuf = scn.enter_context(nc.sbuf_tensor("ubuf", [128, TC], F32))
        tbuf = scn.enter_context(nc.sbuf_tensor("tbuf", [128, TC], F32))
        arun = scn.enter_context(nc.sbuf_tensor("arun", [128, TC], F32))
        brun = scn.enter_context(nc.sbuf_tensor("brun", [128, TC], F32))
        aprev = scn.enter_context(nc.sbuf_tensor("aprev", [128, TC], F32))
        bprev = scn.enter_context(nc.sbuf_tensor("bprev", [128, TC], F32))
        ga8 = scn.enter_context(nc.sbuf_tensor("ga8", [8, 256], F32))
        gat = scn.enter_context(nc.sbuf_tensor("gat", [128, 16], F32))

        gldA = s_gld.val
        actA = s_act.val
        dveA = s_dve.val

        @block.gpsimd
        def _(g):
            g.wait_ge(s_st.sem, KVRS_ST)
            for i in range(KT):
                if i >= 2:
                    g.wait_ge(s_dve.sem, dveA + 2 * (i - 2) + 2)
                s_gld.inc(g.dma_start(out=kbuf[i % 2][:], in_=k_hbm[i * 128:(i + 1) * 128, :]), 16)
                s_gld.inc(g.dma_start(out=vbuf[i % 2][:], in_=v_hbm[i * 128:(i + 1) * 128, :]), 16)

        @block.scalar
        def _(s):
            s.wait_ge(s_act.sem, DER_READY)
            for i in range(KT):
                s.wait_ge(s_gld.sem, gldA + 32 * i + 16)
                if i >= 2:
                    s.wait_ge(s_dve.sem, dveA + 2 * (i - 2) + 2)
                s_act.inc(s.activation(ekb[i % 2][:], kbuf[i % 2][:], AF.Exp))
                s.wait_ge(s_dve.sem, dveA + 2 * i + 1)
                s_act.inc(s.activation(wrev[:], argb[:], AF.Exp))

        @block.vector
        def _(v):
            v.wait_ge(s_act.sem, DER_READY)
            v.wait_ge(s_st.sem, KVRS_ST)
            for i in range(KT):
                if i >= 1:
                    v.wait_ge(s_act.sem, actA + 2 * i)
                s_dve.inc(v.tensor_scalar(argb[:], RR[:], dw(i), None, op0=OP.mult))
                v.wait_ge(s_act.sem, actA + 2 * i + 2)
                v.wait_ge(s_gld.sem, gldA + 32 * (i + 1))
                v.tensor_tensor(ubuf[:], ekb[i % 2][:], vbuf[i % 2][:], op=OP.mult)
                v.tensor_tensor(tbuf[:], ubuf[:], wrev[:], op=OP.mult)
                v.tensor_reduce(TOTS[:, i:i + 1], tbuf[:], axis=AX.X, op=OP.add)
                v.tensor_tensor(tbuf[:], ekb[i % 2][:], wrev[:], op=OP.mult)
                s_dve.inc(v.tensor_reduce(TOTS[:, KT + i:KT + i + 1], tbuf[:], axis=AX.X, op=OP.add))
        PASSA_DVE = s_dve.val

        ccA = s_cc.val
        ptot_es = ExitStack()
        ptot = ptot_es.enter_context(nc.psum_tensor("ptot", [32, 128], F32))
        mmT_b = s_mm.val

        @block.tensor
        def _(t):
            t.wait_ge(s_dve.sem, PASSA_DVE)
            s_mm.inc(t.transpose(ptot[:], TOTS[:], IDF[:]))

        dveT_b = s_dve.val

        @block.vector
        def _(v):
            v.wait_ge(s_mm.sem, mmT_b + 1)
            s_dve.inc(v.tensor_copy(TOTT[:], ptot[:]))
        ptot_es.close()

        @block.gpsimd
        def _(g):
            g.wait_ge(s_dve.sem, dveT_b + 1)
            s_st.inc(g.dma_start(out=cin_tot.rearrange("(r p) -> r p", r=32), in_=TOTT[:]), 16)
            g.wait_ge(s_st.sem, s_st.val)
            s_cc.inc(g.collective_compute(
                "AllGather", OP.bypass, replica_groups=[list(range(NCORE))],
                ins=[cin_tot[:]], outs=[cout_tot[:]]))

        ldC = s_ld.val
        mmC = s_mm.val
        dveC = s_dve.val
        pcar = ExitStack()
        pga = pcar.enter_context(nc.psum_tensor("pga", [128, 8], F32))
        pgb = pcar.enter_context(nc.psum_tensor("pgb", [128, 8], F32))

        @block.sync
        def _(sync):
            sync.wait_ge(s_cc.sem, ccA + 1)
            for i in range(KT):
                if i >= 1:
                    sync.wait_ge(s_mm.sem, mmC + 2 * i)
                cg = cout_tot
                s_ld.inc(sync.dma_start(out=ga8[:, 0:128], in_=cg[:, i * 128:(i + 1) * 128]), 16)
                s_ld.inc(sync.dma_start(out=ga8[:, 128:256],
                                        in_=cg[:, 2048 + i * 128:2048 + (i + 1) * 128]), 16)

        @block.tensor
        def _(t):
            for i in range(KT):
                t.wait_ge(s_ld.sem, ldC + 32 * (i + 1))
                if i >= 1:
                    t.wait_ge(s_dve.sem, dveC + 3 * i)
                t.transpose(pga[:], ga8[:, 0:128], ID8[:])
                s_mm.inc(t.transpose(pgb[:], ga8[:, 128:256], ID8[:]), 2)

        @block.vector
        def _(v):
            for i in range(KT):
                v.wait_ge(s_mm.sem, mmC + 2 * (i + 1))
                v.tensor_tensor(gat[:, 0:8], pga[:], FCT[:, 8 * i:8 * i + 8], op=OP.mult)
                s_dve.inc(v.tensor_tensor(gat[:, 8:16], pgb[:], FCT[:, 8 * i:8 * i + 8], op=OP.mult))
                v.wait_ge(s_dve.sem, dveC + 3 * i + 1)
                v.tensor_reduce(CARR[:, i:i + 1], gat[:, 0:8], axis=AX.X, op=OP.add)
                s_dve.inc(v.tensor_reduce(CARR[:, KT + i:KT + i + 1], gat[:, 8:16],
                                          axis=AX.X, op=OP.add), 2)
        CARR_DVE = s_dve.val
        pcar.close()

        gldB = s_gld.val
        actB = s_act.val
        dveB = s_dve.val

        @block.gpsimd
        def _(g):
            for i in range(KT):
                if i >= 2:
                    g.wait_ge(s_dve.sem, dveB + (i - 1))
                s_gld.inc(g.dma_start(out=kbuf[i % 2][:], in_=k_hbm[i * 128:(i + 1) * 128, :]), 16)
                s_gld.inc(g.dma_start(out=vbuf[i % 2][:], in_=v_hbm[i * 128:(i + 1) * 128, :]), 16)
                s_gld.inc(g.dma_start(out=srbuf[i % 2][:], in_=sr_hbm[i * 128:(i + 1) * 128, :]), 16)

        @block.scalar
        def _(s):
            for i in range(KT):
                s.wait_ge(s_gld.sem, gldB + 48 * i + 16)
                if i >= 2:
                    s.wait_ge(s_dve.sem, dveB + (i - 1))
                s_act.inc(s.activation(ekb[i % 2][:], kbuf[i % 2][:], AF.Exp))

        @block.vector
        def _(v):
            v.wait_ge(s_dve.sem, CARR_DVE)
            for i in range(KT):
                v.wait_ge(s_act.sem, actB + i + 1)
                v.wait_ge(s_gld.sem, gldB + 48 * (i + 1))
                ek = ekb[i % 2]
                v.tensor_tensor(ubuf[:], ek[:], vbuf[i % 2][:], op=OP.mult)
                v.tensor_tensor_scan(arun[:], dew(i).broadcast_to([128, TC]), ubuf[:],
                                     CARR[:, i:i + 1], op0=OP.mult, op1=OP.add)
                v.tensor_tensor_scan(brun[:], dew(i).broadcast_to([128, TC]), ek[:],
                                     CARR[:, KT + i:KT + i + 1], op0=OP.mult, op1=OP.add)
                v.tensor_copy(aprev[:, 1:TC], arun[:, 0:TC - 1])
                v.tensor_copy(aprev[:, 0:1], CARR[:, i:i + 1])
                v.tensor_copy(bprev[:, 1:TC], brun[:, 0:TC - 1])
                v.tensor_copy(bprev[:, 0:1], CARR[:, KT + i:KT + i + 1])
                v.scalar_tensor_tensor(ubuf[:], ubuf[:], deb(i), aprev[:], op0=OP.mult, op1=OP.add)
                v.scalar_tensor_tensor(tbuf[:], ek[:], deb(i), bprev[:], op0=OP.mult, op1=OP.add)
                v.reciprocal(tbuf[:], tbuf[:])
                v.tensor_tensor(ubuf[:], ubuf[:], tbuf[:], op=OP.mult)
                s_dve.inc(v.tensor_tensor(Bt(i), ubuf[:], srbuf[i % 2][:], op=OP.mult))
        SCAN_DVE = s_dve.val
        scn.close()

        wo_ps = ExitStack()

        def evac_wo(e, psum, j, tb):
            xsl = Xt(j, 1 + tb * 512, 1 + (tb + 1) * 512)
            return e.scalar_tensor_tensor(xsl, psum, pcol("bo", j), xsl, op0=OP.add, op1=OP.add)

        emit_proj("wo", woT, D, KT, lambda kk, tb: Bt(kk, tb * 512, (tb + 1) * 512),
                  evac_wo, "vector", (s_dve, SCAN_DVE), pre_waits=((s_dve, SCAN_DVE),))
        wo_ps.close()
        STAGE1_EV = s_ev.val

        if stages >= 2:
            LN2_MM0 = s_mm.val
            LN2_EV = emit_ln(1, TC, "l2", pre_wait=(s_ev, s_ev.val))
            LN2_MM = s_mm.val

            f0 = ExitStack()
            xq1 = f0.enter_context(nc.sbuf_tensor("xq1", [128, TC], F32))
            fq8 = f0.enter_context(nc.sbuf_tensor("fq8", [128, TC], FP8))
            dve_q = s_dve.val

            if F8MODE >= 1:
                @block.vector
                def _(v):
                    v.wait_ge(s_ev.sem, LN2_EV)
                    v.wait_ge(s_dve.sem, SCALES_DVE)
                    for i in range(KT):
                        v.tensor_tensor(xq1[:], Xt(i, 1, TC + 1), MU[:, 0:TC], op=OP.subtract)
                        v.tensor_tensor(xq1[:], xq1[:], RSTD[:, 0:TC], op=OP.mult)
                        v.tensor_scalar(xq1[:], xq1[:], dsc(0), None, op0=OP.mult)
                        v.tensor_copy(fq8[:], xq1[:])
                        s_dve.inc(v.tensor_copy(Bt(i), fq8[:]))
            QDQ_DVE = s_dve.val

            def evac_f8(e, psum, j, tb):
                xsl = Xt(j, 1 + tb * 512, 1 + (tb + 1) * 512)
                return e.scalar_tensor_tensor(xsl, psum, dsc(1), xsl, op0=OP.mult, op1=OP.add)

            if F8MODE >= 2:
                emit_proj("f8", w8T, D, KT, lambda kk, tb: Bt(kk, tb * 512, (tb + 1) * 512),
                          evac_f8, "vector", (s_dve, QDQ_DVE), wdt=FP8,
                          pre_waits=((s_mm, LN2_MM), (s_dve, QDQ_DVE)))
            f0.close()


        Q4 = {}

        def emit_quant_A():
            t4 = ExitStack()
            wsc = [t4.enter_context(nc.sbuf_tensor(f"wsc{p}", [128, 2048], F32)) for p in range(4)]
            pacc = t4.enter_context(nc.sbuf_tensor("pacc", [128, 512], F32))
            prsc = t4.enter_context(nc.sbuf_tensor("prsc", [128, 1], F32))
            pbk4 = t4.enter_context(nc.psum_tensor("pbk4", [128, 512], F32))
            NQC = 3 * NQS

            astA = s_ast.val
            dveA = s_dve.val
            st_gate = s_st.val

            @block.vector
            def _(v):
                v.wait_ge(s_ld.sem, SETUP_LD)
                v.wait_ge(s_st.sem, st_gate)
                s_dve.inc(v.memset(pacc[:], 0.0))

            @block.scalar
            def _(s):
                s.wait_ge(s_st.sem, st_gate)
                for c in range(NQC):
                    if c >= 4:
                        s.wait_ge(s_dve.sem, dveA + 1 + 2 * (c - 3))
                    s_ast.inc(s.dma_start(out=wsc[c % 4][:],
                                          in_=wqmy[c * 128:(c + 1) * 128, :]), 16)

            @block.vector
            def _(v):
                for c in range(NQC):
                    cw = c // NQS
                    v.wait_ge(s_ast.sem, astA + 16 * (c + 1))
                    s_dve.inc(v.tensor_reduce(prsc[:], wsc[c % 4][:], axis=AX.X,
                                              op=OP.add, apply_absolute_value=True))
                    s_dve.inc(v.tensor_tensor(pacc[:, cw:cw + 1], pacc[:, cw:cw + 1],
                                              prsc[:], op=OP.add))
            PACC_DVE = s_dve.val

            st4 = s_st.val

            @block.gpsimd
            def _(g):
                g.wait_ge(s_dve.sem, PACC_DVE)
                s_st.inc(g.dma_start(out=cin_sc.rearrange("(p c) -> p c", p=128), in_=pacc[:]), 16)
                g.wait_ge(s_st.sem, st4 + 16)
                s_cc.inc(g.collective_compute(
                    "AllGather", OP.bypass, replica_groups=[list(range(NCORE))],
                    ins=[cin_sc[:]], outs=[cout_sc[:]]))
            CC_SC = s_cc.val

            ld4b = s_ld.val
            dve4b = s_dve.val

            @block.sync
            def _(sync):
                sync.wait_ge(s_cc.sem, CC_SC)
                for r_ in range(NCORE):
                    if r_ >= 2:
                        sync.wait_ge(s_dve.sem, dve4b + (r_ - 1))
                    s_ld.inc(sync.dma_start(out=wsc[r_ % 2][:, 0:512],
                                            in_=cout_sc[r_, :].rearrange("(p c) -> p c", p=128)), 16)

            @block.vector
            def _(v):
                for r_ in range(NCORE):
                    v.wait_ge(s_ld.sem, ld4b + 16 * (r_ + 1))
                    if r_ == 0:
                        s_dve.inc(v.tensor_copy(pacc[:], wsc[0][:, 0:512]))
                    else:
                        s_dve.inc(v.tensor_tensor(pacc[:], pacc[:], wsc[r_ % 2][:, 0:512],
                                                  op=OP.add))
            SUM8_DVE = s_dve.val

            mm4 = s_mm.val
            act_gate = s_act.val

            @block.tensor
            def _(t):
                t.wait_ge(s_dve.sem, SUM8_DVE)
                t.wait_ge(s_act.sem, act_gate)
                s_mm.inc(t.matmul(pbk4[:], ONESF[:], pacc[:], start=True, stop=True))

            dve4c = s_dve.val

            @block.vector
            def _(v):
                v.wait_ge(s_mm.sem, mm4 + 1)
                s_dve.inc(v.tensor_scalar(DER[:, DSB + 2:DSB + 5], pbk4[:, 0:3], cc(C_INVCNT2),
                                          None, op0=OP.mult))
                v.wait_ge(s_dve.sem, dve4c + 1)
                s_dve.inc(v.tensor_scalar(DER[:, DSB + 5:DSB + 8], DER[:, DSB + 2:DSB + 5],
                                          cc(C_NEG1), None, op0=OP.mult))
            Q4["SCALE4_DVE"] = s_dve.val
            t4.close()

        def emit_quant_B():
            tq = ExitStack()
            wq = [tq.enter_context(nc.sbuf_tensor(f"wq{p}", [128, 2048], F32)) for p in range(2)]
            sgA = tq.enter_context(nc.sbuf_tensor("sgA", [128, 2048], F32))
            qbb = [tq.enter_context(nc.sbuf_tensor(f"qbb{p}", [128, 2048], BF16)) for p in range(2)]
            NQC = 3 * NQS
            astq = s_ast.val
            actq = s_act.val
            dveq = s_dve.val
            SC4D = Q4["SCALE4_DVE"]
            st_gate = s_st.val

            @block.scalar
            def _(s):
                s.wait_ge(s_dve.sem, SC4D)
                s.wait_ge(s_st.sem, st_gate)
                for c in range(NQC):
                    cw = c // NQS
                    s_ast.inc(s.dma_start(out=wq[c % 2][:],
                                          in_=wqmy[c * 128:(c + 1) * 128, :]), 16)
                    if c >= 1:
                        s.wait_ge(s_dve.sem, dveq + c)
                        s_ast.inc(s.dma_start(out=cin_q[(c - 1) * 128:c * 128, :],
                                              in_=qbb[(c - 1) % 2][:]), 16)
                    s.wait_ge(s_ast.sem, astq + 16 * (2 * c + 1))
                    s.activation(sgA[:], wq[c % 2][:], AF.Sign, bias=dsc(5 + cw))
                    s_act.inc(s.activation(qbb[c % 2][:], wq[c % 2][:], AF.Sign,
                                           bias=dsc(2 + cw)))
                s.wait_ge(s_dve.sem, dveq + NQC)
                s_ast.inc(s.dma_start(out=cin_q[(NQC - 1) * 128:NQC * 128, :],
                                      in_=qbb[(NQC - 1) % 2][:]), 16)

            @block.vector
            def _(v):
                for c in range(NQC):
                    v.wait_ge(s_act.sem, actq + c + 1)
                    s_dve.inc(v.tensor_tensor(qbb[c % 2][:], sgA[:], qbb[c % 2][:], op=OP.add))
            Q4["QNT_AST"] = s_ast.val
            tq.close()

        def emit_quant_B_ag():
            @block.gpsimd
            def _(g):
                g.wait_ge(s_ast.sem, Q4["QNT_AST"])
                s_cc.inc(g.collective_compute(
                    "AllGather", OP.bypass, replica_groups=[list(range(NCORE))],
                    ins=[cin_q[:]], outs=[cout_q[:]]))
            Q4["CC_Q"] = s_cc.val

        if stages >= 3:
            LN3_EV = emit_ln(1, TC, "l3", pre_wait=(s_ev, s_ev.val))
            LN3_MM = s_mm.val

            a3 = ExitStack()
            xs3 = a3.enter_context(nc.sbuf_tensor("xs3", [128, TC], F32))
            qst = [a3.enter_context(nc.sbuf_tensor(f"qst{p}", [128, TC], BF16)) for p in range(2)]
            dve_x3 = s_dve.val

            @block.vector
            def _(v):
                v.wait_ge(s_ev.sem, LN3_EV)
                for i in range(KT):
                    v.tensor_tensor(xs3[:], Xt(i, 1, TC + 1), MU[:, 0:TC], op=OP.subtract)
                    s_dve.inc(v.tensor_tensor(Bt(i), xs3[:], RSTD[:, 0:TC], op=OP.mult))
            XLN3_DVE = s_dve.val

            def evac_qkv(e, psum, j, tb):
                return e.activation(qst[j % 2][:, tb * 512:(tb + 1) * 512], psum, AF.Identity,
                                    bias=BQ[:, j:j + 1])

            emit_proj("qkv", wqkv, 3 * D, KT, lambda kk, tb: Bt(kk, tb * 512, (tb + 1) * 512),
                      evac_qkv, "scalar", (s_dve, XLN3_DVE),
                      store_fn=lambda g, j: g.dma_start(out=cin_kqv[j * 128:(j + 1) * 128, :],
                                                        in_=qst[j % 2][:]),
                      pre_waits=((s_mm, LN3_MM),))
            QKV_ST = s_st.val
            a3.close()

            if stages >= 4:
                if "A" in QHOIST:
                    emit_quant_A()
                if "B" in QHOIST:
                    emit_quant_B()

            cc_kb = s_cc.val

            @block.gpsimd
            def _(g):
                g.wait_ge(s_st.sem, QKV_ST)
                s_cc.inc(g.collective_compute(
                    "AllGather", OP.bypass, replica_groups=[list(range(NCORE))],
                    ins=[cin_kqv[:]], outs=[cout_kqv[:]]))
            CC_KQV = s_cc.val

            if stages >= 4 and "B" in QHOIST:
                emit_quant_B_ag()

            att = ExitStack()
            Ksb = att.enter_context(nc.sbuf_tensor("Ksb", [128, T], BF16))
            Vsb = att.enter_context(nc.sbuf_tensor("Vsb", [128, T], BF16))
            Qsb = att.enter_context(nc.sbuf_tensor("Qsb", [128, T], BF16))
            Psb = [att.enter_context(nc.sbuf_tensor(f"Psb{p}", [128, 512], BF16)) for p in range(4)]
            lacc = att.enter_context(nc.sbuf_tensor("lacc", [128, 512], F32))
            lrow = att.enter_context(nc.sbuf_tensor("lrow", [1, 512], F32))
            rbsb = att.enter_context(nc.sbuf_tensor("rbsb", [128, 512], F32))
            Ostg = [att.enter_context(nc.sbuf_tensor(f"Ostg{p}", [128, 512], BF16)) for p in range(2)]
            pS = [att.enter_context(nc.psum_tensor(f"pS{p}", [128, 512], F32)) for p in range(3)]
            pO = [att.enter_context(nc.psum_tensor(f"pO{p}", [128, 512], F32)) for p in range(2)]
            plr = att.enter_context(nc.psum_tensor("plr", [1, 512], F32))
            pbc = att.enter_context(nc.psum_tensor("pbc", [128, 512], F32))

            NQB = T // 512
            NST = T // 128
            DEPTH = 3

            for hh in range(HPC):
                ld_h = s_ld.val
                mm_h = s_mm.val
                act_h = s_act.val
                dve_h = s_dve.val
                st_h = s_st.val
                prev_mm = mm_h
                prev_dve = dve_h

                @block.sync
                def _(sync, hh=hh, prev_mm=prev_mm, prev_dve=prev_dve):
                    sync.wait_ge(s_cc.sem, CC_KQV)
                    if hh == 0 and stages >= 4 and "B" in QHOIST:
                        sync.wait_ge(s_ast.sem, Q4["QNT_AST"])
                    if hh > 0:
                        sync.wait_ge(s_mm.sem, prev_mm)
                        sync.wait_ge(s_dve.sem, prev_dve)
                    pid = sync.partition_id()
                    preg = sync.to_reg(pid)
                    for r in range(NCORE):
                        hg = HPC * r + hh
                        with sync.If_eq(preg, r):
                            for dst, base in ((Qsb, 0), (Ksb, D), (Vsb, 2 * D)):
                                srcv = cout_kqv[:, base + 128 * hg: base + 128 * (hg + 1), :]
                                srcv = srcv.rearrange("r d t -> d r t")
                                dstv = dst.ap().rearrange("p (r t) -> p r t", t=TC)
                                s_ld.inc(sync.dma_start(out=dstv, in_=srcv), 16)
                        if r != NCORE - 1:
                            s_ld.val -= 48
                LD_H = s_ld.val

                @block.tensor
                def _(t):
                    t.wait_ge(s_ld.sem, LD_H)
                    for st in range(NST):
                        if st >= 3:
                            t.wait_ge(s_dve.sem, dve_h + st - 2)
                        s_mm.inc(t.matmul(pS[st % 3][:, 0:128],
                                          Vsb[:, st * 128:(st + 1) * 128], IDN[:],
                                          start=True, stop=True))

                @block.vector
                def _(v):
                    for st in range(NST):
                        v.wait_ge(s_mm.sem, mm_h + st + 1)
                        s_dve.inc(v.tensor_copy(Vsb[:, st * 128:(st + 1) * 128],
                                                pS[st % 3][:, 0:128]))

                mmA = s_mm.val
                actA2 = s_act.val
                dveA2 = s_dve.val
                stA = s_st.val

                flatS = []
                for qb in range(NQB):
                    for st in range(4 * (qb + 1)):
                        flatS.append((qb, st))
                NG = len(flatS)

                ops = []
                tails = []
                for gi in range(NG + DEPTH):
                    if gi < NG:
                        ops.append(("S",) + flatS[gi])
                    j = gi - DEPTH
                    if j >= 0:
                        qb, st = flatS[j]
                        ops.append(("PV", qb, st))
                        if st == 4 * (qb + 1) - 1:
                            tails.append([gi + 1, ("LR", qb)])
                            tails.append([gi + 2, ("BC", qb)])
                    while tails and tails[0][0] <= gi:
                        ops.append(tails.pop(0)[1])
                for _, op in tails:
                    ops.append(op)

                mm_S = {}; mm_PV = {}; mm_lr = {}; mm_bc = {}
                act_exp = {}; act_rb = {}
                dve_msk = {}; dve_l = {}; dve_rc = {}; dve_om = {}; st_o = {}
                cmm = cact = cdve = cst = 0
                for op in ops:
                    if op[0] == "S":
                        cmm += 1; mm_S[op[1:]] = cmm
                        cact += 1; act_exp[op[1:]] = cact
                        qb, st = op[1:]
                        if st >= 4 * qb:
                            cdve += 1; dve_msk[op[1:]] = cdve
                        cdve += 1; dve_l[op[1:]] = cdve
                    elif op[0] == "PV":
                        cmm += 1; mm_PV[op[1:]] = cmm
                    elif op[0] == "LR":
                        cmm += 1; mm_lr[op[1]] = cmm
                        cdve += 1; dve_rc[op[1]] = cdve
                    else:
                        cmm += 1; mm_bc[op[1]] = cmm
                        cact += 1; act_rb[op[1]] = cact
                        cdve += 1; dve_om[op[1]] = cdve
                        cst += 16; st_o[op[1]] = cst
                gidx = {k: i for i, k in enumerate(flatS)}

                @block.tensor
                def _(t):
                    t.wait_ge(s_dve.sem, dveA2)
                    for op in ops:
                        if op[0] == "S":
                            qb, st = op[1:]
                            gi = gidx[(qb, st)]
                            if gi >= 3:
                                t.wait_ge(s_act.sem, actA2 + act_exp[flatS[gi - 3]])
                            s_mm.inc(t.matmul(pS[gi % 3][:], Ksb[:, st * 128:(st + 1) * 128],
                                              Qsb[:, qb * 512:(qb + 1) * 512],
                                              start=True, stop=True))
                        elif op[0] == "PV":
                            qb, st = op[1:]
                            gi = gidx[(qb, st)]
                            ns = 4 * (qb + 1)
                            if st == 0 and qb >= 2:
                                t.wait_ge(s_dve.sem, dveA2 + dve_om[qb - 2])
                            if st >= 4 * qb:
                                t.wait_ge(s_dve.sem, dveA2 + dve_msk[(qb, st)])
                            else:
                                t.wait_ge(s_act.sem, actA2 + act_exp[(qb, st)])
                            s_mm.inc(t.matmul(pO[qb % 2][:], Vsb[:, st * 128:(st + 1) * 128],
                                              Psb[gi % 4][:],
                                              start=(st == 0), stop=(st == ns - 1)))
                        elif op[0] == "LR":
                            qb = op[1]
                            t.wait_ge(s_dve.sem, dveA2 + dve_l[(qb, 4 * (qb + 1) - 1)])
                            if qb >= 1:
                                t.wait_ge(s_dve.sem, dveA2 + dve_rc[qb - 1])
                            s_mm.inc(t.matmul(plr[:], OCol[:], lacc[:], start=True, stop=True))
                        else:
                            qb = op[1]
                            t.wait_ge(s_dve.sem, dveA2 + dve_rc[qb])
                            if qb >= 1:
                                t.wait_ge(s_act.sem, actA2 + act_rb[qb - 1])
                            s_mm.inc(t.matmul(pbc[:], ORow[:], lrow[0:1, :], start=True, stop=True))

                @block.scalar
                def _(s):
                    for op in ops:
                        if op[0] == "S":
                            qb, st = op[1:]
                            gi = gidx[(qb, st)]
                            s.wait_ge(s_mm.sem, mmA + mm_S[(qb, st)])
                            if gi >= 4:
                                s.wait_ge(s_mm.sem, mmA + mm_PV[flatS[gi - 4]])
                                s.wait_ge(s_dve.sem, dveA2 + dve_l[flatS[gi - 4]])
                            s_act.inc(s.activation(Psb[gi % 4][:], pS[gi % 3][:], AF.Exp,
                                                   scale=cc(C_ATTSC), bias=cc(C_NSHIFT)))
                        elif op[0] == "BC":
                            qb = op[1]
                            s.wait_ge(s_mm.sem, mmA + mm_bc[qb])
                            if qb >= 1:
                                s.wait_ge(s_dve.sem, dveA2 + dve_om[qb - 1])
                            s_act.inc(s.activation(rbsb[:], pbc[:], AF.Identity))

                @block.vector
                def _(v):
                    for op in ops:
                        if op[0] == "S":
                            qb, st = op[1:]
                            gi = gidx[(qb, st)]
                            v.wait_ge(s_act.sem, actA2 + act_exp[(qb, st)])
                            rel = st - 4 * qb
                            if rel >= 0:
                                s_dve.inc(v.tensor_tensor(Psb[gi % 4][:], Psb[gi % 4][:],
                                                          MSK[:, rel * 512:(rel + 1) * 512],
                                                          op=OP.mult))
                            if st == 0:
                                if qb >= 1:
                                    v.wait_ge(s_mm.sem, mmA + mm_lr[qb - 1])
                                s_dve.inc(v.tensor_copy(lacc[:], Psb[gi % 4][:]))
                            else:
                                s_dve.inc(v.tensor_tensor(lacc[:], lacc[:], Psb[gi % 4][:],
                                                          op=OP.add))
                        elif op[0] == "LR":
                            qb = op[1]
                            v.wait_ge(s_mm.sem, mmA + mm_lr[qb])
                            s_dve.inc(v.reciprocal(lrow[0:1, :], plr[:]))
                        elif op[0] == "BC":
                            qb = op[1]
                            v.wait_ge(s_act.sem, actA2 + act_rb[qb])
                            if qb >= 2:
                                v.wait_ge(s_st.sem, stA + st_o[qb - 2])
                            s_dve.inc(v.tensor_tensor(Ostg[qb % 2][:], pO[qb % 2][:], rbsb[:],
                                                      op=OP.mult))

                @block.gpsimd
                def _(g, hh=hh):
                    for qb in range(NQB):
                        g.wait_ge(s_dve.sem, dveA2 + dve_om[qb])
                        s_st.inc(g.dma_start(
                            out=cin_o[hh * 128:(hh + 1) * 128, qb * 512:(qb + 1) * 512],
                            in_=Ostg[qb % 2][:]), 16)

            ATT_ST = s_st.val
            att.close()

            @block.gpsimd
            def _(g):
                g.wait_ge(s_st.sem, ATT_ST)
                s_cc.inc(g.collective_compute(
                    "AllGather", OP.bypass, replica_groups=[list(range(NCORE))],
                    ins=[cin_o[:]], outs=[cout_o[:]]))
            CC_O = s_cc.val

            @block.sync
            def _(sync):
                sync.wait_ge(s_cc.sem, CC_O)
                sync.wait_ge(s_mm.sem, s_mm.val)
                pid = sync.partition_id()
                preg = sync.to_reg(pid)
                for r in range(NCORE):
                    with sync.If_eq(preg, r):
                        for i in range(KT):
                            src = cout_o[i // HPC, (i % HPC) * 128:(i % HPC + 1) * 128,
                                         r * TC:(r + 1) * TC]
                            s_ld.inc(sync.dma_start(out=Bt(i), in_=src), 16)
                    if r != NCORE - 1:
                        s_ld.val -= 16 * KT
            OB_LD = s_ld.val

            def evac_wo1(e, psum, j, tb):
                xsl = Xt(j, 1 + tb * 512, 1 + (tb + 1) * 512)
                return e.scalar_tensor_tensor(xsl, psum, pcol("bo1", j), xsl,
                                              op0=OP.add, op1=OP.add)

            emit_proj("wo1", wo1, D, KT, lambda kk, tb: Bt(kk, tb * 512, (tb + 1) * 512),
                      evac_wo1, "vector", (s_ld, OB_LD), pre_waits=((s_ld, OB_LD),))


        if stages >= 4:
            P4 = _s4parts()
            if "C" in P4:
                LN4_EV = emit_ln(1, TC, "l4", upto=S4LN,
                                 pre_wait=[(s_ev, s_ev.val), (s_dve, Q4["SCALE4_DVE"])])

                f4 = ExitStack()
                xs4 = f4.enter_context(nc.sbuf_tensor("xs4", [128, TC], F32))
                dve_x4 = s_dve.val

                if "c" in P4:
                    @block.vector
                    def _(v):
                        v.wait_ge(s_ev.sem, LN4_EV)
                        v.wait_ge(s_mm.sem, s_mm.val)
                        for i in range(KT):
                            v.tensor_tensor(xs4[:], Xt(i, 1, TC + 1), MU[:, 0:TC], op=OP.subtract)
                            s_dve.inc(v.tensor_tensor(Bt(i), xs4[:], RSTD[:, 0:TC], op=OP.mult))
                XLN4_DVE = s_dve.val

            if "D" in P4:
                gsb = f4.enter_context(nc.sbuf_tensor("gsb", [128, JT * 512], BF16))
                q1s = [f4.enter_context(nc.sbuf_tensor(f"q1s{p}", [128, KT * 128], BF16)) for p in range(2)]
                q3s = [f4.enter_context(nc.sbuf_tensor(f"q3s{p}", [128, KT * 128], BF16)) for p in range(2)]
                silb = [f4.enter_context(nc.sbuf_tensor(f"silb{p}", [128, 512], F32)) for p in range(2)]
                q2s = [f4.enter_context(nc.sbuf_tensor(f"q2s{p}", [128, 1024], BF16)) for p in range(2)]

                W2MM = {"v": s_mm.val}
                W2DVE = {"v": s_dve.val}
                for half in range(2):
                    hp = ExitStack()
                    ph1 = [hp.enter_context(nc.psum_tensor(f"ph1{half}{p}", [128, 512], F32)) for p in range(2)]
                    ph3 = [hp.enter_context(nc.psum_tensor(f"ph3{half}{p}", [128, 512], F32)) for p in range(2)]

                    ldh = s_ld.val
                    mmh = s_mm.val
                    acth = s_act.val
                    dveh = s_dve.val
                    w2prev = W2MM["v"]
                    w2dveprev = W2DVE["v"]

                    @block.sync
                    def _(sync, half=half, w2prev=w2prev):
                        sync.wait_ge(s_cc.sem, Q4["CC_Q"])
                        sync.wait_ge(s_dve.sem, XLN4_DVE)
                        if half > 0:
                            sync.wait_ge(s_mm.sem, w2prev)
                        for j in range(JT):
                            if j >= 1:
                                sync.wait_ge(s_mm.sem, mmh + 2 * j - 2 if j >= 2 else mmh)
                            s_ld.inc(sync.dma_start(
                                out=q1s[j % 2][:],
                                in_=cout_q[j % NCORE,
                                           (0 * NQS + j // NCORE) * 128:
                                           (0 * NQS + j // NCORE + 1) * 128, :]), 16)
                            s_ld.inc(sync.dma_start(
                                out=q3s[j % 2][:],
                                in_=cout_q[j % NCORE,
                                           (1 * NQS + j // NCORE) * 128:
                                           (1 * NQS + j // NCORE + 1) * 128, :]), 16)

                    @block.tensor
                    def _(t, half=half, w2prev=w2prev, w2dveprev=w2dveprev):
                        t.wait_ge(s_dve.sem, XLN4_DVE)
                        if half > 0:
                            t.wait_ge(s_mm.sem, w2prev)
                            t.wait_ge(s_dve.sem, w2dveprev)
                        for j in range(JT):
                            t.wait_ge(s_ld.sem, ldh + 32 * (j + 1))
                            if j >= 2:
                                t.wait_ge(s_dve.sem, dveh + (j - 1))
                            inst = None
                            for kk in range(KT):
                                inst = t.matmul(ph1[j % 2][:], q1s[j % 2][:, kk * 128:(kk + 1) * 128],
                                                Bt(kk, half * 512, (half + 1) * 512),
                                                start=(kk == 0), stop=(kk == KT - 1))
                            s_mm.inc(inst)
                            for kk in range(KT):
                                inst = t.matmul(ph3[j % 2][:], q3s[j % 2][:, kk * 128:(kk + 1) * 128],
                                                Bt(kk, half * 512, (half + 1) * 512),
                                                start=(kk == 0), stop=(kk == KT - 1))
                            s_mm.inc(inst)

                    @block.scalar
                    def _(s):
                        for j in range(JT):
                            s.wait_ge(s_mm.sem, mmh + 2 * j + 1)
                            if j >= 2:
                                s.wait_ge(s_dve.sem, dveh + (j - 1))
                            s_act.inc(s.activation(silb[j % 2][:], ph1[j % 2][:], AF.Silu,
                                                   scale=dsc(2)))

                    @block.vector
                    def _(v, half=half):
                        for j in range(JT):
                            v.wait_ge(s_mm.sem, mmh + 2 * j + 2)
                            v.wait_ge(s_act.sem, acth + j + 1)
                            s_dve.inc(v.scalar_tensor_tensor(gsb[:, j * 512:(j + 1) * 512],
                                                             ph3[j % 2][:], dsc(3), silb[j % 2][:],
                                                             op0=OP.mult, op1=OP.mult))
                    G_DVE = s_dve.val
                    hp.close()

                    w2p = ExitStack()
                    pw2 = [w2p.enter_context(nc.psum_tensor(f"pw2{half}{b}", [128, 512], F32))
                           for b in range(8)]
                    for grp in range(2):
                        ldw2 = s_ld.val
                        mmw2 = s_mm.val
                        dvew2 = s_dve.val

                        @block.sync
                        def _(sync, grp=grp, half=half):
                            sync.wait_ge(s_dve.sem, G_DVE)
                            for j in range(JT):
                                if j >= 2:
                                    sync.wait_ge(s_mm.sem, mmw2 + (j - 1))
                                src = cout_q[j % NCORE,
                                             (2 * NQS + j // NCORE) * 128:
                                             (2 * NQS + j // NCORE + 1) * 128,
                                             grp * 1024:(grp + 1) * 1024]
                                s_ld.inc(sync.dma_start(out=q2s[j % 2][:], in_=src), 16)

                        @block.tensor
                        def _(t, grp=grp, half=half):
                            t.wait_ge(s_dve.sem, G_DVE)
                            if grp > 0:
                                t.wait_ge(s_dve.sem, dvew2)
                            for j in range(JT):
                                t.wait_ge(s_ld.sem, ldw2 + 16 * (j + 1))
                                inst = None
                                for dt_ in range(8):
                                    inst = t.matmul(pw2[dt_][:],
                                                    q2s[j % 2][:, dt_ * 128:(dt_ + 1) * 128],
                                                    gsb[:, j * 512:(j + 1) * 512],
                                                    start=(j == 0), stop=(j == JT - 1))
                                s_mm.inc(inst)
                        MM_W2 = s_mm.val

                        @block.vector
                        def _(v, grp=grp, half=half):
                            v.wait_ge(s_mm.sem, MM_W2)
                            for dt_ in range(8):
                                i_gl = grp * 8 + dt_
                                xsl = Xt(i_gl, 1 + half * 512, 1 + (half + 1) * 512)
                                s_dve.inc(v.scalar_tensor_tensor(xsl, pw2[dt_][:], dsc(4), xsl,
                                                                 op0=OP.mult, op1=OP.add))
                    W2MM["v"] = s_mm.val
                    W2DVE["v"] = s_dve.val
                    w2p.close()
                f4.close()
            if "C" in P4 and "D" not in P4:
                f4.close()

        if NPAD:
            eng = {"vector": block.vector, "tensor": block.tensor,
                   "scalar": block.scalar, "sync": block.sync}[NPAD_ENG]

            @eng
            def _(e):
                for _ in range(NPAD):
                    e.wait_ge(s_ld.sem, 0)

        if DBG:
            dmu = nc.dram_tensor("dmu", [128, TC + 1], F32, kind="ExternalOutput").ap()
            drstd = nc.dram_tensor("drstd", [128, TC + 1], F32, kind="ExternalOutput").ap()
            dcarr = nc.dram_tensor("dcarr", [128, 2 * KT], F32, kind="ExternalOutput").ap()
            dtots = nc.dram_tensor("dtots", [128, 2 * KT], F32, kind="ExternalOutput").ap()
            dbb = nc.dram_tensor("dbb", [128, KT * TC], BF16, kind="ExternalOutput").ap()
            dder = nc.dram_tensor("dder", [128, 3 * KT + 8], F32, kind="ExternalOutput").ap()
            dfct = nc.dram_tensor("dfct", [128, 8 * KT], F32, kind="ExternalOutput").ap()
            dcout = nc.dram_tensor("dcout", [NCORE, 2 * KT * 128], F32, kind="ExternalOutput").ap()

        @block.gpsimd
        def _(g):
            g.wait_ge(s_ev.sem, s_ev.val)
            g.wait_ge(s_dve.sem, s_dve.val)
            for i in range(KT):
                s_st.inc(g.dma_start(out=yout[i * 128:(i + 1) * 128, :], in_=Xt(i, 1, TC + 1)), 16)
            if DBG:
                s_st.inc(g.dma_start(out=dmu[:], in_=MU[:]), 16)
                s_st.inc(g.dma_start(out=drstd[:], in_=RSTD[:]), 16)
                s_st.inc(g.dma_start(out=dcarr[:], in_=CARR[:]), 16)
                s_st.inc(g.dma_start(out=dtots[:], in_=TOTS[:]), 16)
                s_st.inc(g.dma_start(out=dbb[:], in_=BB[:]), 16)
                s_st.inc(g.dma_start(out=dder[:], in_=DER[:]), 16)
                s_st.inc(g.dma_start(out=dfct[:], in_=FCT[:]), 16)
                s_st.inc(g.dma_start(out=dcout[:], in_=cout_tot[:]), 16)
            g.wait_ge(s_st.sem, s_st.val)

    return nc


def _tp(a):
    return np.ascontiguousarray(np.asarray(a).T)


def _wtile(w):
    w = np.asarray(w)
    K, M = w.shape
    nk, nj = K // 128, M // 128
    return np.ascontiguousarray(
        w.reshape(nk, 128, nj, 128).transpose(2, 1, 0, 3).reshape(nj * 128, nk * 128))


def _colpack(vec):
    return np.ascontiguousarray(np.asarray(vec).reshape(KT, 128).T)


def kernel(**inputs):
    x = np.asarray(inputs["L0_attn_x"])[0]
    stages = N_STAGES

    pc = np.zeros((128, NPAR * KT), np.float32)
    pars = {
        "tmk": np.asarray(inputs["L0_attn_tmk"]).reshape(D),
        "tmv": np.asarray(inputs["L0_attn_tmv"]).reshape(D),
        "tmr": np.asarray(inputs["L0_attn_tmr"]).reshape(D),
        "bk": np.asarray(inputs["L0_attn_bk"]),
        "bv": np.asarray(inputs["L0_attn_bv"]),
        "br": np.asarray(inputs["L0_attn_br"]),
        "bo": np.asarray(inputs["L0_attn_bo"]),
        "td": np.asarray(inputs["L0_attn_time_decay"]),
        "tf": np.asarray(inputs["L0_attn_time_first"]),
        "bo1": np.asarray(inputs["L1_attn_bo"]),
    }
    for n, v in pars.items():
        pc[:, PAR[n] * KT:(PAR[n] + 1) * KT] = _colpack(v.astype(np.float32))

    bq = np.ascontiguousarray(np.asarray(inputs["L1_attn_bqkv"]).astype(np.float32)
                              .reshape(48, 128).T)

    consts = np.zeros((128, NCONST), np.float32)
    consts[:, C_NEG1] = -1.0
    consts[:, C_INVD] = INV_D
    consts[:, C_EPS] = EPS
    consts[:, C_ATTSC] = ATT_SCALE
    consts[:, C_NSHIFT] = -ATT_SHIFT
    consts[:, C_INVCNT2] = 0.5 / (JH * D)
    consts[:, C_ONE] = 1.0

    ws = np.float32(np.asarray(inputs["L0_ffn_weight_scale"]))
    sin = np.float32(np.asarray(inputs["L0_ffn_input_scale"]).astype(np.float32))
    scl = np.tile(np.array([[ws, sin]], np.float32), (128, 1))

    revramp = np.tile((np.float32(TC - 1) - np.arange(TC, dtype=np.float32))[None, :], (128, 1))

    m = np.zeros((128, 4 * 512), BF)
    s_i = np.arange(128)[:, None]
    q_i = np.arange(512)[None, :]
    for r in range(4):
        m[:, r * 512:(r + 1) * 512] = (s_i <= q_i - 128 * r).astype(BF)

    ident = np.eye(128, dtype=BF)
    idnf = np.eye(128, dtype=np.float32)
    idn8 = np.eye(8, dtype=np.float32)
    onescol = np.ones((128, 1), np.float32)
    onesrow = np.ones((1, 128), np.float32)
    onesf = np.ones((128, 128), np.float32)

    wkT = _wtile(_tp(inputs["L0_attn_Wk"]))
    wvT = _wtile(_tp(inputs["L0_attn_Wv"]))
    wrT = _wtile(_tp(inputs["L0_attn_Wr"]))
    woT = _wtile(_tp(inputs["L0_attn_Wo"]))
    w8T = _wtile(np.ascontiguousarray(np.asarray(inputs["L0_ffn_weight"]).view(F8).T))
    wqkv = _wtile(np.asarray(inputs["L1_attn_Wqkv"]).astype(np.float32))
    wo1 = _wtile(np.asarray(inputs["L1_attn_Wo"]).astype(np.float32))
    w1 = np.asarray(inputs["L1_ffn_w1"])
    w3 = np.asarray(inputs["L1_ffn_w3"])
    w2 = np.asarray(inputs["L1_ffn_w2"])
    w1T = np.zeros((D, JP), np.float32); w1T[:, 0:JH] = w1.T
    w3T = np.zeros((D, JP), np.float32); w3T[:, 0:JH] = w3.T
    w1T = _wtile(w1T)
    w3T = _wtile(w3T)
    w2T = np.zeros((JP, D), np.float32); w2T[0:JH, :] = w2.T
    wqmy_r = []
    for r in range(NCORE):
        qm = np.zeros((3 * NQS * 128, D), np.float32)
        for cw, wt in enumerate((w1T, w3T, w2T)):
            for s_i, j in enumerate(range(r, JT, NCORE)):
                qm[(cw * NQS + s_i) * 128:(cw * NQS + s_i + 1) * 128] = \
                    wt[j * 128:(j + 1) * 128]
        wqmy_r.append(qm)

    shared = dict(wkT=wkT, wvT=wvT, wrT=wrT, woT=woT, w8T=w8T, wqkv=wqkv, wo1=wo1,
                  pcols=pc, bqcols=bq, consts=consts,
                  scl=scl, revramp=revramp, masks=m, ident=ident, idn8=idn8, idnf=idnf, onesf=onesf,
                  onescol=onescol, onesrow=onesrow)

    in_maps = []
    for r in range(NCORE):
        xs = np.zeros((TC + 1, D), np.float32)
        if r > 0:
            xs[0] = x[r * TC - 1]
        xs[1:] = x[r * TC:(r + 1) * TC]
        e8 = np.zeros((128, 8), np.float32)
        for rp in range(8):
            e8[:, rp] = 1024.0 * (r - 1 - rp) if rp < r else 1e9
        in_maps.append(dict(shared, xT=_tp(xs), e1024=e8, wqmy=wqmy_r[r]))

    nc = build_nc(stages)
    res = bass_utils.run_bass_kernel_spmd(nc, in_maps, core_ids=list(range(NCORE)))
    if DBG:
        import pickle
        with open("/tmp/dbg_results.pkl", "wb") as f:
            pickle.dump([{k: np.asarray(v) for k, v in r.items()} for r in res.results], f)
    out = np.concatenate([np.asarray(r["y"]).T for r in res.results], axis=0)
    return out.reshape(1, T, D).astype(np.float32)


# version 08
# speedup vs baseline: 1.6094x; 1.0554x over previous
import numpy as np
import ml_dtypes
from contextlib import ExitStack

import concourse.bass as bass
import concourse.mybir as mybir
from concourse import bass_utils

F32 = mybir.dt.float32
BF16 = mybir.dt.bfloat16
FP8 = mybir.dt.float8e4
AF = mybir.ActivationFunctionType
OP = mybir.AluOpType
AX = mybir.AxisListType

NCORE = 8
D = 2048
T = 8192
TC = T // NCORE
KT = D // 128
NH, HD, HPC = 16, 128, 2
JH = 5461
JP = 5504
JT = JP // 128
NQS = (JT + NCORE - 1) // NCORE
INV_D = 1.0 / D
EPS = 1e-5
ATT_SCALE = float(1.0 / np.sqrt(128.0))
ATT_SHIFT = 8.0

N_STAGES = 4
S4 = 4
S4PARTS = None
S4LN = 4
S4A = 4


def _s4parts():
    if S4PARTS is not None:
        return S4PARTS
    p = ""
    if S4 >= 1: p += "A"
    if S4 >= 2: p += "B"
    if S4 >= 3: p += "C"
    if S4 >= 3.5: p += "c"
    if S4 >= 4: p += "D"
    return p
NPAD = 0
QHOIST = "AB"
NPAD_ENG = "vector"
DBG = 0
F8MODE = 2

BF = ml_dtypes.bfloat16
F8 = ml_dtypes.float8_e4m3

C_NEG1, C_INVD, C_EPS, C_ATTSC, C_NSHIFT, C_INVCNT2, C_ONE = range(7)
NCONST = 7
PAR = {n: i for i, n in enumerate(
    ["tmk", "tmv", "tmr", "bk", "bv", "br", "bo", "td", "tf", "bo1"])}
NPAR = len(PAR)


class SC:
    def __init__(self, nc, es, name):
        self.sem = es.enter_context(nc.semaphore(name))
        self.val = 0

    def inc(self, inst, n=1):
        inst.then_inc(self.sem, n)
        self.val += n
        return inst


def build_nc(stages):
    nc = bass.Bass("TRN2", target_bir_lowering=False, debug=False, num_devices=NCORE,
                   detect_race_conditions=False)
    es = ExitStack()

    def din(name, shape, dt):
        return nc.dram_tensor(name, shape, dt, kind="ExternalInput").ap()

    def dint(name, shape, dt):
        return nc.dram_tensor(name, shape, dt, kind="Internal").ap()

    def dsh(name, shape, dt):
        return nc.dram_tensor(name, shape, dt, kind="Internal", addr_space="Shared").ap()

    xT = din("xT", [D, TC + 1], F32)
    wkT = din("wkT", [D, D], F32)
    wvT = din("wvT", [D, D], F32)
    wrT = din("wrT", [D, D], F32)
    woT = din("woT", [D, D], F32)
    w8T = din("w8T", [D, D], FP8)
    wqkv = din("wqkv", [3 * D, D], F32)
    wo1 = din("wo1", [D, D], F32)
    wqmy = din("wqmy", [3 * NQS * 128, D], F32)
    pcolsI = din("pcols", [128, NPAR * KT], F32)
    bqcolsI = din("bqcols", [128, 48], F32)
    constsI = din("consts", [128, NCONST], F32)
    sclI = din("scl", [128, 2], F32)
    e1024I = din("e1024", [128, 8], F32)
    revrampI = din("revramp", [128, TC], F32)
    masksI = din("masks", [128, 4 * 512], BF16)
    identI = din("ident", [128, 128], BF16)
    idn8I = din("idn8", [8, 8], F32)
    idnfI = din("idnf", [128, 128], F32)
    onescolI = din("onescol", [128, 1], F32)
    onesrowI = din("onesrow", [1, 128], F32)
    onesfI = din("onesf", [128, 128], F32)

    yout = nc.dram_tensor("y", [D, TC], F32, kind="ExternalOutput").ap()

    dkind = "ExternalOutput" if DBG else "Internal"
    k_hbm = nc.dram_tensor("k_hbm", [D, TC], F32, kind=dkind).ap()
    v_hbm = nc.dram_tensor("v_hbm", [D, TC], BF16, kind=dkind).ap()
    sr_hbm = nc.dram_tensor("sr_hbm", [D, TC], BF16, kind=dkind).ap()
    cin_tot = dint("cin_tot", [2 * KT * 128], F32)
    cout_tot = dsh("cout_tot", [NCORE, 2 * KT * 128], F32)
    cin_sc = dint("cin_sc", [128 * 512], F32)
    cout_sc = dsh("cout_sc", [NCORE, 128 * 512], F32)
    cin_kqv = dint("cin_kqv", [3 * D, TC], BF16)
    cout_kqv = dsh("cout_kqv", [NCORE, 3 * D, TC], BF16)
    cin_o = dint("cin_o", [HPC * HD, T], BF16)
    cout_o = dsh("cout_o", [NCORE, HPC * HD, T], BF16)
    q_loc = dint("q_loc", [HPC * HD, T], BF16)
    o_loc = dint("o_loc", [D, TC], BF16)
    cin_q = dint("cin_q", [3 * NQS * 128, D], BF16)
    cout_q = dsh("cout_q", [NCORE, 3 * NQS * 128, D], BF16)
    cin_f = dint("cin_f", [8], F32)
    cout_f = dsh("cout_f", [NCORE, 8], F32)
    cin_s2 = dint("cin_s2", [512], F32)
    cout_s2 = dsh("cout_s2", [NCORE, 512], F32)

    with es:
        X = es.enter_context(nc.sbuf_tensor("X", [128, KT * (TC + 1)], F32))
        BB = es.enter_context(nc.sbuf_tensor("BB", [128, KT * TC], BF16))
        PC = es.enter_context(nc.sbuf_tensor("PC", [128, NPAR * KT], F32))
        BQ = es.enter_context(nc.sbuf_tensor("BQ", [128, 48], F32))
        CST = es.enter_context(nc.sbuf_tensor("CST", [128, NCONST], F32))
        SCLT = es.enter_context(nc.sbuf_tensor("SCLT", [128, 2], F32))
        E8 = es.enter_context(nc.sbuf_tensor("E8", [128, 8], F32))
        RR = es.enter_context(nc.sbuf_tensor("RR", [128, TC], F32))
        MSK = es.enter_context(nc.sbuf_tensor("MSK", [128, 4 * 512], BF16))
        IDN = es.enter_context(nc.sbuf_tensor("IDN", [128, 128], BF16))
        ID8 = es.enter_context(nc.sbuf_tensor("ID8", [8, 8], F32))
        IDF = es.enter_context(nc.sbuf_tensor("IDF", [128, 128], F32))
        OCol = es.enter_context(nc.sbuf_tensor("OCol", [128, 1], F32))
        ORow = es.enter_context(nc.sbuf_tensor("ORow", [1, 128], F32))
        ONESF = es.enter_context(nc.sbuf_tensor("ONESF", [128, 128], F32))
        DER = es.enter_context(nc.sbuf_tensor("DER", [128, 3 * KT + 8], F32))
        FCT = es.enter_context(nc.sbuf_tensor("FCT", [128, 8 * KT], F32))
        MU = es.enter_context(nc.sbuf_tensor("MU", [128, TC + 1], F32))
        RSTD = es.enter_context(nc.sbuf_tensor("RSTD", [128, TC + 1], F32))
        ST1 = es.enter_context(nc.sbuf_tensor("ST1", [1, TC + 1], F32))
        ST2 = es.enter_context(nc.sbuf_tensor("ST2", [1, TC + 1], F32))
        SROW = es.enter_context(nc.sbuf_tensor("SROW", [1, TC + 1], F32))
        CARR = es.enter_context(nc.sbuf_tensor("CARR", [128, 2 * KT], F32))
        TOTS = es.enter_context(nc.sbuf_tensor("TOTS", [128, 2 * KT], F32))
        SC4 = es.enter_context(nc.sbuf_tensor("SC4", [8, 4], F32))
        SCD = es.enter_context(nc.sbuf_tensor("SCD", [1, 8], F32))
        TOTT = es.enter_context(nc.sbuf_tensor("TOTT", [32, 128], F32))

        def Xt(i, lo=0, hi=TC + 1):
            return X[:, i * (TC + 1) + lo: i * (TC + 1) + hi]

        def Bt(i, lo=0, hi=TC):
            return BB[:, i * TC + lo: i * TC + hi]

        def pcol(name, i):
            j = PAR[name] * KT + i
            return PC[:, j:j + 1]

        def dw(i):  return DER[:, i:i + 1]
        def dew(i): return DER[:, KT + i:KT + i + 1]
        def deb(i): return DER[:, 2 * KT + i:2 * KT + i + 1]
        DSB = 3 * KT
        def dsc(j): return DER[:, DSB + j:DSB + j + 1]
        def cc(j):  return CST[:, j:j + 1]
        def cc1(j): return CST[0:1, j:j + 1]

        s_ld = SC(nc, es, "s_ld")
        s_st = SC(nc, es, "s_st")
        s_gld = SC(nc, es, "s_gld")
        s_ast = SC(nc, es, "s_ast")
        s_act = SC(nc, es, "s_act")
        s_dve = SC(nc, es, "s_dve")
        s_wld = SC(nc, es, "s_wld")
        s_wc = SC(nc, es, "s_wc")
        s_mm = SC(nc, es, "s_mm")
        s_ev = SC(nc, es, "s_ev")
        s_cc = SC(nc, es, "s_cc")

        block = es.enter_context(nc.Block())

        @block.sync
        def _(sync):
            for i in range(KT):
                s_ld.inc(sync.dma_start(out=Xt(i), in_=xT[i * 128:(i + 1) * 128, :]), 16)
            for dst, src in [(PC, pcolsI), (BQ, bqcolsI), (CST, constsI), (SCLT, sclI),
                             (E8, e1024I), (RR, revrampI), (MSK, masksI), (IDN, identI),
                             (ID8, idn8I), (IDF, idnfI), (OCol, onescolI), (ORow, onesrowI),
                             (ONESF, onesfI)]:
                s_ld.inc(sync.dma_start(out=dst[:], in_=src[:]), 16)
        SETUP_LD = s_ld.val

        a0 = s_act.val

        @block.scalar
        def _(s):
            s.wait_ge(s_ld.sem, SETUP_LD)
            s_act.inc(s.activation(DER[:, 0:KT], PC[:, PAR["td"] * KT:(PAR["td"] + 1) * KT], AF.Exp))

        d0 = s_dve.val

        @block.vector
        def _(v):
            v.wait_ge(s_act.sem, a0 + 1)
            s_dve.inc(v.tensor_scalar(DER[:, 0:KT], DER[:, 0:KT], cc(C_NEG1), None, op0=OP.mult))
            v.wait_ge(s_dve.sem, d0 + 1)
            for i in range(KT - 1):
                v.tensor_scalar(FCT[:, 8 * i:8 * i + 8], E8[:], dw(i), None, op0=OP.mult)
            s_dve.inc(v.tensor_scalar(FCT[:, 8 * (KT - 1):8 * KT], E8[:], dw(KT - 1), None, op0=OP.mult))

        @block.scalar
        def _(s):
            s.wait_ge(s_dve.sem, d0 + 2)
            s.activation(DER[:, KT:2 * KT], DER[:, 0:KT], AF.Exp)
            s.activation(FCT[:], FCT[:], AF.Exp)
            s_act.inc(s.activation(DER[:, 2 * KT:3 * KT], PC[:, PAR["tf"] * KT:(PAR["tf"] + 1) * KT], AF.Exp))
        DER_READY = s_act.val

        dve_scb = s_dve.val

        @block.vector
        def _(v):
            v.wait_ge(s_ld.sem, SETUP_LD)
            v.reciprocal(DER[:, DSB:DSB + 1], SCLT[:, 1:2])
            s_dve.inc(v.tensor_tensor(DER[:, DSB + 1:DSB + 2], SCLT[:, 0:1], SCLT[:, 1:2],
                                      op=OP.mult))
        SCALES_DVE = s_dve.val

        def emit_ln(off, ncols, tag, pre_wait=None, upto=4):
            if pre_wait is not None and not isinstance(pre_wait, list):
                pre_wait = [pre_wait]
            ps = ExitStack()
            chunks = []
            c0 = 0
            while c0 < ncols:
                chunks.append((c0, min(512, ncols - c0)))
                c0 += 512
            nch = len(chunks)
            psx = [ps.enter_context(nc.psum_tensor(f"psx{tag}{c}", [1, 512], F32)) for c in range(nch)]
            psq = [ps.enter_context(nc.psum_tensor(f"psq{tag}{c}", [1, 512], F32)) for c in range(nch)]
            sqb = [ps.enter_context(nc.sbuf_tensor(f"sq{tag}{c}", [128, ncols], F32)) for c in range(2)]

            act_b = s_act.val
            mm_b = s_mm.val

            @block.scalar
            def _(s):
                s.wait_ge(s_ld.sem, SETUP_LD)
                if pre_wait is not None:
                    for pw_s, pw_v in pre_wait:
                        s.wait_ge(pw_s.sem, pw_v)
                for i in range(KT):
                    if i >= 2:
                        s.wait_ge(s_mm.sem, mm_b + (i - 1))
                    s_act.inc(s.activation(sqb[i % 2][:], Xt(i, off, off + ncols), AF.Square))

            @block.tensor
            def _(t):
                t.wait_ge(s_ld.sem, SETUP_LD)
                if pre_wait is not None:
                    for pw_s, pw_v in pre_wait:
                        t.wait_ge(pw_s.sem, pw_v)
                for i in range(KT):
                    for ci, (o, w) in enumerate(chunks):
                        t.matmul(psx[ci][:, 0:w], OCol[:], Xt(i, off + o, off + o + w),
                                 start=(i == 0), stop=(i == KT - 1))
                    t.wait_ge(s_act.sem, act_b + i + 1)
                    inst = None
                    for ci, (o, w) in enumerate(chunks):
                        inst = t.matmul(psq[ci][:, 0:w], OCol[:], sqb[i % 2][:, o:o + w],
                                        start=(i == 0), stop=(i == KT - 1))
                    s_mm.inc(inst)
            MM_STATS = s_mm.val

            dve_lnb = s_dve.val

            if upto < 2:
                ps.close()
                return s_ev.val

            @block.vector
            def _(v):
                v.wait_ge(s_mm.sem, MM_STATS)
                for ci, (o, w) in enumerate(chunks):
                    v.tensor_scalar(ST1[0:1, o:o + w], psx[ci][:, 0:w], cc1(C_INVD), None, op0=OP.mult)
                    v.tensor_scalar(ST2[0:1, o:o + w], psq[ci][:, 0:w], cc1(C_INVD), None, op0=OP.mult)
                s_dve.inc(v.engine_nop())
                v.wait_ge(s_dve.sem, dve_lnb + 1)
                s_dve.inc(v.tensor_tensor(SROW[0:1, 0:ncols], ST1[0:1, 0:ncols], ST1[0:1, 0:ncols],
                                          op=OP.mult))
                v.wait_ge(s_dve.sem, dve_lnb + 2)
                s_dve.inc(v.tensor_tensor(ST2[0:1, 0:ncols], ST2[0:1, 0:ncols], SROW[0:1, 0:ncols],
                                          op=OP.subtract))
            DVE_VAR = s_dve.val

            act_b2 = s_act.val

            if upto < 3:
                ps.close()
                return s_ev.val

            @block.scalar
            def _(s):
                s.wait_ge(s_dve.sem, DVE_VAR)
                s_act.inc(s.activation(SROW[0:1, 0:ncols], ST2[0:1, 0:ncols], AF.Sqrt, bias=cc1(C_EPS)))

            @block.vector
            def _(v):
                v.wait_ge(s_act.sem, act_b2 + 1)
                s_dve.inc(v.reciprocal(SROW[0:1, 0:ncols], SROW[0:1, 0:ncols]))
            DVE_RSTD = s_dve.val

            ps.close()
            if upto < 4:
                return s_ev.val
            pb = ExitStack()
            psbc = [pb.enter_context(nc.psum_tensor(f"psb{tag}{c}", [128, 512], F32))
                    for c in range(2 * nch)]
            mm_b2 = s_mm.val

            @block.tensor
            def _(t):
                t.wait_ge(s_dve.sem, DVE_RSTD)
                inst = None
                for ci, (o, w) in enumerate(chunks):
                    t.matmul(psbc[2 * ci][:, 0:w], ORow[:], ST1[0:1, o:o + w], start=True, stop=True)
                    inst = t.matmul(psbc[2 * ci + 1][:, 0:w], ORow[:], SROW[0:1, o:o + w],
                                    start=True, stop=True)
                s_mm.inc(inst)
            MM_B = s_mm.val

            @block.vector
            def _(v):
                v.wait_ge(s_mm.sem, MM_B)
                inst = None
                for ci, (o, w) in enumerate(chunks):
                    v.tensor_copy(MU[:, o:o + w], psbc[2 * ci][:, 0:w])
                    inst = v.tensor_copy(RSTD[:, o:o + w], psbc[2 * ci + 1][:, 0:w])
                s_ev.inc(inst)
            pb.close()
            return s_ev.val

        def emit_proj(tag, wdram, ncols, nk, rhs_fn, evac_fn, evac_eng, rhs_wait,
                      store_fn=None, wdt=F32, pre_waits=()):
            ps = ExitStack()
            nj = ncols // 128
            ntb = 2
            wf = ps.enter_context(nc.sbuf_tensor(f"wf{tag}", [128, nk * 128], wdt))
            wb = [ps.enter_context(nc.sbuf_tensor(f"wb{tag}{p}", [128, nk * 128], BF16))
                  for p in range(2)]
            NB = 4
            pmm = [ps.enter_context(nc.psum_tensor(f"pp{tag}{b}", [128, 512], F32)) for b in range(NB)]

            wld_b = s_wld.val
            wc_b = s_wc.val
            mm_b = s_mm.val
            ev_b = s_ev.val
            st_b = s_st.val

            @block.sync
            def _(sync):
                for sc_, v_ in pre_waits:
                    sync.wait_ge(sc_.sem, v_)
                for j in range(nj):
                    sync.wait_ge(s_wc.sem, wc_b + j)
                    s_wld.inc(sync.dma_start(out=wf[:], in_=wdram[j * 128:(j + 1) * 128, :]), 16)

            def _emit_evac_tile(e, j):
                for tb in range(ntb):
                    g = ntb * j + tb
                    e.wait_ge(s_mm.sem, mm_b + g + 1)
                    if store_fn is not None and tb == 0:
                        e.wait_ge(s_st.sem, st_b + 16 * (j - 1) if j >= 2 else st_b)
                    inst = evac_fn(e, pmm[g % NB][:], j, tb)
                    s_ev.inc(inst)

            if evac_eng == "scalar":
                @block.scalar
                def _(s):
                    for j in range(nj):
                        s.wait_ge(s_wld.sem, wld_b + 16 * (j + 1))
                        s.wait_ge(s_mm.sem, mm_b + ntb * (j - 1) if j >= 2 else mm_b)
                        s_wc.inc(s.activation(wb[j % 2][:], wf[:], AF.Identity))
                        if j >= 2:
                            _emit_evac_tile(s, j - 2)
                    for j in (nj - 2, nj - 1):
                        _emit_evac_tile(s, j)
            else:
                @block.scalar
                def _(s):
                    for j in range(nj):
                        s.wait_ge(s_wld.sem, wld_b + 16 * (j + 1))
                        s.wait_ge(s_mm.sem, mm_b + ntb * (j - 1) if j >= 2 else mm_b)
                        s_wc.inc(s.activation(wb[j % 2][:], wf[:], AF.Identity))

            @block.tensor
            def _(t):
                t.wait_ge(rhs_wait[0].sem, rhs_wait[1])
                for j in range(nj):
                    t.wait_ge(s_wc.sem, wc_b + j + 1)
                    for tb in range(ntb):
                        g = ntb * j + tb
                        t.wait_ge(s_ev.sem, ev_b + g - NB + 1 if g >= NB else ev_b)
                        inst = None
                        for kk in range(nk):
                            inst = t.matmul(pmm[g % NB][:],
                                            wb[j % 2][:, kk * 128:(kk + 1) * 128],
                                            rhs_fn(kk, tb),
                                            start=(kk == 0), stop=(kk == nk - 1))
                        s_mm.inc(inst)

            if evac_eng != "scalar":
                @block.vector
                def _(v):
                    for j in range(nj):
                        _emit_evac_tile(v, j)

            if store_fn is not None:
                @block.gpsimd
                def _(g):
                    for j in range(nj):
                        g.wait_ge(s_ev.sem, ev_b + ntb * (j + 1))
                        s_st.inc(store_fn(g, j), 16)
            ps.close()

        LN1_EV = emit_ln(0, TC + 1, "l1")

        l0 = ExitStack()
        xs1 = l0.enter_context(nc.sbuf_tensor("xs1", [128, TC + 1], F32))
        xs2 = l0.enter_context(nc.sbuf_tensor("xs2", [128, TC], F32))
        kvst = [l0.enter_context(nc.sbuf_tensor(f"kvst{p}", [128, TC], F32)) for p in range(2)]
        bfst = [l0.enter_context(nc.sbuf_tensor(f"bfst{p}", [128, TC], BF16)) for p in range(2)]

        def emit_lerp(parname, waitv):
            @block.vector
            def _(v):
                v.wait_ge(s_ev.sem, LN1_EV)
                if waitv is not None:
                    v.wait_ge(s_mm.sem, waitv)
                for i in range(KT):
                    v.tensor_tensor(xs1[:], Xt(i), MU[:, 0:TC + 1], op=OP.subtract)
                    v.tensor_tensor(xs1[:], xs1[:], RSTD[:, 0:TC + 1], op=OP.mult)
                    v.tensor_tensor(xs2[:], xs1[:, 1:TC + 1], xs1[:, 0:TC], op=OP.subtract)
                    s_dve.inc(v.scalar_tensor_tensor(Bt(i), xs2[:], pcol(parname, i),
                                                     xs1[:, 0:TC], op0=OP.mult, op1=OP.add))
            return s_dve.val

        LERP_K = emit_lerp("tmk", None)

        def evac_k(e, psum, j, tb):
            return e.activation(kvst[j % 2][:, tb * 512:(tb + 1) * 512], psum, AF.Identity,
                                bias=pcol("bk", j))

        emit_proj("k", wkT, D, KT, lambda kk, tb: Bt(kk, tb * 512, (tb + 1) * 512),
                  evac_k, "scalar", (s_dve, LERP_K),
                  store_fn=lambda g, j: g.dma_start(out=k_hbm[j * 128:(j + 1) * 128, :],
                                                    in_=kvst[j % 2][:]))

        LERP_V = emit_lerp("tmv", s_mm.val)

        def evac_v(e, psum, j, tb):
            return e.activation(bfst[j % 2][:, tb * 512:(tb + 1) * 512], psum, AF.Identity,
                                bias=pcol("bv", j))

        emit_proj("v", wvT, D, KT, lambda kk, tb: Bt(kk, tb * 512, (tb + 1) * 512),
                  evac_v, "scalar", (s_dve, LERP_V),
                  store_fn=lambda g, j: g.dma_start(out=v_hbm[j * 128:(j + 1) * 128, :],
                                                    in_=bfst[j % 2][:]))

        LERP_R = emit_lerp("tmr", s_mm.val)

        def evac_r(e, psum, j, tb):
            return e.activation(bfst[j % 2][:, tb * 512:(tb + 1) * 512], psum, AF.Sigmoid,
                                bias=pcol("br", j))

        emit_proj("r", wrT, D, KT, lambda kk, tb: Bt(kk, tb * 512, (tb + 1) * 512),
                  evac_r, "scalar", (s_dve, LERP_R),
                  store_fn=lambda g, j: g.dma_start(out=sr_hbm[j * 128:(j + 1) * 128, :],
                                                    in_=bfst[j % 2][:]))
        KVRS_ST = s_st.val

        l0.close()

        scn = ExitStack()
        kbuf = [scn.enter_context(nc.sbuf_tensor(f"kbf{p}", [128, TC], F32)) for p in range(2)]
        vbuf = [scn.enter_context(nc.sbuf_tensor(f"vbf{p}", [128, TC], BF16)) for p in range(2)]
        srbuf = [scn.enter_context(nc.sbuf_tensor(f"srbf{p}", [128, TC], BF16)) for p in range(2)]
        ekb = [scn.enter_context(nc.sbuf_tensor(f"ekbf{p}", [128, TC], F32)) for p in range(2)]
        argb = scn.enter_context(nc.sbuf_tensor("argb", [128, TC], F32))
        wrev = scn.enter_context(nc.sbuf_tensor("wrev", [128, TC], F32))
        ubuf = scn.enter_context(nc.sbuf_tensor("ubuf", [128, TC], F32))
        tbuf = scn.enter_context(nc.sbuf_tensor("tbuf", [128, TC], F32))
        arun = scn.enter_context(nc.sbuf_tensor("arun", [128, TC], F32))
        brun = scn.enter_context(nc.sbuf_tensor("brun", [128, TC], F32))
        aprev = scn.enter_context(nc.sbuf_tensor("aprev", [128, TC], F32))
        bprev = scn.enter_context(nc.sbuf_tensor("bprev", [128, TC], F32))
        ga8 = scn.enter_context(nc.sbuf_tensor("ga8", [8, 256], F32))
        gat = scn.enter_context(nc.sbuf_tensor("gat", [128, 16], F32))

        gldA = s_gld.val
        actA = s_act.val
        dveA = s_dve.val

        @block.gpsimd
        def _(g):
            g.wait_ge(s_st.sem, KVRS_ST)
            for i in range(KT):
                if i >= 2:
                    g.wait_ge(s_dve.sem, dveA + 2 * (i - 2) + 2)
                s_gld.inc(g.dma_start(out=kbuf[i % 2][:], in_=k_hbm[i * 128:(i + 1) * 128, :]), 16)
                s_gld.inc(g.dma_start(out=vbuf[i % 2][:], in_=v_hbm[i * 128:(i + 1) * 128, :]), 16)

        @block.scalar
        def _(s):
            s.wait_ge(s_act.sem, DER_READY)
            for i in range(KT):
                s.wait_ge(s_gld.sem, gldA + 32 * i + 16)
                if i >= 2:
                    s.wait_ge(s_dve.sem, dveA + 2 * (i - 2) + 2)
                s_act.inc(s.activation(ekb[i % 2][:], kbuf[i % 2][:], AF.Exp))
                s.wait_ge(s_dve.sem, dveA + 2 * i + 1)
                s_act.inc(s.activation(wrev[:], argb[:], AF.Exp))

        @block.vector
        def _(v):
            v.wait_ge(s_act.sem, DER_READY)
            v.wait_ge(s_st.sem, KVRS_ST)
            for i in range(KT):
                if i >= 1:
                    v.wait_ge(s_act.sem, actA + 2 * i)
                s_dve.inc(v.tensor_scalar(argb[:], RR[:], dw(i), None, op0=OP.mult))
                v.wait_ge(s_act.sem, actA + 2 * i + 2)
                v.wait_ge(s_gld.sem, gldA + 32 * (i + 1))
                v.tensor_tensor(ubuf[:], ekb[i % 2][:], vbuf[i % 2][:], op=OP.mult)
                v.tensor_tensor(tbuf[:], ubuf[:], wrev[:], op=OP.mult)
                v.tensor_reduce(TOTS[:, i:i + 1], tbuf[:], axis=AX.X, op=OP.add)
                v.tensor_tensor(tbuf[:], ekb[i % 2][:], wrev[:], op=OP.mult)
                s_dve.inc(v.tensor_reduce(TOTS[:, KT + i:KT + i + 1], tbuf[:], axis=AX.X, op=OP.add))
        PASSA_DVE = s_dve.val

        ccA = s_cc.val
        ptot_es = ExitStack()
        ptot = ptot_es.enter_context(nc.psum_tensor("ptot", [32, 128], F32))
        mmT_b = s_mm.val

        @block.tensor
        def _(t):
            t.wait_ge(s_dve.sem, PASSA_DVE)
            s_mm.inc(t.transpose(ptot[:], TOTS[:], IDF[:]))

        dveT_b = s_dve.val

        @block.vector
        def _(v):
            v.wait_ge(s_mm.sem, mmT_b + 1)
            s_dve.inc(v.tensor_copy(TOTT[:], ptot[:]))
        ptot_es.close()

        @block.gpsimd
        def _(g):
            g.wait_ge(s_dve.sem, dveT_b + 1)
            s_st.inc(g.dma_start(out=cin_tot.rearrange("(r p) -> r p", r=32), in_=TOTT[:]), 16)
            g.wait_ge(s_st.sem, s_st.val)
            s_cc.inc(g.collective_compute(
                "AllGather", OP.bypass, replica_groups=[list(range(NCORE))],
                ins=[cin_tot[:]], outs=[cout_tot[:]]))
            s_cc.inc(g.collective_compute(
                "AllGather", OP.bypass, replica_groups=[list(range(NCORE))],
                ins=[cin_f[:]], outs=[cout_f[:]]))

        ldC = s_ld.val
        mmC = s_mm.val
        dveC = s_dve.val
        pcar = ExitStack()
        pga = pcar.enter_context(nc.psum_tensor("pga", [128, 8], F32))
        pgb = pcar.enter_context(nc.psum_tensor("pgb", [128, 8], F32))

        @block.sync
        def _(sync):
            sync.wait_ge(s_cc.sem, ccA + 2)
            for i in range(KT):
                if i >= 1:
                    sync.wait_ge(s_mm.sem, mmC + 2 * i)
                cg = cout_tot
                s_ld.inc(sync.dma_start(out=ga8[:, 0:128], in_=cg[:, i * 128:(i + 1) * 128]), 16)
                s_ld.inc(sync.dma_start(out=ga8[:, 128:256],
                                        in_=cg[:, 2048 + i * 128:2048 + (i + 1) * 128]), 16)

        @block.tensor
        def _(t):
            for i in range(KT):
                t.wait_ge(s_ld.sem, ldC + 32 * (i + 1))
                if i >= 1:
                    t.wait_ge(s_dve.sem, dveC + 3 * i)
                t.transpose(pga[:], ga8[:, 0:128], ID8[:])
                s_mm.inc(t.transpose(pgb[:], ga8[:, 128:256], ID8[:]), 2)

        @block.vector
        def _(v):
            for i in range(KT):
                v.wait_ge(s_mm.sem, mmC + 2 * (i + 1))
                v.tensor_tensor(gat[:, 0:8], pga[:], FCT[:, 8 * i:8 * i + 8], op=OP.mult)
                s_dve.inc(v.tensor_tensor(gat[:, 8:16], pgb[:], FCT[:, 8 * i:8 * i + 8], op=OP.mult))
                v.wait_ge(s_dve.sem, dveC + 3 * i + 1)
                v.tensor_reduce(CARR[:, i:i + 1], gat[:, 0:8], axis=AX.X, op=OP.add)
                s_dve.inc(v.tensor_reduce(CARR[:, KT + i:KT + i + 1], gat[:, 8:16],
                                          axis=AX.X, op=OP.add), 2)
        CARR_DVE = s_dve.val
        pcar.close()

        gldB = s_gld.val
        actB = s_act.val
        dveB = s_dve.val

        @block.gpsimd
        def _(g):
            for i in range(KT):
                if i >= 2:
                    g.wait_ge(s_dve.sem, dveB + (i - 1))
                s_gld.inc(g.dma_start(out=kbuf[i % 2][:], in_=k_hbm[i * 128:(i + 1) * 128, :]), 16)
                s_gld.inc(g.dma_start(out=vbuf[i % 2][:], in_=v_hbm[i * 128:(i + 1) * 128, :]), 16)
                s_gld.inc(g.dma_start(out=srbuf[i % 2][:], in_=sr_hbm[i * 128:(i + 1) * 128, :]), 16)

        @block.scalar
        def _(s):
            for i in range(KT):
                s.wait_ge(s_gld.sem, gldB + 48 * i + 16)
                if i >= 2:
                    s.wait_ge(s_dve.sem, dveB + (i - 1))
                s_act.inc(s.activation(ekb[i % 2][:], kbuf[i % 2][:], AF.Exp))

        @block.vector
        def _(v):
            v.wait_ge(s_dve.sem, CARR_DVE)
            for i in range(KT):
                v.wait_ge(s_act.sem, actB + i + 1)
                v.wait_ge(s_gld.sem, gldB + 48 * (i + 1))
                ek = ekb[i % 2]
                v.tensor_tensor(ubuf[:], ek[:], vbuf[i % 2][:], op=OP.mult)
                v.tensor_tensor_scan(arun[:], dew(i).broadcast_to([128, TC]), ubuf[:],
                                     CARR[:, i:i + 1], op0=OP.mult, op1=OP.add)
                v.tensor_tensor_scan(brun[:], dew(i).broadcast_to([128, TC]), ek[:],
                                     CARR[:, KT + i:KT + i + 1], op0=OP.mult, op1=OP.add)
                v.tensor_copy(aprev[:, 1:TC], arun[:, 0:TC - 1])
                v.tensor_copy(aprev[:, 0:1], CARR[:, i:i + 1])
                v.tensor_copy(bprev[:, 1:TC], brun[:, 0:TC - 1])
                v.tensor_copy(bprev[:, 0:1], CARR[:, KT + i:KT + i + 1])
                v.scalar_tensor_tensor(ubuf[:], ubuf[:], deb(i), aprev[:], op0=OP.mult, op1=OP.add)
                v.scalar_tensor_tensor(tbuf[:], ek[:], deb(i), bprev[:], op0=OP.mult, op1=OP.add)
                v.reciprocal(tbuf[:], tbuf[:])
                v.tensor_tensor(ubuf[:], ubuf[:], tbuf[:], op=OP.mult)
                s_dve.inc(v.tensor_tensor(Bt(i), ubuf[:], srbuf[i % 2][:], op=OP.mult))
        SCAN_DVE = s_dve.val
        scn.close()

        wo_ps = ExitStack()

        def evac_wo(e, psum, j, tb):
            xsl = Xt(j, 1 + tb * 512, 1 + (tb + 1) * 512)
            return e.scalar_tensor_tensor(xsl, psum, pcol("bo", j), xsl, op0=OP.add, op1=OP.add)

        emit_proj("wo", woT, D, KT, lambda kk, tb: Bt(kk, tb * 512, (tb + 1) * 512),
                  evac_wo, "vector", (s_dve, SCAN_DVE), pre_waits=((s_dve, SCAN_DVE),))
        wo_ps.close()
        STAGE1_EV = s_ev.val

        if stages >= 2:
            LN2_MM0 = s_mm.val
            LN2_EV = emit_ln(1, TC, "l2", pre_wait=(s_ev, s_ev.val))
            LN2_MM = s_mm.val

            f0 = ExitStack()
            xq1 = f0.enter_context(nc.sbuf_tensor("xq1", [128, TC], F32))
            fq8 = f0.enter_context(nc.sbuf_tensor("fq8", [128, TC], FP8))
            dve_q = s_dve.val

            if F8MODE >= 1:
                @block.vector
                def _(v):
                    v.wait_ge(s_ev.sem, LN2_EV)
                    v.wait_ge(s_dve.sem, SCALES_DVE)
                    for i in range(KT):
                        v.tensor_tensor(xq1[:], Xt(i, 1, TC + 1), MU[:, 0:TC], op=OP.subtract)
                        v.tensor_tensor(xq1[:], xq1[:], RSTD[:, 0:TC], op=OP.mult)
                        v.tensor_scalar(xq1[:], xq1[:], dsc(0), None, op0=OP.mult)
                        v.tensor_copy(fq8[:], xq1[:])
                        s_dve.inc(v.tensor_copy(Bt(i), fq8[:]))
            QDQ_DVE = s_dve.val

            def evac_f8(e, psum, j, tb):
                xsl = Xt(j, 1 + tb * 512, 1 + (tb + 1) * 512)
                return e.scalar_tensor_tensor(xsl, psum, dsc(1), xsl, op0=OP.mult, op1=OP.add)

            if F8MODE >= 2:
                emit_proj("f8", w8T, D, KT, lambda kk, tb: Bt(kk, tb * 512, (tb + 1) * 512),
                          evac_f8, "vector", (s_dve, QDQ_DVE), wdt=FP8,
                          pre_waits=((s_mm, LN2_MM), (s_dve, QDQ_DVE)))
            f0.close()


        Q4 = {}

        def emit_quant_A():
            t4 = ExitStack()
            wsc = [t4.enter_context(nc.sbuf_tensor(f"wsc{p}", [128, 2048], F32)) for p in range(4)]
            pacc = t4.enter_context(nc.sbuf_tensor("pacc", [128, 512], F32))
            prsc = t4.enter_context(nc.sbuf_tensor("prsc", [128, 1], F32))
            pbk4 = t4.enter_context(nc.psum_tensor("pbk4", [128, 512], F32))
            NQC = 3 * NQS

            astA = s_ast.val
            dveA = s_dve.val
            st_gate = s_st.val

            @block.vector
            def _(v):
                v.wait_ge(s_ld.sem, SETUP_LD)
                v.wait_ge(s_st.sem, st_gate)
                s_dve.inc(v.memset(pacc[:], 0.0))

            @block.sync
            def _(sync):
                sync.wait_ge(s_st.sem, st_gate)
                for c in range(NQC):
                    if c >= 4:
                        sync.wait_ge(s_dve.sem, dveA + 1 + 2 * (c - 3))
                    s_ast.inc(sync.dma_start(out=wsc[c % 4][:],
                                             in_=wqmy[c * 128:(c + 1) * 128, :]), 16)

            @block.vector
            def _(v):
                for c in range(NQC):
                    cw = c // NQS
                    v.wait_ge(s_ast.sem, astA + 16 * (c + 1))
                    s_dve.inc(v.tensor_reduce(prsc[:], wsc[c % 4][:], axis=AX.X,
                                              op=OP.add, apply_absolute_value=True))
                    s_dve.inc(v.tensor_tensor(pacc[:, cw:cw + 1], pacc[:, cw:cw + 1],
                                              prsc[:], op=OP.add))
            PACC_DVE = s_dve.val

            prow = t4.enter_context(nc.psum_tensor("prow", [1, 512], F32))
            srow4 = t4.enter_context(nc.sbuf_tensor("srow4", [1, 512], F32))
            g8 = t4.enter_context(nc.sbuf_tensor("g8", [8, 512], F32))
            mm4a = s_mm.val
            act_gate = s_act.val
            dve4a = s_dve.val

            @block.tensor
            def _(t):
                t.wait_ge(s_dve.sem, PACC_DVE)
                t.wait_ge(s_act.sem, act_gate)
                s_mm.inc(t.matmul(prow[:], OCol[:], pacc[:], start=True, stop=True))

            @block.vector
            def _(v):
                v.wait_ge(s_mm.sem, mm4a + 1)
                s_dve.inc(v.tensor_copy(srow4[:], prow[:]))

            st4 = s_st.val

            @block.gpsimd
            def _(g):
                g.wait_ge(s_dve.sem, dve4a + 1)
                s_st.inc(g.dma_start(out=cin_s2.rearrange("(o c) -> o c", o=1),
                                     in_=srow4[:]), 16)
                g.wait_ge(s_st.sem, st4 + 16)
                s_cc.inc(g.collective_compute(
                    "AllGather", OP.bypass, replica_groups=[list(range(NCORE))],
                    ins=[cin_s2[:]], outs=[cout_s2[:]]))
            CC_SC = s_cc.val

            ld4b = s_ld.val

            @block.sync
            def _(sync):
                sync.wait_ge(s_cc.sem, CC_SC)
                s_ld.inc(sync.dma_start(out=g8[:], in_=cout_s2[:]), 16)

            mm4 = s_mm.val

            @block.tensor
            def _(t):
                t.wait_ge(s_ld.sem, ld4b + 16)
                s_mm.inc(t.matmul(prow[:], OCol[0:8, :], g8[:], start=True, stop=True))

            @block.vector
            def _(v):
                v.wait_ge(s_mm.sem, mm4 + 1)
                s_dve.inc(v.tensor_copy(srow4[:], prow[:]))
            SUM8_DVE = s_dve.val

            @block.tensor
            def _(t):
                t.wait_ge(s_dve.sem, SUM8_DVE)
                s_mm.inc(t.matmul(pbk4[:], ORow[:], srow4[:], start=True, stop=True))

            dve4c = s_dve.val

            @block.vector
            def _(v):
                v.wait_ge(s_mm.sem, mm4 + 2)
                s_dve.inc(v.tensor_scalar(DER[:, DSB + 2:DSB + 5], pbk4[:, 0:3], cc(C_INVCNT2),
                                          None, op0=OP.mult))
                v.wait_ge(s_dve.sem, dve4c + 1)
                s_dve.inc(v.tensor_scalar(DER[:, DSB + 5:DSB + 8], DER[:, DSB + 2:DSB + 5],
                                          cc(C_NEG1), None, op0=OP.mult))
            Q4["SCALE4_DVE"] = s_dve.val
            t4.close()

        def emit_quant_B():
            tq = ExitStack()
            wq = [tq.enter_context(nc.sbuf_tensor(f"wq{p}", [128, 2048], F32)) for p in range(2)]
            sgA = tq.enter_context(nc.sbuf_tensor("sgA", [128, 2048], F32))
            qbb = [tq.enter_context(nc.sbuf_tensor(f"qbb{p}", [128, 2048], BF16)) for p in range(2)]
            NQC = 3 * NQS
            astq = s_ast.val
            actq = s_act.val
            dveq = s_dve.val
            SC4D = Q4["SCALE4_DVE"]
            st_gate = s_st.val

            @block.sync
            def _(sync):
                sync.wait_ge(s_dve.sem, SC4D)
                sync.wait_ge(s_st.sem, st_gate)
                for c in range(NQC):
                    if c >= 2:
                        sync.wait_ge(s_act.sem, actq + c - 1)
                    s_ast.inc(sync.dma_start(out=wq[c % 2][:],
                                             in_=wqmy[c * 128:(c + 1) * 128, :]), 16)
                    if c >= 1:
                        sync.wait_ge(s_dve.sem, dveq + c)
                        s_ast.inc(sync.dma_start(out=cin_q[(c - 1) * 128:c * 128, :],
                                                 in_=qbb[(c - 1) % 2][:]), 16)
                sync.wait_ge(s_dve.sem, dveq + NQC)
                s_ast.inc(sync.dma_start(out=cin_q[(NQC - 1) * 128:NQC * 128, :],
                                         in_=qbb[(NQC - 1) % 2][:]), 16)

            @block.scalar
            def _(s):
                for c in range(NQC):
                    cw = c // NQS
                    s.wait_ge(s_ast.sem, astq + 16 * (2 * c + 1))
                    if c >= 1:
                        s.wait_ge(s_dve.sem, dveq + c)
                    s.activation(sgA[:], wq[c % 2][:], AF.Sign, bias=dsc(5 + cw))
                    s_act.inc(s.activation(qbb[c % 2][:], wq[c % 2][:], AF.Sign,
                                           bias=dsc(2 + cw)))

            @block.vector
            def _(v):
                for c in range(NQC):
                    v.wait_ge(s_act.sem, actq + c + 1)
                    s_dve.inc(v.tensor_tensor(qbb[c % 2][:], sgA[:], qbb[c % 2][:], op=OP.add))
            Q4["QNT_AST"] = s_ast.val
            tq.close()

        def emit_quant_B_ag():
            @block.gpsimd
            def _(g):
                g.wait_ge(s_ast.sem, Q4["QNT_AST"])
                s_cc.inc(g.collective_compute(
                    "AllGather", OP.bypass, replica_groups=[list(range(NCORE))],
                    ins=[cin_q[:]], outs=[cout_q[:]]))
                s_cc.inc(g.collective_compute(
                    "AllGather", OP.bypass, replica_groups=[list(range(NCORE))],
                    ins=[cin_f[:]], outs=[cout_f[:]]))
            Q4["CC_Q"] = s_cc.val

        if stages >= 3:
            LN3_EV = emit_ln(1, TC, "l3", pre_wait=(s_ev, s_ev.val))
            LN3_MM = s_mm.val

            a3 = ExitStack()
            xs3 = a3.enter_context(nc.sbuf_tensor("xs3", [128, TC], F32))
            qst = [a3.enter_context(nc.sbuf_tensor(f"qst{p}", [128, TC], BF16)) for p in range(2)]
            dve_x3 = s_dve.val

            @block.vector
            def _(v):
                v.wait_ge(s_ev.sem, LN3_EV)
                for i in range(KT):
                    v.tensor_tensor(xs3[:], Xt(i, 1, TC + 1), MU[:, 0:TC], op=OP.subtract)
                    s_dve.inc(v.tensor_tensor(Bt(i), xs3[:], RSTD[:, 0:TC], op=OP.mult))
            XLN3_DVE = s_dve.val

            def evac_qkv(e, psum, j, tb):
                return e.activation(qst[j % 2][:, tb * 512:(tb + 1) * 512], psum, AF.Identity,
                                    bias=BQ[:, j:j + 1])

            emit_proj("qkv", wqkv, 3 * D, KT, lambda kk, tb: Bt(kk, tb * 512, (tb + 1) * 512),
                      evac_qkv, "scalar", (s_dve, XLN3_DVE),
                      store_fn=lambda g, j: g.dma_start(out=cin_kqv[j * 128:(j + 1) * 128, :],
                                                        in_=qst[j % 2][:]),
                      pre_waits=((s_mm, LN3_MM),))
            QKV_ST = s_st.val
            a3.close()

            if stages >= 4:
                if "A" in QHOIST:
                    emit_quant_A()
                if "B" in QHOIST:
                    emit_quant_B()

            cc_kb = s_cc.val

            @block.gpsimd
            def _(g):
                g.wait_ge(s_st.sem, QKV_ST)
                s_cc.inc(g.collective_compute(
                    "AllGather", OP.bypass, replica_groups=[list(range(NCORE))],
                    ins=[cin_kqv[:]], outs=[cout_kqv[:]]))
                s_cc.inc(g.collective_compute(
                    "AllGather", OP.bypass, replica_groups=[list(range(NCORE))],
                    ins=[cin_f[:]], outs=[cout_f[:]]))
            CC_KQV = s_cc.val

            if stages >= 4 and "B" in QHOIST:
                emit_quant_B_ag()

            att = ExitStack()
            Ksb = att.enter_context(nc.sbuf_tensor("Ksb", [128, T], BF16))
            Vsb = att.enter_context(nc.sbuf_tensor("Vsb", [128, T], BF16))
            Qsb = att.enter_context(nc.sbuf_tensor("Qsb", [128, T], BF16))
            Psb = [att.enter_context(nc.sbuf_tensor(f"Psb{p}", [128, 512], BF16)) for p in range(4)]
            lacc = att.enter_context(nc.sbuf_tensor("lacc", [128, 512], F32))
            lrow = att.enter_context(nc.sbuf_tensor("lrow", [1, 512], F32))
            rbsb = att.enter_context(nc.sbuf_tensor("rbsb", [128, 512], F32))
            Ostg = [att.enter_context(nc.sbuf_tensor(f"Ostg{p}", [128, 512], BF16)) for p in range(2)]
            pS = [att.enter_context(nc.psum_tensor(f"pS{p}", [128, 512], F32)) for p in range(3)]
            pO = [att.enter_context(nc.psum_tensor(f"pO{p}", [128, 512], F32)) for p in range(2)]
            plr = att.enter_context(nc.psum_tensor("plr", [1, 512], F32))
            pbc = att.enter_context(nc.psum_tensor("pbc", [128, 512], F32))

            NQB = T // 512
            NST = T // 128
            DEPTH = 3

            for hh in range(HPC):
                ld_h = s_ld.val
                mm_h = s_mm.val
                act_h = s_act.val
                dve_h = s_dve.val
                st_h = s_st.val
                prev_mm = mm_h
                prev_dve = dve_h

                @block.sync
                def _(sync, hh=hh, prev_mm=prev_mm, prev_dve=prev_dve):
                    sync.wait_ge(s_cc.sem, CC_KQV)
                    if hh == 0 and stages >= 4 and "B" in QHOIST:
                        sync.wait_ge(s_ast.sem, Q4["QNT_AST"])
                    if hh > 0:
                        sync.wait_ge(s_mm.sem, prev_mm)
                        sync.wait_ge(s_dve.sem, prev_dve)
                    pid = sync.partition_id()
                    preg = sync.to_reg(pid)
                    for r in range(NCORE):
                        hg = HPC * r + hh
                        with sync.If_eq(preg, r):
                            for dst, base in ((Qsb, 0), (Ksb, D), (Vsb, 2 * D)):
                                srcv = cout_kqv[:, base + 128 * hg: base + 128 * (hg + 1), :]
                                srcv = srcv.rearrange("r d t -> d r t")
                                dstv = dst.ap().rearrange("p (r t) -> p r t", t=TC)
                                s_ld.inc(sync.dma_start(out=dstv, in_=srcv), 16)
                        if r != NCORE - 1:
                            s_ld.val -= 48
                LD_H = s_ld.val

                @block.tensor
                def _(t):
                    t.wait_ge(s_ld.sem, LD_H)
                    for st in range(NST):
                        if st >= 3:
                            t.wait_ge(s_dve.sem, dve_h + st - 2)
                        s_mm.inc(t.matmul(pS[st % 3][:, 0:128],
                                          Vsb[:, st * 128:(st + 1) * 128], IDN[:],
                                          start=True, stop=True))

                @block.vector
                def _(v):
                    for st in range(NST):
                        v.wait_ge(s_mm.sem, mm_h + st + 1)
                        s_dve.inc(v.tensor_copy(Vsb[:, st * 128:(st + 1) * 128],
                                                pS[st % 3][:, 0:128]))

                mmA = s_mm.val
                actA2 = s_act.val
                dveA2 = s_dve.val
                stA = s_st.val

                flatS = []
                for qb in range(NQB):
                    for st in range(4 * (qb + 1)):
                        flatS.append((qb, st))
                NG = len(flatS)

                ops = []
                tails = []
                for gi in range(NG + DEPTH):
                    if gi < NG:
                        ops.append(("S",) + flatS[gi])
                    j = gi - DEPTH
                    if j >= 0:
                        qb, st = flatS[j]
                        ops.append(("PV", qb, st))
                        if st == 4 * (qb + 1) - 1:
                            tails.append([gi + 1, ("LR", qb)])
                            tails.append([gi + 2, ("BC", qb)])
                    while tails and tails[0][0] <= gi:
                        ops.append(tails.pop(0)[1])
                for _, op in tails:
                    ops.append(op)

                mm_S = {}; mm_PV = {}; mm_lr = {}; mm_bc = {}
                act_exp = {}; act_rb = {}
                dve_msk = {}; dve_l = {}; dve_rc = {}; dve_om = {}; st_o = {}
                cmm = cact = cdve = cst = 0
                for op in ops:
                    if op[0] == "S":
                        cmm += 1; mm_S[op[1:]] = cmm
                        cact += 1; act_exp[op[1:]] = cact
                        qb, st = op[1:]
                        if st >= 4 * qb:
                            cdve += 1; dve_msk[op[1:]] = cdve
                        cdve += 1; dve_l[op[1:]] = cdve
                    elif op[0] == "PV":
                        cmm += 1; mm_PV[op[1:]] = cmm
                    elif op[0] == "LR":
                        cmm += 1; mm_lr[op[1]] = cmm
                        cdve += 1; dve_rc[op[1]] = cdve
                    else:
                        cmm += 1; mm_bc[op[1]] = cmm
                        cact += 1; act_rb[op[1]] = cact
                        cdve += 1; dve_om[op[1]] = cdve
                        cst += 16; st_o[op[1]] = cst
                gidx = {k: i for i, k in enumerate(flatS)}

                @block.tensor
                def _(t):
                    t.wait_ge(s_dve.sem, dveA2)
                    for op in ops:
                        if op[0] == "S":
                            qb, st = op[1:]
                            gi = gidx[(qb, st)]
                            if gi >= 3:
                                t.wait_ge(s_act.sem, actA2 + act_exp[flatS[gi - 3]])
                            s_mm.inc(t.matmul(pS[gi % 3][:], Ksb[:, st * 128:(st + 1) * 128],
                                              Qsb[:, qb * 512:(qb + 1) * 512],
                                              start=True, stop=True))
                        elif op[0] == "PV":
                            qb, st = op[1:]
                            gi = gidx[(qb, st)]
                            ns = 4 * (qb + 1)
                            if st == 0 and qb >= 2:
                                t.wait_ge(s_dve.sem, dveA2 + dve_om[qb - 2])
                            if st >= 4 * qb:
                                t.wait_ge(s_dve.sem, dveA2 + dve_msk[(qb, st)])
                            else:
                                t.wait_ge(s_act.sem, actA2 + act_exp[(qb, st)])
                            s_mm.inc(t.matmul(pO[qb % 2][:], Vsb[:, st * 128:(st + 1) * 128],
                                              Psb[gi % 4][:],
                                              start=(st == 0), stop=(st == ns - 1)))
                        elif op[0] == "LR":
                            qb = op[1]
                            t.wait_ge(s_dve.sem, dveA2 + dve_l[(qb, 4 * (qb + 1) - 1)])
                            if qb >= 1:
                                t.wait_ge(s_dve.sem, dveA2 + dve_rc[qb - 1])
                            s_mm.inc(t.matmul(plr[:], OCol[:], lacc[:], start=True, stop=True))
                        else:
                            qb = op[1]
                            t.wait_ge(s_dve.sem, dveA2 + dve_rc[qb])
                            if qb >= 1:
                                t.wait_ge(s_act.sem, actA2 + act_rb[qb - 1])
                            s_mm.inc(t.matmul(pbc[:], ORow[:], lrow[0:1, :], start=True, stop=True))

                @block.scalar
                def _(s):
                    for op in ops:
                        if op[0] == "S":
                            qb, st = op[1:]
                            gi = gidx[(qb, st)]
                            s.wait_ge(s_mm.sem, mmA + mm_S[(qb, st)])
                            if gi >= 4:
                                s.wait_ge(s_mm.sem, mmA + mm_PV[flatS[gi - 4]])
                                s.wait_ge(s_dve.sem, dveA2 + dve_l[flatS[gi - 4]])
                            s_act.inc(s.activation(Psb[gi % 4][:], pS[gi % 3][:], AF.Exp,
                                                   scale=cc(C_ATTSC), bias=cc(C_NSHIFT)))
                        elif op[0] == "BC":
                            qb = op[1]
                            s.wait_ge(s_mm.sem, mmA + mm_bc[qb])
                            if qb >= 1:
                                s.wait_ge(s_dve.sem, dveA2 + dve_om[qb - 1])
                            s_act.inc(s.activation(rbsb[:], pbc[:], AF.Identity))

                @block.vector
                def _(v):
                    for op in ops:
                        if op[0] == "S":
                            qb, st = op[1:]
                            gi = gidx[(qb, st)]
                            v.wait_ge(s_act.sem, actA2 + act_exp[(qb, st)])
                            rel = st - 4 * qb
                            if rel >= 0:
                                s_dve.inc(v.tensor_tensor(Psb[gi % 4][:], Psb[gi % 4][:],
                                                          MSK[:, rel * 512:(rel + 1) * 512],
                                                          op=OP.mult))
                            if st == 0:
                                if qb >= 1:
                                    v.wait_ge(s_mm.sem, mmA + mm_lr[qb - 1])
                                s_dve.inc(v.tensor_copy(lacc[:], Psb[gi % 4][:]))
                            else:
                                s_dve.inc(v.tensor_tensor(lacc[:], lacc[:], Psb[gi % 4][:],
                                                          op=OP.add))
                        elif op[0] == "LR":
                            qb = op[1]
                            v.wait_ge(s_mm.sem, mmA + mm_lr[qb])
                            s_dve.inc(v.reciprocal(lrow[0:1, :], plr[:]))
                        elif op[0] == "BC":
                            qb = op[1]
                            v.wait_ge(s_act.sem, actA2 + act_rb[qb])
                            if qb >= 2:
                                v.wait_ge(s_st.sem, stA + st_o[qb - 2])
                            s_dve.inc(v.tensor_tensor(Ostg[qb % 2][:], pO[qb % 2][:], rbsb[:],
                                                      op=OP.mult))

                @block.gpsimd
                def _(g, hh=hh):
                    for qb in range(NQB):
                        g.wait_ge(s_dve.sem, dveA2 + dve_om[qb])
                        s_st.inc(g.dma_start(
                            out=cin_o[hh * 128:(hh + 1) * 128, qb * 512:(qb + 1) * 512],
                            in_=Ostg[qb % 2][:]), 16)

            ATT_ST = s_st.val
            att.close()

            @block.gpsimd
            def _(g):
                g.wait_ge(s_st.sem, ATT_ST)
                s_cc.inc(g.collective_compute(
                    "AllGather", OP.bypass, replica_groups=[list(range(NCORE))],
                    ins=[cin_o[:]], outs=[cout_o[:]]))
                s_cc.inc(g.collective_compute(
                    "AllGather", OP.bypass, replica_groups=[list(range(NCORE))],
                    ins=[cin_f[:]], outs=[cout_f[:]]))
            CC_O = s_cc.val

            @block.sync
            def _(sync):
                sync.wait_ge(s_cc.sem, CC_O)
                sync.wait_ge(s_mm.sem, s_mm.val)
                pid = sync.partition_id()
                preg = sync.to_reg(pid)
                for r in range(NCORE):
                    with sync.If_eq(preg, r):
                        for i in range(KT):
                            src = cout_o[i // HPC, (i % HPC) * 128:(i % HPC + 1) * 128,
                                         r * TC:(r + 1) * TC]
                            s_ld.inc(sync.dma_start(out=Bt(i), in_=src), 16)
                    if r != NCORE - 1:
                        s_ld.val -= 16 * KT
            OB_LD = s_ld.val

            def evac_wo1(e, psum, j, tb):
                xsl = Xt(j, 1 + tb * 512, 1 + (tb + 1) * 512)
                return e.scalar_tensor_tensor(xsl, psum, pcol("bo1", j), xsl,
                                              op0=OP.add, op1=OP.add)

            emit_proj("wo1", wo1, D, KT, lambda kk, tb: Bt(kk, tb * 512, (tb + 1) * 512),
                      evac_wo1, "vector", (s_ld, OB_LD), pre_waits=((s_ld, OB_LD),))


        if stages >= 4:
            P4 = _s4parts()
            if "C" in P4:
                LN4_EV = emit_ln(1, TC, "l4", upto=S4LN,
                                 pre_wait=[(s_ev, s_ev.val), (s_dve, Q4["SCALE4_DVE"])])

                f4 = ExitStack()
                xs4 = f4.enter_context(nc.sbuf_tensor("xs4", [128, TC], F32))
                dve_x4 = s_dve.val

                if "c" in P4:
                    @block.vector
                    def _(v):
                        v.wait_ge(s_ev.sem, LN4_EV)
                        v.wait_ge(s_mm.sem, s_mm.val)
                        for i in range(KT):
                            v.tensor_tensor(xs4[:], Xt(i, 1, TC + 1), MU[:, 0:TC], op=OP.subtract)
                            s_dve.inc(v.tensor_tensor(Bt(i), xs4[:], RSTD[:, 0:TC], op=OP.mult))
                XLN4_DVE = s_dve.val

            if "D" in P4:
                gsb = f4.enter_context(nc.sbuf_tensor("gsb", [128, JT * 512], BF16))
                q1s = [f4.enter_context(nc.sbuf_tensor(f"q1s{p}", [128, KT * 128], BF16)) for p in range(2)]
                q3s = [f4.enter_context(nc.sbuf_tensor(f"q3s{p}", [128, KT * 128], BF16)) for p in range(2)]
                silb = [f4.enter_context(nc.sbuf_tensor(f"silb{p}", [128, 512], F32)) for p in range(2)]
                q2s = [f4.enter_context(nc.sbuf_tensor(f"q2s{p}", [128, 1024], BF16)) for p in range(2)]

                W2MM = {"v": s_mm.val}
                W2DVE = {"v": s_dve.val}
                for half in range(2):
                    hp = ExitStack()
                    ph1 = [hp.enter_context(nc.psum_tensor(f"ph1{half}{p}", [128, 512], F32)) for p in range(2)]
                    ph3 = [hp.enter_context(nc.psum_tensor(f"ph3{half}{p}", [128, 512], F32)) for p in range(2)]

                    ldh = s_ld.val
                    mmh = s_mm.val
                    acth = s_act.val
                    dveh = s_dve.val
                    w2prev = W2MM["v"]
                    w2dveprev = W2DVE["v"]

                    @block.sync
                    def _(sync, half=half, w2prev=w2prev):
                        sync.wait_ge(s_cc.sem, Q4["CC_Q"])
                        sync.wait_ge(s_dve.sem, XLN4_DVE)
                        if half > 0:
                            sync.wait_ge(s_mm.sem, w2prev)
                        for j in range(JT):
                            if j >= 1:
                                sync.wait_ge(s_mm.sem, mmh + 2 * j - 2 if j >= 2 else mmh)
                            s_ld.inc(sync.dma_start(
                                out=q1s[j % 2][:],
                                in_=cout_q[j % NCORE,
                                           (0 * NQS + j // NCORE) * 128:
                                           (0 * NQS + j // NCORE + 1) * 128, :]), 16)
                            s_ld.inc(sync.dma_start(
                                out=q3s[j % 2][:],
                                in_=cout_q[j % NCORE,
                                           (1 * NQS + j // NCORE) * 128:
                                           (1 * NQS + j // NCORE + 1) * 128, :]), 16)

                    @block.tensor
                    def _(t, half=half, w2prev=w2prev, w2dveprev=w2dveprev):
                        t.wait_ge(s_dve.sem, XLN4_DVE)
                        if half > 0:
                            t.wait_ge(s_mm.sem, w2prev)
                            t.wait_ge(s_dve.sem, w2dveprev)
                        for j in range(JT):
                            t.wait_ge(s_ld.sem, ldh + 32 * (j + 1))
                            if j >= 2:
                                t.wait_ge(s_dve.sem, dveh + (j - 1))
                            inst = None
                            for kk in range(KT):
                                inst = t.matmul(ph1[j % 2][:], q1s[j % 2][:, kk * 128:(kk + 1) * 128],
                                                Bt(kk, half * 512, (half + 1) * 512),
                                                start=(kk == 0), stop=(kk == KT - 1))
                            s_mm.inc(inst)
                            for kk in range(KT):
                                inst = t.matmul(ph3[j % 2][:], q3s[j % 2][:, kk * 128:(kk + 1) * 128],
                                                Bt(kk, half * 512, (half + 1) * 512),
                                                start=(kk == 0), stop=(kk == KT - 1))
                            s_mm.inc(inst)

                    @block.scalar
                    def _(s):
                        for j in range(JT):
                            s.wait_ge(s_mm.sem, mmh + 2 * j + 1)
                            if j >= 2:
                                s.wait_ge(s_dve.sem, dveh + (j - 1))
                            s_act.inc(s.activation(silb[j % 2][:], ph1[j % 2][:], AF.Silu,
                                                   scale=dsc(2)))

                    @block.vector
                    def _(v, half=half):
                        for j in range(JT):
                            v.wait_ge(s_mm.sem, mmh + 2 * j + 2)
                            v.wait_ge(s_act.sem, acth + j + 1)
                            s_dve.inc(v.scalar_tensor_tensor(gsb[:, j * 512:(j + 1) * 512],
                                                             ph3[j % 2][:], dsc(3), silb[j % 2][:],
                                                             op0=OP.mult, op1=OP.mult))
                    G_DVE = s_dve.val
                    hp.close()

                    w2p = ExitStack()
                    pw2 = [w2p.enter_context(nc.psum_tensor(f"pw2{half}{b}", [128, 512], F32))
                           for b in range(8)]
                    for grp in range(2):
                        ldw2 = s_ld.val
                        mmw2 = s_mm.val
                        dvew2 = s_dve.val

                        @block.sync
                        def _(sync, grp=grp, half=half):
                            sync.wait_ge(s_dve.sem, G_DVE)
                            for j in range(JT):
                                if j >= 2:
                                    sync.wait_ge(s_mm.sem, mmw2 + (j - 1))
                                src = cout_q[j % NCORE,
                                             (2 * NQS + j // NCORE) * 128:
                                             (2 * NQS + j // NCORE + 1) * 128,
                                             grp * 1024:(grp + 1) * 1024]
                                s_ld.inc(sync.dma_start(out=q2s[j % 2][:], in_=src), 16)

                        @block.tensor
                        def _(t, grp=grp, half=half):
                            t.wait_ge(s_dve.sem, G_DVE)
                            if grp > 0:
                                t.wait_ge(s_dve.sem, dvew2)
                            for j in range(JT):
                                t.wait_ge(s_ld.sem, ldw2 + 16 * (j + 1))
                                inst = None
                                for dt_ in range(8):
                                    inst = t.matmul(pw2[dt_][:],
                                                    q2s[j % 2][:, dt_ * 128:(dt_ + 1) * 128],
                                                    gsb[:, j * 512:(j + 1) * 512],
                                                    start=(j == 0), stop=(j == JT - 1))
                                s_mm.inc(inst)
                        MM_W2 = s_mm.val

                        @block.vector
                        def _(v, grp=grp, half=half):
                            v.wait_ge(s_mm.sem, MM_W2)
                            for dt_ in range(8):
                                i_gl = grp * 8 + dt_
                                xsl = Xt(i_gl, 1 + half * 512, 1 + (half + 1) * 512)
                                s_dve.inc(v.scalar_tensor_tensor(xsl, pw2[dt_][:], dsc(4), xsl,
                                                                 op0=OP.mult, op1=OP.add))
                    W2MM["v"] = s_mm.val
                    W2DVE["v"] = s_dve.val
                    w2p.close()
                f4.close()
            if "C" in P4 and "D" not in P4:
                f4.close()

        if NPAD:
            eng = {"vector": block.vector, "tensor": block.tensor,
                   "scalar": block.scalar, "sync": block.sync}[NPAD_ENG]

            @eng
            def _(e):
                for _ in range(NPAD):
                    e.wait_ge(s_ld.sem, 0)

        if DBG:
            dmu = nc.dram_tensor("dmu", [128, TC + 1], F32, kind="ExternalOutput").ap()
            drstd = nc.dram_tensor("drstd", [128, TC + 1], F32, kind="ExternalOutput").ap()
            dcarr = nc.dram_tensor("dcarr", [128, 2 * KT], F32, kind="ExternalOutput").ap()
            dtots = nc.dram_tensor("dtots", [128, 2 * KT], F32, kind="ExternalOutput").ap()
            dbb = nc.dram_tensor("dbb", [128, KT * TC], BF16, kind="ExternalOutput").ap()
            dder = nc.dram_tensor("dder", [128, 3 * KT + 8], F32, kind="ExternalOutput").ap()
            dfct = nc.dram_tensor("dfct", [128, 8 * KT], F32, kind="ExternalOutput").ap()
            dcout = nc.dram_tensor("dcout", [NCORE, 2 * KT * 128], F32, kind="ExternalOutput").ap()
            dcinsc = nc.dram_tensor("dcinsc", [128, 512], F32, kind="ExternalOutput").ap()
            dcsc = nc.dram_tensor("dcsc", [NCORE, 128 * 512], F32, kind="ExternalOutput").ap()

        @block.gpsimd
        def _(g):
            g.wait_ge(s_ev.sem, s_ev.val)
            g.wait_ge(s_dve.sem, s_dve.val)
            for i in range(KT):
                s_st.inc(g.dma_start(out=yout[i * 128:(i + 1) * 128, :], in_=Xt(i, 1, TC + 1)), 16)
            if DBG:
                s_st.inc(g.dma_start(out=dmu[:], in_=MU[:]), 16)
                s_st.inc(g.dma_start(out=drstd[:], in_=RSTD[:]), 16)
                s_st.inc(g.dma_start(out=dcarr[:], in_=CARR[:]), 16)
                s_st.inc(g.dma_start(out=dtots[:], in_=TOTS[:]), 16)
                s_st.inc(g.dma_start(out=dbb[:], in_=BB[:]), 16)
                s_st.inc(g.dma_start(out=dder[:], in_=DER[:]), 16)
                s_st.inc(g.dma_start(out=dfct[:], in_=FCT[:]), 16)
                s_st.inc(g.dma_start(out=dcout[:], in_=cout_tot[:]), 16)
                s_st.inc(g.dma_start(out=dcinsc[:], in_=cin_sc.rearrange("(p c) -> p c", p=128)), 16)
                s_st.inc(g.dma_start(out=dcsc[:], in_=cout_sc[:]), 16)
            g.wait_ge(s_st.sem, s_st.val)

    return nc


def _tp(a):
    return np.ascontiguousarray(np.asarray(a).T)


def _wtile(w):
    w = np.asarray(w)
    K, M = w.shape
    nk, nj = K // 128, M // 128
    return np.ascontiguousarray(
        w.reshape(nk, 128, nj, 128).transpose(2, 1, 0, 3).reshape(nj * 128, nk * 128))


def _colpack(vec):
    return np.ascontiguousarray(np.asarray(vec).reshape(KT, 128).T)


def kernel(**inputs):
    x = np.asarray(inputs["L0_attn_x"])[0]
    stages = N_STAGES

    pc = np.zeros((128, NPAR * KT), np.float32)
    pars = {
        "tmk": np.asarray(inputs["L0_attn_tmk"]).reshape(D),
        "tmv": np.asarray(inputs["L0_attn_tmv"]).reshape(D),
        "tmr": np.asarray(inputs["L0_attn_tmr"]).reshape(D),
        "bk": np.asarray(inputs["L0_attn_bk"]),
        "bv": np.asarray(inputs["L0_attn_bv"]),
        "br": np.asarray(inputs["L0_attn_br"]),
        "bo": np.asarray(inputs["L0_attn_bo"]),
        "td": np.asarray(inputs["L0_attn_time_decay"]),
        "tf": np.asarray(inputs["L0_attn_time_first"]),
        "bo1": np.asarray(inputs["L1_attn_bo"]),
    }
    for n, v in pars.items():
        pc[:, PAR[n] * KT:(PAR[n] + 1) * KT] = _colpack(v.astype(np.float32))

    bq = np.ascontiguousarray(np.asarray(inputs["L1_attn_bqkv"]).astype(np.float32)
                              .reshape(48, 128).T)

    consts = np.zeros((128, NCONST), np.float32)
    consts[:, C_NEG1] = -1.0
    consts[:, C_INVD] = INV_D
    consts[:, C_EPS] = EPS
    consts[:, C_ATTSC] = ATT_SCALE
    consts[:, C_NSHIFT] = -ATT_SHIFT
    consts[:, C_INVCNT2] = 0.5 / (JH * D)
    consts[:, C_ONE] = 1.0

    ws = np.float32(np.asarray(inputs["L0_ffn_weight_scale"]))
    sin = np.float32(np.asarray(inputs["L0_ffn_input_scale"]).astype(np.float32))
    scl = np.tile(np.array([[ws, sin]], np.float32), (128, 1))

    revramp = np.tile((np.float32(TC - 1) - np.arange(TC, dtype=np.float32))[None, :], (128, 1))

    m = np.zeros((128, 4 * 512), BF)
    s_i = np.arange(128)[:, None]
    q_i = np.arange(512)[None, :]
    for r in range(4):
        m[:, r * 512:(r + 1) * 512] = (s_i <= q_i - 128 * r).astype(BF)

    ident = np.eye(128, dtype=BF)
    idnf = np.eye(128, dtype=np.float32)
    idn8 = np.eye(8, dtype=np.float32)
    onescol = np.ones((128, 1), np.float32)
    onesrow = np.ones((1, 128), np.float32)
    onesf = np.ones((128, 128), np.float32)

    wkT = _wtile(_tp(inputs["L0_attn_Wk"]))
    wvT = _wtile(_tp(inputs["L0_attn_Wv"]))
    wrT = _wtile(_tp(inputs["L0_attn_Wr"]))
    woT = _wtile(_tp(inputs["L0_attn_Wo"]))
    w8T = _wtile(np.ascontiguousarray(np.asarray(inputs["L0_ffn_weight"]).view(F8).T))
    wqkv = _wtile(np.asarray(inputs["L1_attn_Wqkv"]).astype(np.float32))
    wo1 = _wtile(np.asarray(inputs["L1_attn_Wo"]).astype(np.float32))
    w1 = np.asarray(inputs["L1_ffn_w1"])
    w3 = np.asarray(inputs["L1_ffn_w3"])
    w2 = np.asarray(inputs["L1_ffn_w2"])
    w1T = np.zeros((D, JP), np.float32); w1T[:, 0:JH] = w1.T
    w3T = np.zeros((D, JP), np.float32); w3T[:, 0:JH] = w3.T
    w1T = _wtile(w1T)
    w3T = _wtile(w3T)
    w2T = np.zeros((JP, D), np.float32); w2T[0:JH, :] = w2.T
    wqmy_r = []
    for r in range(NCORE):
        qm = np.zeros((3 * NQS * 128, D), np.float32)
        for cw, wt in enumerate((w1T, w3T, w2T)):
            for s_i, j in enumerate(range(r, JT, NCORE)):
                qm[(cw * NQS + s_i) * 128:(cw * NQS + s_i + 1) * 128] = \
                    wt[j * 128:(j + 1) * 128]
        wqmy_r.append(qm)

    shared = dict(wkT=wkT, wvT=wvT, wrT=wrT, woT=woT, w8T=w8T, wqkv=wqkv, wo1=wo1,
                  pcols=pc, bqcols=bq, consts=consts,
                  scl=scl, revramp=revramp, masks=m, ident=ident, idn8=idn8, idnf=idnf, onesf=onesf,
                  onescol=onescol, onesrow=onesrow)

    in_maps = []
    for r in range(NCORE):
        xs = np.zeros((TC + 1, D), np.float32)
        if r > 0:
            xs[0] = x[r * TC - 1]
        xs[1:] = x[r * TC:(r + 1) * TC]
        e8 = np.zeros((128, 8), np.float32)
        for rp in range(8):
            e8[:, rp] = 1024.0 * (r - 1 - rp) if rp < r else 1e9
        in_maps.append(dict(shared, xT=_tp(xs), e1024=e8, wqmy=wqmy_r[r]))

    nc = build_nc(stages)
    res = bass_utils.run_bass_kernel_spmd(nc, in_maps, core_ids=list(range(NCORE)))
    if DBG:
        import pickle
        with open("/tmp/dbg_results.pkl", "wb") as f:
            pickle.dump([{k: np.asarray(v) for k, v in r.items()} for r in res.results], f)
    out = np.concatenate([np.asarray(r["y"]).T for r in res.results], axis=0)
    return out.reshape(1, T, D).astype(np.float32)
